# version 04
# baseline (speedup 1.0000x reference)
import numpy as np
import ml_dtypes
from contextlib import ExitStack

import concourse.bass as bass
import concourse.mybir as mybir
from concourse.bass_utils import run_bass_kernel_spmd

F32 = mybir.dt.float32
BF16 = mybir.dt.bfloat16
FP16 = mybir.dt.float16
I32 = mybir.dt.int32
U16 = mybir.dt.uint16
U8 = mybir.dt.uint8
AF = mybir.ActivationFunctionType
ALU = mybir.AluOpType
AX = mybir.AxisListType.X

B, T, D = 4, 2048, 4096
NC, TPC, NCH = 8, 1024, 8
HPC, DH = 4, 128
M = 14336
MSL, MT, HT = M // NC, (M // NC) // 128, D // 128
EPS = 1e-5
SCL = float(1.0 / np.sqrt(DH))

NF4 = [-1.0, -0.6961928009986877, -0.5250730514526367, -0.39491748809814453,
       -0.28444138169288635, -0.18477343022823334, -0.09105003625154495, 0.0,
       0.07958029955625534, 0.16093020141124725, 0.24611230194568634,
       0.33791524171829224, 0.44070982933044434, 0.5626170039176941,
       0.7229568362236023, 1.0]

DFD = [512, 512, 512, 256]
DOF = [0, 512, 1024, 1536]
DQ_TILES = []
for _H in range(HT):
    for _s in range(4):
        DQ_TILES.append(("d", _H, _s))
for _k in range(MT):
    for _s in range(8):
        DQ_TILES.append(("g", _k, _s))
    for _s in range(8):
        DQ_TILES.append(("u", _k, _s))
NDQ = len(DQ_TILES)
N_DOWN = 128


def dq_fd(t):
    return DFD[t[2]] if t[0] == "d" else 512


DQ_CUMP = [0]
for _t in DQ_TILES:
    DQ_CUMP.append(DQ_CUMP[-1] + dq_fd(_t) // 128)

NB_CODE = 3

ATT_P = [[(kb, max(0, kb - 4 * q) * 128) for kb in range(4 * q + 4)]
         for q in range(4)]
NP_QS = [len(p) for p in ATT_P]
NP_U = sum(NP_QS)

ATT_DVE_POS = [min(NDQ - 1, 120 + 14 * n) for n in range(16)]
C0_SILU_POS = [min(NDQ - 1, max(ATT_DVE_POS[-1] + 1 + 2 * k,
                                N_DOWN + (k + 1) * 16)) for k in range(MT)]

DEBUG = False
SKIP_ACC = False
ACC_BYPASS = False
PB_LOADS = 16
WST_PER_CH = 1 + 2 * MT + HT


def build_nc():
    nc = bass.Bass("TRN2", target_bir_lowering=False)

    x_rsT = nc.dram_tensor("x_rsT", [NCH * D, 128], F32, kind="ExternalInput")
    x_own = nc.dram_tensor("x_own", [TPC, D], F32, kind="ExternalInput")
    wq_s = nc.dram_tensor("wq_s", [D, HPC * DH], F32, kind="ExternalInput")
    wo_s = nc.dram_tensor("wo_s", [HPC * DH, D], F32, kind="ExternalInput")
    wkd = nc.dram_tensor("wkd", [D, DH], F32, kind="ExternalInput")
    wvd = nc.dram_tensor("wvd", [D, DH], F32, kind="ExternalInput")
    gqd = nc.dram_tensor("gqd", [MSL, D], I32, kind="ExternalInput")
    uqd = nc.dram_tensor("uqd", [MSL, D], I32, kind="ExternalInput")
    dqd = nc.dram_tensor("dqd", [D, MSL], I32, kind="ExternalInput")
    gamd = nc.dram_tensor("gamd", [MSL, 64], F32, kind="ExternalInput")
    uamd = nc.dram_tensor("uamd", [MSL, 64], F32, kind="ExternalInput")
    damd = nc.dram_tensor("damd", [D, 28], F32, kind="ExternalInput")
    idn = nc.dram_tensor("idn", [128, 128], F32, kind="ExternalInput")
    idnb = nc.dram_tensor("idnb", [128, 128], BF16, kind="ExternalInput")
    cmkb = nc.dram_tensor("cmkb", [128, 128], BF16, kind="ExternalInput")

    out_sl = nc.dram_tensor("out_sl", [NCH * D, 128], F32, kind="ExternalOutput")
    if DEBUG:
        dbg_xnT = nc.dram_tensor("dbg_xnT", [128, HT * TPC], BF16, kind="ExternalOutput")
        dbg_kT = nc.dram_tensor("dbg_kT", [128, TPC], BF16, kind="ExternalOutput")
        dbg_v = nc.dram_tensor("dbg_v", [128, 8 * 128], BF16, kind="ExternalOutput")
        dbg_g = nc.dram_tensor("dbg_g", [128, D], BF16, kind="ExternalOutput")
        dbg_u = nc.dram_tensor("dbg_u", [128, D], BF16, kind="ExternalOutput")
        dbg_d = nc.dram_tensor("dbg_d", [128, MSL], BF16, kind="ExternalOutput")
        dbg_rs = nc.dram_tensor("dbg_rs", [D, 128], F32, kind="ExternalOutput")
        dbg_ro = nc.dram_tensor("dbg_ro", [D, 128], F32, kind="ExternalOutput")
        dbg_rs3 = nc.dram_tensor("dbg_rs3", [D, 128], F32, kind="ExternalOutput")
        dbg_ro3 = nc.dram_tensor("dbg_ro3", [D, 128], F32, kind="ExternalOutput")
        dbg_qT = nc.dram_tensor("dbg_qT", [128, 8192], BF16, kind="ExternalOutput")
        dbg_avb = nc.dram_tensor("dbg_avb", [128, 8192], BF16, kind="ExternalOutput")
        dbg_hT = nc.dram_tensor("dbg_hT", [128, MT * TPC], BF16, kind="ExternalOutput")
        dbg_exs = nc.dram_tensor("dbg_exs", [128, 8192], BF16, kind="ExternalOutput")

    xnT_st = nc.dram_tensor("xnT_st", [128, HT * TPC], BF16)
    xnT_all = nc.dram_tensor("xnT_all", [NC * 128, HT * TPC], BF16)
    kT_st = nc.dram_tensor("kT_st", [128, TPC], BF16)
    kT_all = nc.dram_tensor("kT_all", [NC * 128, TPC], BF16)
    v_st = nc.dram_tensor("v_st", [128, 8 * 128], BF16)
    v_all = nc.dram_tensor("v_all", [NC * 128, 8 * 128], BF16)
    g_st = nc.dram_tensor("g_st", [128, MT * D], BF16)
    u_st = nc.dram_tensor("u_st", [128, MT * D], BF16)
    d_st = nc.dram_tensor("d_st", [128, HT * MSL], BF16)
    rs_in = nc.dram_tensor("rs_in", [NCH * 8 * D, 128], F32)
    rs_jnk = nc.dram_tensor("rs_jnk", [8 * D, 128], F32)
    rs_out = nc.dram_tensor("rs_out", [NCH * D, 128], F32)

    st = ExitStack()

    def sem(name):
        return st.enter_context(nc.semaphore(name))

    s_wa = sem("s_wa")
    s_waf = sem("s_waf")
    s_xin = sem("s_xin")
    s_code = [sem(f"s_code{i}") for i in range(NB_CODE)]
    s_lnr = sem("s_lnr")
    s_sta = sem("s_sta")
    s_lnv = sem("s_lnv")
    s_sd = sem("s_sd")
    s_lnn = sem("s_lnn")
    s_xt = sem("s_xt")
    s_evx = sem("s_evx")
    s_kv = sem("s_kv")
    s_kve = sem("s_kve")
    s_stg = sem("s_stg")
    s_ag = sem("s_ag")
    s_dqw = sem("s_dqw")
    s_dqt = sem("s_dqt")
    s_dqs = sem("s_dqs")
    s_p1l = sem("s_p1l")
    s_qpl = sem("s_qpl")
    s_xchl = sem("s_xchl")
    s_kvl = sem("s_kvl")
    s_wol = sem("s_wol")
    s_wxc = sem("s_wxc")
    s_wg2 = [sem("s_wg0"), sem("s_wg1")]
    s_wu2 = [sem("s_wu0"), sem("s_wu1")]
    s_wd2 = [sem("s_wd0"), sem("s_wd1")]
    s_wodp = [sem("s_wodp0"), sem("s_wodp1")]
    s_dnap = [sem("s_dnap0"), sem("s_dnap1")]
    s_qc = sem("s_qc")
    s_oc = sem("s_oc")
    s_xncf = sem("s_xncf")
    s_qev = sem("s_qev")
    s_pe1 = sem("s_pe1")
    s_exp = sem("s_exp")
    s_av1 = sem("s_av1")
    s_dnme = sem("s_dnme")
    s_dva = sem("s_dva")
    s_pbc = sem("s_pbc")
    s_abe = sem("s_abe")
    s_pva = sem("s_pva")
    s_wvd = sem("s_wvd")
    s_woe = sem("s_woe")
    s_wod = sem("s_wod")
    s_wst = sem("s_wst")
    s_gu = sem("s_gu")
    s_sig = sem("s_sig")
    s_hT = sem("s_hT")
    s_dn = sem("s_dn")
    s_dne = sem("s_dne")
    s_dnacc = sem("s_dnacc")
    s_rs = sem("s_rs")
    s_fin = sem("s_fin")
    s_fad = sem("s_fad")
    s_out = sem("s_out")

    identf = st.enter_context(nc.sbuf_tensor("identf", [128, 128], F32))
    identb = st.enter_context(nc.sbuf_tensor("identb", [128, 128], BF16))
    cmaskb = st.enter_context(nc.sbuf_tensor("cmaskb", [128, 128], BF16))
    onesr = st.enter_context(nc.sbuf_tensor("onesr", [1, 128], BF16))
    onesc = st.enter_context(nc.sbuf_tensor("onesc", [128, 1], BF16))
    gamx = st.enter_context(nc.sbuf_tensor("gamx", [128, MT * 64], F32))
    uamx = st.enter_context(nc.sbuf_tensor("uamx", [128, MT * 64], F32))
    damx = st.enter_context(nc.sbuf_tensor("damx", [128, HT * 28], F32))
    cq = st.enter_context(nc.sbuf_tensor("cq", [128, NB_CODE * 512], I32))
    qu16 = st.enter_context(nc.sbuf_tensor("qu16", [128, 512], U16))
    mby = [st.enter_context(nc.sbuf_tensor(f"mb{i}", [128, 512], U16))
           for i in range(4)]
    ut = st.enter_context(nc.sbuf_tensor("ut", [128, 8 * 512], FP16))
    wdt = st.enter_context(nc.sbuf_tensor("wdt", [128, 2 * 512], BF16))
    trs = st.enter_context(nc.sbuf_tensor("trs", [128, 2 * 512], BF16))
    lnS = st.enter_context(nc.sbuf_tensor("lnS", [128, 8], F32))
    lnQ = st.enter_context(nc.sbuf_tensor("lnQ", [128, 8], F32))
    lnNM = st.enter_context(nc.sbuf_tensor("lnNM", [128, 8], F32))
    lnM2 = st.enter_context(nc.sbuf_tensor("lnM2", [128, 8], F32))
    lnT2 = st.enter_context(nc.sbuf_tensor("lnT2", [128, 8], F32))
    lnV = st.enter_context(nc.sbuf_tensor("lnV", [128, 8], F32))
    lnSD = st.enter_context(nc.sbuf_tensor("lnSD", [128, 8], F32))
    lnR = st.enter_context(nc.sbuf_tensor("lnR", [128, 8], F32))
    dnm = st.enter_context(nc.sbuf_tensor("dnm", [1, 512], F32))
    rcp = st.enter_context(nc.sbuf_tensor("rcp", [1, 512], F32))
    rcpb = st.enter_context(nc.sbuf_tensor("rcpb", [1, 512], BF16))

    AR = 150 * 1024
    arena = st.enter_context(nc.sbuf_tensor("arena", [128, AR], U8))

    def cv(off, cols, dt):
        esz = {F32: 4, BF16: 2, FP16: 2, I32: 4, U16: 2}[dt]
        return arena[:, off:off + cols * esz].bitcast(dt)

    oP_xt = 0
    oP_sq = 16384
    oP_xnf = 32768
    oP_xtt = 65536
    oP_wstg = 81920
    oP_kts = 98304
    oP_vts = 100352
    oP_wkb = 102400
    oP_wvb = 110592
    oQ_wqb = 0
    oQ_woev = 0
    oQ_xch = 32768
    oQ_wob = 32768
    oQ_qT = 65536
    oQ_exs = 81920
    oQ_avb = 98304
    oQ_ktb = 114688
    oQ_vtb = 118784
    oQ_ae = 122880
    oQ_be = 123904
    oQ_stg = 124928
    oF_xch = 0
    oF_wg = 65536
    oF_wu = 81920
    oF_wd = 98304
    oF_hT = 105472
    oF_dnev = 134144
    oF_sg = 142336
    oF_fr = 144384
    oF_fx = 148480
    assert oF_fx + 4096 <= AR

    pb = [st.enter_context(nc.psum_tensor(f"pb{i}", [128, 512], F32))
          for i in range(8)]

    blk = st.enter_context(nc.Block())

    def code_src(t):
        kind = t[0]
        if kind == "d":
            _, H, s = t
            return (bass.AP(dqd, H * 128 * MSL + DOF[s], [[MSL, 128], [1, DFD[s]]]),
                    DFD[s])
        _, k, s = t
        ten = gqd if kind == "g" else uqd
        return bass.AP(ten, k * 128 * D + s * 512, [[D, 128], [1, 512]]), 512

    @blk.sync
    def _(sync):
        sync.dma_start(out=identf[:, :], in_=idn[:, :]).then_inc(s_wa, 16)
        sync.dma_start(out=identb[:, :], in_=idnb[:, :]).then_inc(s_wa, 16)
        sync.dma_start(out=cmaskb[:, :], in_=cmkb[:, :]).then_inc(s_wa, 16)
        sync.wait_ge(s_wa, 48)
        sync.dma_start(
            out=cv(oP_wstg, D, F32),
            in_=bass.AP(wkd, 0, [[DH, 128], [128 * DH, HT], [1, DH]]),
        ).then_inc(s_wa, 16)
        sync.wait_ge(s_waf, 1)
        sync.dma_start(
            out=cv(oP_wstg, D, F32),
            in_=bass.AP(wvd, 0, [[DH, 128], [128 * DH, HT], [1, DH]]),
        ).then_inc(s_wa, 16)
        sync.wait_ge(s_wa, 80)
        sync.dma_start(out=gamx[:, :],
                       in_=bass.AP(gamd, 0, [[64, 128], [128 * 64, MT], [1, 64]])
                       ).then_inc(s_wa, 16)
        sync.dma_start(out=uamx[:, :],
                       in_=bass.AP(uamd, 0, [[64, 128], [128 * 64, MT], [1, 64]])
                       ).then_inc(s_wa, 16)
        sync.dma_start(out=damx[:, :],
                       in_=bass.AP(damd, 0, [[28, 128], [128 * 28, HT], [1, 28]])
                       ).then_inc(s_wa, 16)
        for tt in range(8):
            if tt >= 1:
                sync.wait_ge(s_lnn, tt)
            sync.dma_start(out=cv(oP_xt, D, F32),
                           in_=x_own[tt * 128:(tt + 1) * 128, :]).then_inc(s_xin, 16)
        for i in range(NB_CODE):
            src, w = code_src(DQ_TILES[i])
            sync.dma_start(out=cq[:, (i % NB_CODE) * 512:(i % NB_CODE) * 512 + w],
                           in_=src).then_inc(s_code[i % NB_CODE], 16)
        for i, til in enumerate(DQ_TILES):
            fd = dq_fd(til)
            ob = (i % 2) * 512
            sync.wait_ge(s_dqw, i + 1)
            if i >= 2:
                sync.wait_ge(s_dqs, 16 * (i - 1))
            for j in range(fd // 128):
                sync.dma_start_transpose(
                    trs[:, ob + j * 128: ob + (j + 1) * 128],
                    wdt[:, ob + j * 128: ob + (j + 1) * 128],
                ).then_inc(s_dqt, 16)
            sync.wait_ge(s_dqt, 16 * DQ_CUMP[i + 1])
            kind = til[0]
            if kind == "d":
                _, H, s = til
                dst = d_st[:, H * MSL + DOF[s]: H * MSL + DOF[s] + fd]
            elif kind == "g":
                _, k, s = til
                dst = g_st[:, k * D + s * 512: k * D + (s + 1) * 512]
            else:
                _, k, s = til
                dst = u_st[:, k * D + s * 512: k * D + (s + 1) * 512]
            sync.dma_start(out=dst, in_=trs[:, ob:ob + fd]).then_inc(s_dqs, 16)
            nx = i + NB_CODE
            if nx < NDQ:
                src, w = code_src(DQ_TILES[nx])
                sync.dma_start(out=cq[:, (nx % NB_CODE) * 512:(nx % NB_CODE) * 512 + w],
                               in_=src).then_inc(s_code[nx % NB_CODE], 16)

    @blk.scalar
    def _(scalar):
        scalar.wait_ge(s_wa, 64)
        scalar.activation(cv(oP_wkb, D, BF16), cv(oP_wstg, D, F32),
                          AF.Copy).then_inc(s_waf, 1)
        scalar.wait_ge(s_wa, 80)
        scalar.activation(cv(oP_wvb, D, BF16), cv(oP_wstg, D, F32),
                          AF.Copy).then_inc(s_waf, 1)
        for tt in range(8):
            scalar.wait_ge(s_xin, (tt + 1) * 16)
            scalar.activation(cv(oP_sq, D, F32), cv(oP_xt, D, F32), AF.Square,
                              accum_out=lnQ[:, tt:tt + 1])
            scalar.wait_ge(s_lnr, tt + 1)
            scalar.activation(lnNM[:, tt:tt + 1], lnS[:, tt:tt + 1], AF.Copy,
                              scale=-1.0 / D)
            scalar.activation(lnM2[:, tt:tt + 1], lnNM[:, tt:tt + 1], AF.Square)
            scalar.activation(lnT2[:, tt:tt + 1], lnQ[:, tt:tt + 1], AF.Copy,
                              scale=1.0 / D, bias=EPS).then_inc(s_sta, 1)
            scalar.wait_ge(s_lnv, tt + 1)
            scalar.sqrt(lnSD[:, tt:tt + 1], lnV[:, tt:tt + 1]).then_inc(s_sd, 1)
            xb = oP_xtt + (tt % 2) * 8192
            if tt >= 2:
                scalar.wait_ge(s_kv, 2 * (tt - 1))
                scalar.wait_ge(s_stg, 16 * 4 * (tt - 1))
            for g in range(8):
                scalar.wait_ge(s_xt, tt * 8 + g + 1)
                scalar.activation(cv(xb + g * 1024, 512, BF16),
                                  pb[4 + g % 2][:, :],
                                  AF.Copy).then_inc(s_evx, 1)
            scalar.wait_ge(s_kv, 2 * tt + 1)
            scalar.activation(cv(oP_kts + tt * 256, 128, BF16), pb[0][:, 0:128], AF.Copy)
            scalar.wait_ge(s_kv, 2 * tt + 2)
            scalar.activation(cv(oP_vts + tt * 256, 128, BF16), pb[1][:, 0:128],
                              AF.Copy).then_inc(s_kve, 1)

        g_exp = 0
        for b in range(4):
            scalar.wait_ge(s_wodp[0], b * 16 * 16)
            scalar.wait_ge(s_wodp[1], b * 16 * 16)
            for g in range(4):
                scalar.wait_ge(s_qpl, (b * 4 + g + 1) * 16)
                scalar.activation(cv(oQ_wqb + g * 8192, 4096, BF16),
                                  cv(oQ_stg, 4096, F32), AF.Copy).then_inc(s_qc, 1)
            for cs in range(4):
                scalar.wait_ge(s_xncf, b * 4 + cs + 1)
                for dp in range(4):
                    ins = scalar.activation(
                        cv(oQ_qT + dp * 4096 + (cs // 2) * 2048 + (cs % 2) * 1024,
                           512, BF16),
                        pb[dp][:, :], AF.Copy)
                ins.then_inc(s_qev, 1)
            scalar.wait_ge(s_xncf, b * 4 + 4)
            for g in range(4):
                scalar.wait_ge(s_wol, (b * 4 + g + 1) * 16)
                scalar.activation(cv(oQ_wob + g * 8192, 4096, BF16),
                                  cv(oQ_stg, 4096, F32), AF.Copy).then_inc(s_oc, 1)
            for h in range(HPC):
                for qs in range(4):
                    uq_i = (b * 4 + h) * 4 + qs
                    if uq_i >= 1:
                        scalar.wait_ge(s_av1, uq_i)
                    for pi, (kb, lo) in enumerate(ATT_P[qs]):
                        g_exp += 1
                        scalar.wait_ge(s_pe1, g_exp)
                        slot = pi % 4
                        scalar.activation(
                            cv(oQ_exs + kb * 1024 + lo * 2, 512 - lo, BF16),
                            pb[4 + slot][:, lo:512],
                            AF.Exp, scale=SCL).then_inc(s_exp, 1)
                    scalar.wait_ge(s_av1, uq_i + 1)
                    if uq_i >= 1:
                        scalar.wait_ge(s_dva, uq_i)
                    scalar.activation(dnm[0:1, :], pb[1][0:1, :],
                                      AF.Copy).then_inc(s_dnme, 1)
                    scalar.wait_ge(s_pbc, uq_i + 1)
                    if uq_i >= 1:
                        scalar.wait_ge(s_pva, uq_i)
                    scalar.activation(cv(oQ_ae, 512, BF16), pb[0][:, :], AF.Copy)
                    scalar.activation(cv(oQ_be, 512, BF16), pb[2][:, :],
                                      AF.Copy).then_inc(s_abe, 1)
            for ct in range(32):
                scalar.wait_ge(s_wvd, b * 32 + ct + 1)
                if b * 32 + ct >= 2:
                    scalar.wait_ge(s_wodp[ct % 2], (b * 16 + ct // 2) * 16)
                for ts4 in range(4):
                    ins = scalar.activation(
                        cv(oQ_woev + (ct % 2) * 8192 + ts4 * 2048, 512, F32),
                        pb[(ct % 2) * 4 + ts4][:, :], AF.Copy)
                ins.then_inc(s_woe, 1)

        for c in range(NCH):
            for k in range(MT):
                scalar.wait_ge(s_gu, c * MT + k + 1)
                if c * MT + k >= 1:
                    scalar.wait_ge(s_hT, c * MT + k)
                for half in range(2):
                    ins = scalar.activation(cv(oF_sg + half * 1024, 512, BF16),
                                            pb[(k % 2) * 4 + half][:, :], AF.Sigmoid)
                ins.then_inc(s_sig, 1)
            for ho in range(HT):
                scalar.wait_ge(s_dn, c * HT + ho + 1)
                if c * HT + ho >= 2:
                    scalar.wait_ge(s_dnap[ho % 2], (c * 16 + ho // 2) * 16)
                for half in range(2):
                    ins = scalar.activation(
                        cv(oF_dnev + (ho % 2) * 4096 + half * 2048, 512, F32),
                        pb[2 * (ho % 4) + half][:, :], AF.Copy)
                ins.then_inc(s_dne, 1)

    @blk.vector
    def _(vector):
        vector.memset(onesr[0:1, :], 1.0)
        vector.memset(onesc[:, :], 1.0)
        for tt in range(8):
            vector.wait_ge(s_xin, (tt + 1) * 16)
            vector.tensor_reduce(lnS[:, tt:tt + 1], cv(oP_xt, D, F32), AX,
                                 ALU.add).then_inc(s_lnr, 1)
            vector.wait_ge(s_sta, tt + 1)
            vector.tensor_tensor(lnV[:, tt:tt + 1], lnT2[:, tt:tt + 1],
                                 lnM2[:, tt:tt + 1], ALU.subtract).then_inc(s_lnv, 1)
            vector.wait_ge(s_sd, tt + 1)
            vector.reciprocal(lnR[:, tt:tt + 1], lnSD[:, tt:tt + 1])
            if tt >= 2:
                vector.wait_ge(s_xt, (tt - 1) * 8)
            vector.tensor_scalar(cv(oP_xnf + (tt % 2) * 16384, D, F32),
                                 cv(oP_xt, D, F32), lnNM[:, tt:tt + 1],
                                 lnR[:, tt:tt + 1], ALU.add, ALU.mult
                                 ).then_inc(s_lnn, 1)

        state = {"att": 0, "silu": 0}

        def emit_att_unit(n):
            for qs in range(4):
                u = n * 4 + qs
                vector.wait_ge(s_dnme, u + 1)
                if u >= 1:
                    vector.wait_ge(s_pbc, u)
                vector.reciprocal(rcp[0:1, :], dnm[0:1, :])
                vector.tensor_copy(rcpb[0:1, :], rcp[0:1, :]).then_inc(s_dva, 1)

        def emit_silu(c, k):
            vector.wait_ge(s_sig, c * MT + k + 1)
            for half in range(2):
                vector.tensor_tensor(cv(oF_sg + half * 1024, 512, BF16),
                                     cv(oF_sg + half * 1024, 512, BF16),
                                     pb[(k % 2) * 4 + half][:, :], ALU.mult)
            for half in range(2):
                ins = vector.tensor_tensor(
                    cv(oF_hT + k * 2048 + half * 1024, 512, BF16),
                    cv(oF_sg + half * 1024, 512, BF16),
                    pb[(k % 2) * 4 + 2 + half][:, :], ALU.mult)
            ins.then_inc(s_hT, 1)

        vector.wait_ge(s_wa, 128)
        for i, til in enumerate(DQ_TILES):
            fd = dq_fd(til)
            slot = (i % NB_CODE) * 512
            ob = (i % 2) * 512
            vector.wait_ge(s_code[i % NB_CODE], (i // NB_CODE + 1) * 16)
            vector.tensor_copy(qu16[:, 0:fd], cq[:, slot:slot + fd])
            vector.tensor_scalar(mby[0][:, 0:fd], qu16[:, 0:fd], 1, None,
                                 ALU.bitwise_and)
            vector.tensor_scalar(mby[1][:, 0:fd], qu16[:, 0:fd], 2, None,
                                 ALU.bitwise_and)
            vector.tensor_scalar(mby[2][:, 0:fd], qu16[:, 0:fd], 4, None,
                                 ALU.bitwise_and)
            vector.tensor_scalar(mby[3][:, 0:fd], qu16[:, 0:fd], 8, None,
                                 ALU.bitwise_and)
            for j in range(8):
                c0, c1 = NF4[2 * j], NF4[2 * j + 1]
                vector.tensor_scalar(ut[:, j * 512:j * 512 + fd], mby[0][:, 0:fd],
                                     c1 - c0, c0, ALU.mult, ALU.add)
            for j in range(4):
                vector.copy_predicated(ut[:, (2 * j) * 512:(2 * j) * 512 + fd],
                                       mby[1][:, 0:fd],
                                       ut[:, (2 * j + 1) * 512:(2 * j + 1) * 512 + fd])
            vector.copy_predicated(ut[:, 0:fd], mby[2][:, 0:fd],
                                   ut[:, 1024:1024 + fd])
            vector.copy_predicated(ut[:, 2048:2048 + fd], mby[2][:, 0:fd],
                                   ut[:, 3072:3072 + fd])
            vector.copy_predicated(ut[:, 0:fd], mby[3][:, 0:fd],
                                   ut[:, 2048:2048 + fd])
            if i >= 2:
                vector.wait_ge(s_dqt, 16 * DQ_CUMP[i - 1])
            kind = til[0]
            ins = None
            for bb in range(fd // 64):
                if kind == "d":
                    _, H, s = til
                    amc = damx[:, H * 28 + DOF[s] // 64 + bb:
                               H * 28 + DOF[s] // 64 + bb + 1]
                elif kind == "g":
                    _, k, s = til
                    amc = gamx[:, k * 64 + s * 8 + bb: k * 64 + s * 8 + bb + 1]
                else:
                    _, k, s = til
                    amc = uamx[:, k * 64 + s * 8 + bb: k * 64 + s * 8 + bb + 1]
                ins = vector.tensor_scalar(wdt[:, ob + bb * 64: ob + (bb + 1) * 64],
                                           ut[:, bb * 64:(bb + 1) * 64],
                                           amc, None, ALU.mult)
            ins.then_inc(s_dqw, 1)
            while state["att"] < 16 and ATT_DVE_POS[state["att"]] <= i:
                emit_att_unit(state["att"])
                state["att"] += 1
            while state["silu"] < MT and C0_SILU_POS[state["silu"]] <= i:
                emit_silu(0, state["silu"])
                state["silu"] += 1
        while state["att"] < 16:
            emit_att_unit(state["att"])
            state["att"] += 1
        while state["silu"] < MT:
            emit_silu(0, state["silu"])
            state["silu"] += 1

        def emit_fin(cc):
            for qtr in range(4):
                vector.wait_ge(s_fin, (cc * 4 + qtr + 1) * 32)
                vector.tensor_tensor(cv(oF_fr, 1024, F32), cv(oF_fr, 1024, F32),
                                     cv(oF_fx, 1024, F32),
                                     ALU.add).then_inc(s_fad, 1)

        for c in range(1, NCH):
            for k in range(MT):
                emit_silu(c, k)
                if k == 11:
                    emit_fin(c - 1)
        emit_fin(NCH - 1)

    @blk.tensor
    def _(tensor):
        tensor.wait_ge(s_wa, 48)
        for tt in range(8):
            xb = oP_xnf + (tt % 2) * 16384
            tensor.wait_ge(s_lnn, tt + 1)
            for g in range(8):
                if tt * 8 + g >= 2:
                    tensor.wait_ge(s_evx, tt * 8 + g - 1)
                for q4 in range(4):
                    hh = g * 4 + q4
                    ins = tensor.transpose(
                        pb[4 + g % 2][:, q4 * 128:(q4 + 1) * 128],
                        cv(xb + hh * 512, 128, F32), identf[:, :])
                ins.then_inc(s_xt, 1)
            tensor.wait_ge(s_waf, 2)
            tensor.wait_ge(s_evx, (tt + 1) * 8)
            if tt >= 1:
                tensor.wait_ge(s_kve, tt)
            xtb = oP_xtt + (tt % 2) * 8192
            for H in range(HT):
                ins = tensor.matmul(pb[0][:, 0:128], cv(oP_wkb + H * 256, 128, BF16),
                                    cv(xtb + H * 256, 128, BF16),
                                    start=(H == 0), stop=(H == HT - 1))
            ins.then_inc(s_kv, 1)
            for H in range(HT):
                ins = tensor.matmul(pb[1][:, 0:128], cv(xtb + H * 256, 128, BF16),
                                    cv(oP_wvb + H * 256, 128, BF16),
                                    start=(H == 0), stop=(H == HT - 1))
            ins.then_inc(s_kv, 1)

        tensor.wait_ge(s_lnn, 8)
        tensor.wait_ge(s_kve, 8)
        tensor.wait_ge(s_evx, 64)
        g_pe = 0
        for b in range(4):
            for cs in range(4):
                tensor.wait_ge(s_qc, b * 4 + 4)
                if cs == 0 and b >= 1:
                    tensor.wait_ge(s_woe, b * 32)
                tensor.wait_ge(s_xchl, (b * 4 + cs + 1) * 16)
                if b * 4 + cs >= 1:
                    tensor.wait_ge(s_qev, b * 4 + cs)
                for dp in range(4):
                    for H in range(HT):
                        ins = tensor.matmul(pb[dp][:, :],
                                            cv(oQ_wqb + H * 1024 + dp * 256, 128, BF16),
                                            cv(oQ_xch + H * 1024, 512, BF16),
                                            start=(H == 0), stop=(H == HT - 1))
                tensor.nop()
                ins.then_inc(s_xncf, 1)
            tensor.wait_ge(s_kvl, (b + 1) * 64)
            for h in range(HPC):
                for qs in range(4):
                    uq_i = (b * 4 + h) * 4 + qs
                    tensor.wait_ge(s_qev, b * 4 + 4)
                    for pi, (kb, lo) in enumerate(ATT_P[qs]):
                        slot = pi % 4
                        g_pe += 1
                        if g_pe >= 5:
                            tensor.wait_ge(s_exp, g_pe - 4)
                        diag = kb >= 4 * qs
                        ins = tensor.matmul(pb[4 + slot][:, lo:512],
                                            cv(oQ_ktb + kb * 256, 128, BF16),
                                            cv(oQ_qT + h * 4096 + qs * 1024 + lo * 2,
                                               512 - lo, BF16),
                                            start=True, stop=not diag)
                        if diag:
                            ins = tensor.matmul(
                                pb[4 + slot][:, lo:lo + 128],
                                identb[:, :], cmaskb[:, :], start=False, stop=True)
                        ins.then_inc(s_pe1, 1)
                    tensor.wait_ge(s_exp, g_pe)
                    if uq_i >= 1:
                        tensor.wait_ge(s_abe, uq_i)
                        tensor.wait_ge(s_dnme, uq_i)
                    np_ = len(ATT_P[qs])
                    for pi, (kb, lo) in enumerate(ATT_P[qs]):
                        tensor.matmul(pb[0][:, lo:512],
                                      cv(oQ_vtb + kb * 256, 128, BF16),
                                      cv(oQ_exs + kb * 1024 + lo * 2, 512 - lo, BF16),
                                      start=(pi == 0), stop=(pi == np_ - 1))
                    for pi, (kb, lo) in enumerate(ATT_P[qs]):
                        ins = tensor.matmul(pb[1][0:1, lo:512],
                                            onesc[:, :],
                                            cv(oQ_exs + kb * 1024 + lo * 2,
                                               512 - lo, BF16),
                                            start=(pi == 0), stop=(pi == np_ - 1))
                    ins.then_inc(s_av1, 1)
                    tensor.wait_ge(s_dva, uq_i + 1)
                    tensor.matmul(pb[2][:, :], onesr[0:1, :], rcpb[0:1, :],
                                  start=True, stop=True).then_inc(s_pbc, 1)
            tensor.wait_ge(s_pva, (b + 1) * 16)
            tensor.wait_ge(s_oc, (b + 1) * 4)
            for ct in range(32):
                if b * 32 + ct >= 2:
                    tensor.wait_ge(s_woe, b * 32 + ct - 1)
                for dp in range(4):
                    for ts4 in range(4):
                        ins = tensor.matmul(pb[(ct % 2) * 4 + ts4][:, :],
                                            cv(oQ_wob + dp * 8192 + ct * 256, 128, BF16),
                                            cv(oQ_avb + dp * 4096 + ts4 * 1024,
                                               512, BF16),
                                            start=(dp == 0), stop=(dp == 3))
                ins.then_inc(s_wvd, 1)

        tensor.wait_ge(s_woe, 128)
        for c in range(NCH):
            tensor.wait_ge(s_wxc, (c + 1) * 16)
            if c >= 1:
                tensor.wait_ge(s_dne, c * HT)
            for k in range(MT):
                nslot = c * (MT // 2) + k // 2 + 1
                tensor.wait_ge(s_wg2[k % 2], nslot * 16)
                tensor.wait_ge(s_wu2[k % 2], nslot * 16)
                if c * MT + k >= 2:
                    tensor.wait_ge(s_hT, c * MT + k - 1)
                for half in range(2):
                    for H in range(HT):
                        tensor.matmul(pb[(k % 2) * 4 + half][:, :],
                                      cv(oF_wg + (k % 2) * 8192 + H * 256, 128, BF16),
                                      cv(oF_xch + H * 2048 + half * 1024, 512, BF16),
                                      start=(H == 0), stop=(H == HT - 1))
                for half in range(2):
                    for H in range(HT):
                        ins = tensor.matmul(
                            pb[(k % 2) * 4 + 2 + half][:, :],
                            cv(oF_wu + (k % 2) * 8192 + H * 256, 128, BF16),
                            cv(oF_xch + H * 2048 + half * 1024, 512, BF16),
                            start=(H == 0), stop=(H == HT - 1))
                ins.then_inc(s_gu, 1)
            tensor.wait_ge(s_hT, (c + 1) * MT)
            for ho in range(HT):
                tensor.wait_ge(s_wd2[ho % 2], (c * (HT // 2) + ho // 2 + 1) * 16)
                if c * HT + ho >= 4:
                    tensor.wait_ge(s_dne, c * HT + ho - 3)
                for k in range(MT):
                    for half in range(2):
                        ins = tensor.matmul(
                            pb[2 * (ho % 4) + half][:, :],
                            cv(oF_wd + (ho % 2) * 3584 + k * 256, 128, BF16),
                            cv(oF_hT + k * 2048 + half * 1024, 512, BF16),
                            start=(k == 0), stop=(k == MT - 1))
                ins.then_inc(s_dn, 1)

    @blk.gpsimd
    def _(gpsimd):
        for tt in range(8):
            gpsimd.wait_ge(s_evx, (tt + 1) * 8)
            xtb = oP_xtt + (tt % 2) * 8192
            for g4 in range(4):
                gpsimd.dma_start(
                    out=bass.AP(xnT_st, g4 * 8 * TPC + tt * 128,
                                [[HT * TPC, 128], [TPC, 8], [1, 128]]),
                    in_=cv(xtb + g4 * 2048, 1024, BF16),
                ).then_inc(s_stg, 16)
        gpsimd.wait_ge(s_kve, 8)
        gpsimd.dma_start(out=kT_st[:, :], in_=cv(oP_kts, TPC, BF16)).then_inc(s_stg, 16)
        gpsimd.dma_start(out=v_st[:, :], in_=cv(oP_vts, TPC, BF16)).then_inc(s_stg, 16)
        gpsimd.wait_ge(s_stg, 16 * 34)
        gpsimd.collective_compute(
            "AllGather", ALU.bypass, replica_groups=[list(range(NC))],
            ins=[kT_st[:, :]], outs=[kT_all[:, :]]).then_inc(s_ag, 1)
        gpsimd.collective_compute(
            "AllGather", ALU.bypass, replica_groups=[list(range(NC))],
            ins=[v_st[:, :]], outs=[v_all[:, :]]).then_inc(s_ag, 1)
        gpsimd.collective_compute(
            "AllGather", ALU.bypass, replica_groups=[list(range(NC))],
            ins=[xnT_st[:, :]], outs=[xnT_all[:, :]]).then_inc(s_ag, 1)
        gpsimd.wait_ge(s_ag, 3)

        for b in range(4):
            for g in range(4):
                gpsimd.wait_ge(s_qc, b * 4 + g)
                if g == 0:
                    gpsimd.wait_ge(s_oc, b * 4)
                gpsimd.dma_start(
                    out=cv(oQ_stg, 4096, F32),
                    in_=bass.AP(wq_s, g * 8 * 128 * 512,
                                [[512, 128], [128 * 512, 8], [1, 512]]),
                ).then_inc(s_qpl, 16)
            for cs in range(4):
                c = 2 * b + cs // 2
                s = cs % 2
                gpsimd.wait_ge(s_xncf, b * 4 + cs)
                if cs == 0:
                    gpsimd.wait_ge(s_wvd, b * 32)
                gpsimd.dma_start(
                    out=cv(oQ_xch, HT * 512, BF16),
                    in_=bass.AP(xnT_all, c * 128 * HT * TPC + s * 512,
                                [[HT * TPC, 128], [TPC, HT], [1, 512]]),
                ).then_inc(s_xchl, 16)
            gpsimd.wait_ge(s_av1, b * 16)
            for half in range(2):
                cc = 2 * b + half
                gpsimd.dma_start(
                    out=cv(oQ_ktb + half * 2048, TPC, BF16),
                    in_=kT_all[cc * 128:(cc + 1) * 128, :]).then_inc(s_kvl, 16)
            for half in range(2):
                cc = 2 * b + half
                gpsimd.dma_start(
                    out=cv(oQ_vtb + half * 2048, TPC, BF16),
                    in_=v_all[cc * 128:(cc + 1) * 128, :]).then_inc(s_kvl, 16)
            for g in range(4):
                gpsimd.wait_ge(s_oc, b * 4 + g)
                if g == 0:
                    gpsimd.wait_ge(s_qc, (b + 1) * 4)
                gpsimd.dma_start(out=cv(oQ_stg, 4096, F32),
                                 in_=wo_s[g * 128:(g + 1) * 128, :]
                                 ).then_inc(s_wol, 16)
            gpsimd.wait_ge(s_wvd, b * 32)
            for h in range(HPC):
                for qs in range(4):
                    uq_i = (b * 4 + h) * 4 + qs
                    gpsimd.wait_ge(s_abe, uq_i + 1)
                    gpsimd.tensor_tensor(
                        cv(oQ_avb + h * 4096 + qs * 1024, 512, BF16),
                        cv(oQ_ae, 512, BF16), cv(oQ_be, 512, BF16),
                        ALU.mult).then_inc(s_pva, 1)
            for ct in range(32):
                gpsimd.wait_ge(s_woe, b * 32 + ct + 1)
                gpsimd.dma_start(
                    out=bass.AP(rs_in, (2 * b * 8 * D + ct * 128) * 128,
                                [[128, 128], [D * 128, 16], [1, 128]]),
                    in_=cv(oQ_woev + (ct % 2) * 8192, 2048, F32),
                ).then_inc(s_wodp[ct % 2], 16)
            if DEBUG and b == 0:
                gpsimd.dma_start(out=dbg_qT[:, :],
                                 in_=cv(oQ_qT, 8192, BF16)).then_inc(s_stg, 16)
                gpsimd.dma_start(out=dbg_avb[:, :],
                                 in_=cv(oQ_avb, 8192, BF16)).then_inc(s_stg, 16)
                gpsimd.dma_start(out=dbg_exs[:, :],
                                 in_=cv(oQ_exs, 8192, BF16)).then_inc(s_stg, 16)
                gpsimd.wait_ge(s_stg, 16 * 37)

        def pool_fin(cc):
            gpsimd.wait_ge(s_rs, cc + 1)
            for qtr in range(4):
                if cc * 4 + qtr >= 1:
                    gpsimd.wait_ge(s_out, (cc * 4 + qtr) * 16)
                gpsimd.dma_start(
                    out=cv(oF_fr, 1024, F32),
                    in_=bass.AP(rs_out, (cc * D + qtr * 1024) * 128,
                                [[128, 128], [128 * 128, 8], [1, 128]]),
                ).then_inc(s_fin, 16)
                gpsimd.dma_start(
                    out=cv(oF_fx, 1024, F32),
                    in_=bass.AP(x_rsT, (cc * D + qtr * 1024) * 128,
                                [[128, 128], [128 * 128, 8], [1, 128]]),
                ).then_inc(s_fin, 16)
                gpsimd.wait_ge(s_fad, cc * 4 + qtr + 1)
                gpsimd.dma_start(
                    out=bass.AP(out_sl, (cc * D + qtr * 1024) * 128,
                                [[128, 128], [128 * 128, 8], [1, 128]]),
                    in_=cv(oF_fr, 1024, F32),
                ).then_inc(s_out, 16)

        gpsimd.wait_ge(s_wvd, 4 * 32)
        gpsimd.wait_ge(s_wodp[0], 4 * 16 * 16)
        gpsimd.wait_ge(s_wodp[1], 4 * 16 * 16)
        for c in range(NCH):
            gpsimd.wait_ge(s_gu, c * MT)
            gpsimd.dma_start(out=cv(oF_xch, HT * TPC, BF16),
                             in_=xnT_all[c * 128:(c + 1) * 128, :]).then_inc(s_wxc, 16)
            for k in range(MT):
                gpsimd.wait_ge(s_dqs, (N_DOWN + (k + 1) * 16) * 16)
                if c * MT + k >= 2:
                    gpsimd.wait_ge(s_gu, c * MT + k - 1)
                gpsimd.dma_start(out=cv(oF_wg + (k % 2) * 8192, D, BF16),
                                 in_=g_st[:, k * D:(k + 1) * D]).then_inc(s_wg2[k % 2], 16)
                gpsimd.dma_start(out=cv(oF_wu + (k % 2) * 8192, D, BF16),
                                 in_=u_st[:, k * D:(k + 1) * D]).then_inc(s_wu2[k % 2], 16)
            if c >= 1:
                pool_fin(c - 1)
            if DEBUG and c == 3:
                gpsimd.wait_ge(s_hT, 4 * MT)
                gpsimd.dma_start(out=dbg_hT[:, :],
                                 in_=cv(oF_hT, MT * TPC, BF16)).then_inc(s_stg, 16)
                gpsimd.wait_ge(s_stg, 16 * 35)
            for ho in range(HT):
                gpsimd.wait_ge(s_dqs, N_DOWN * 16)
                if c * HT + ho >= 2:
                    gpsimd.wait_ge(s_dn, c * HT + ho - 1)
                gpsimd.dma_start(out=cv(oF_wd + (ho % 2) * 3584, MSL, BF16),
                                 in_=d_st[:, ho * MSL:(ho + 1) * MSL]).then_inc(s_wd2[ho % 2], 16)
                gpsimd.wait_ge(s_dne, c * HT + ho + 1)
                acc_dst = (bass.AP(rs_jnk, (ho * 128) * 128,
                                   [[128, 128], [D * 128, 8], [1, 128]])
                           if SKIP_ACC else
                           bass.AP(rs_in, (c * 8 * D + ho * 128) * 128,
                                   [[128, 128], [D * 128, 8], [1, 128]]))
                gpsimd.dma_start(
                    out=acc_dst,
                    in_=cv(oF_dnev + (ho % 2) * 4096, 1024, F32),
                    accum_op=(ALU.bypass if ACC_BYPASS else ALU.add)
                    ).then_inc(s_dnap[ho % 2], 16)
            gpsimd.wait_ge(s_dnap[0], (c + 1) * 16 * 16)
            gpsimd.wait_ge(s_dnap[1], (c + 1) * 16 * 16)
            gpsimd.collective_compute(
                "ReduceScatter", ALU.add, replica_groups=[list(range(NC))],
                ins=[rs_in[c * 8 * D:(c + 1) * 8 * D, :]],
                outs=[rs_out[c * D:(c + 1) * D, :]]).then_inc(s_rs, 1)
        pool_fin(NCH - 1)
        if DEBUG:
            gpsimd.dma_start(out=dbg_xnT[:, :], in_=xnT_st[:, :]).then_inc(s_stg, 16)
            gpsimd.dma_start(out=dbg_kT[:, :], in_=kT_st[:, :]).then_inc(s_stg, 16)
            gpsimd.dma_start(out=dbg_v[:, :], in_=v_st[:, :]).then_inc(s_stg, 16)
            gpsimd.dma_start(out=dbg_g[:, :], in_=g_st[:, 0:D]).then_inc(s_stg, 16)
            gpsimd.dma_start(out=dbg_u[:, :], in_=u_st[:, 0:D]).then_inc(s_stg, 16)
            gpsimd.dma_start(out=dbg_d[:, :], in_=d_st[:, 0:MSL]).then_inc(s_stg, 16)
            gpsimd.dma_start(out=dbg_rs[:, :], in_=rs_in[0:D, :]).then_inc(s_stg, 16)
            gpsimd.dma_start(out=dbg_ro[:, :], in_=rs_out[0:D, :]).then_inc(s_stg, 16)
            gpsimd.dma_start(out=dbg_rs3[:, :],
                             in_=rs_in[3 * 8 * D:3 * 8 * D + D, :]).then_inc(s_stg, 16)
            gpsimd.dma_start(out=dbg_ro3[:, :],
                             in_=rs_out[3 * D:4 * D, :]).then_inc(s_stg, 16)
            gpsimd.wait_ge(s_stg, 16 * 48 - 16)
        gpsimd.wait_ge(s_out, NCH * 4 * 16)

    st.close()
    return nc


_NC_CACHE = [None]


def kernel(**inputs):
    x = np.ascontiguousarray(np.asarray(inputs["L0_attn_x"], dtype=np.float32))
    wq = np.asarray(inputs["L0_attn_wq"], dtype=np.float32)
    wkf = np.ascontiguousarray(np.asarray(inputs["L0_attn_wk"], dtype=np.float32))
    wvf = np.ascontiguousarray(np.asarray(inputs["L0_attn_wv"], dtype=np.float32))
    wo = np.asarray(inputs["L0_attn_wo"], dtype=np.float32)
    gqf = np.asarray(inputs["L0_ffn_gate_q"], dtype=np.int32)
    gaf = np.asarray(inputs["L0_ffn_gate_absmax"], dtype=np.float32).reshape(M, 64)
    uqf = np.asarray(inputs["L0_ffn_up_q"], dtype=np.int32)
    uaf = np.asarray(inputs["L0_ffn_up_absmax"], dtype=np.float32).reshape(M, 64)
    dqf = np.asarray(inputs["L0_ffn_down_q"], dtype=np.int32)
    daf = np.asarray(inputs["L0_ffn_down_absmax"], dtype=np.float32).reshape(D, 224)

    idn = np.eye(128, dtype=np.float32)
    idnb_ = np.eye(128).astype(ml_dtypes.bfloat16)
    cm = np.where(np.arange(128)[:, None] > np.arange(128)[None, :],
                  np.float32(-30000.0), np.float32(0.0)).astype(ml_dtypes.bfloat16)

    in_maps = []
    for r in range(NC):
        b, half = r // 2, r % 2
        slabs = []
        for c in range(NCH):
            t0 = TPC * (c % 2) + 128 * r
            slabs.append(np.ascontiguousarray(x[c // 2, t0:t0 + 128, :].T))
        in_maps.append({
            "x_own": np.ascontiguousarray(x[b, half * TPC:(half + 1) * TPC, :]),
            "x_rsT": np.concatenate(slabs, axis=0),
            "wq_s": np.ascontiguousarray(wq[:, r * 512:(r + 1) * 512]),
            "wo_s": np.ascontiguousarray(wo[r * 512:(r + 1) * 512, :]),
            "wkd": wkf, "wvd": wvf,
            "gqd": np.ascontiguousarray(gqf[r * MSL:(r + 1) * MSL, :]),
            "uqd": np.ascontiguousarray(uqf[r * MSL:(r + 1) * MSL, :]),
            "dqd": np.ascontiguousarray(dqf[:, r * MSL:(r + 1) * MSL]),
            "gamd": np.ascontiguousarray(gaf[r * MSL:(r + 1) * MSL, :]),
            "uamd": np.ascontiguousarray(uaf[r * MSL:(r + 1) * MSL, :]),
            "damd": np.ascontiguousarray(daf[:, r * 28:(r + 1) * 28]),
            "idn": idn, "idnb": idnb_, "cmkb": cm,
        })

    if _NC_CACHE[0] is None:
        _NC_CACHE[0] = build_nc()
    res = run_bass_kernel_spmd(_NC_CACHE[0], in_maps, core_ids=list(range(NC)))
    _NC_CACHE.append(res)

    out = np.empty((B, T, D), dtype=np.float32)
    for r in range(NC):
        arr = np.asarray(res.results[r]["out_sl"]).reshape(NCH, D, 128)
        for c in range(NCH):
            t0 = TPC * (c % 2) + 128 * r
            out[c // 2, t0:t0 + 128, :] = arr[c].T
    return out


# version 06
# speedup vs baseline: 1.1284x; 1.1284x over previous
import numpy as np
import ml_dtypes
from contextlib import ExitStack

import concourse.bass as bass
import concourse.mybir as mybir
from concourse.bass_utils import run_bass_kernel_spmd

F32 = mybir.dt.float32
BF16 = mybir.dt.bfloat16
FP16 = mybir.dt.float16
I32 = mybir.dt.int32
U16 = mybir.dt.uint16
U8 = mybir.dt.uint8
AF = mybir.ActivationFunctionType
ALU = mybir.AluOpType
AX = mybir.AxisListType.X

B, T, D = 4, 2048, 4096
NC, TPC, NCH = 8, 1024, 8
HPC, DH = 4, 128
M = 14336
MSL, MT, HT = M // NC, (M // NC) // 128, D // 128
EPS = 1e-5
SCL = float(1.0 / np.sqrt(DH))

NF4 = [-1.0, -0.6961928009986877, -0.5250730514526367, -0.39491748809814453,
       -0.28444138169288635, -0.18477343022823334, -0.09105003625154495, 0.0,
       0.07958029955625534, 0.16093020141124725, 0.24611230194568634,
       0.33791524171829224, 0.44070982933044434, 0.5626170039176941,
       0.7229568362236023, 1.0]

DFD = [512, 512, 512, 256]
DOF = [0, 512, 1024, 1536]
DQ_TILES = []
for _H in range(HT):
    for _s in range(4):
        DQ_TILES.append(("d", _H, _s))
for _k in range(MT):
    for _s in range(8):
        DQ_TILES.append(("g", _k, _s))
    for _s in range(8):
        DQ_TILES.append(("u", _k, _s))
NDQ = len(DQ_TILES)
N_DOWN = 128


def dq_fd(t):
    return DFD[t[2]] if t[0] == "d" else 512


DQ_CUMP = [0]
for _t in DQ_TILES:
    DQ_CUMP.append(DQ_CUMP[-1] + dq_fd(_t) // 128)

NB_CODE = 3

ATT_P = [[(kb, max(0, kb - 4 * q) * 128) for kb in range(4 * q + 4)]
         for q in range(4)]
NP_QS = [len(p) for p in ATT_P]
NP_U = sum(NP_QS)

ATT_DVE_POS = [min(NDQ - 1, 120 + 14 * n) for n in range(16)]
C0_SILU_POS = [min(NDQ - 1, max(ATT_DVE_POS[-1] + 1 + 2 * k,
                                N_DOWN + (k + 1) * 16)) for k in range(MT)]

DEBUG = False
SKIP_ACC = False
ACC_BYPASS = False
PB_LOADS = 16
WST_PER_CH = 1 + 2 * MT + HT


def build_nc():
    nc = bass.Bass("TRN2", target_bir_lowering=False)

    x_rsT = nc.dram_tensor("x_rsT", [NCH * D, 128], F32, kind="ExternalInput")
    x_own = nc.dram_tensor("x_own", [TPC, D], F32, kind="ExternalInput")
    wq_s = nc.dram_tensor("wq_s", [D, HPC * DH], F32, kind="ExternalInput")
    wo_s = nc.dram_tensor("wo_s", [HPC * DH, D], F32, kind="ExternalInput")
    wkd = nc.dram_tensor("wkd", [D, DH], F32, kind="ExternalInput")
    wvd = nc.dram_tensor("wvd", [D, DH], F32, kind="ExternalInput")
    gqd = nc.dram_tensor("gqd", [MSL, D], I32, kind="ExternalInput")
    uqd = nc.dram_tensor("uqd", [MSL, D], I32, kind="ExternalInput")
    dqd = nc.dram_tensor("dqd", [D, MSL], I32, kind="ExternalInput")
    gamd = nc.dram_tensor("gamd", [MSL, 64], F32, kind="ExternalInput")
    uamd = nc.dram_tensor("uamd", [MSL, 64], F32, kind="ExternalInput")
    damd = nc.dram_tensor("damd", [D, 28], F32, kind="ExternalInput")
    idn = nc.dram_tensor("idn", [128, 128], F32, kind="ExternalInput")
    idnb = nc.dram_tensor("idnb", [128, 128], BF16, kind="ExternalInput")
    cmkb = nc.dram_tensor("cmkb", [128, 128], BF16, kind="ExternalInput")

    out_sl = nc.dram_tensor("out_sl", [NCH * D, 128], F32, kind="ExternalOutput")
    if DEBUG:
        dbg_xnT = nc.dram_tensor("dbg_xnT", [128, HT * TPC], BF16, kind="ExternalOutput")
        dbg_kT = nc.dram_tensor("dbg_kT", [128, TPC], BF16, kind="ExternalOutput")
        dbg_v = nc.dram_tensor("dbg_v", [128, 8 * 128], BF16, kind="ExternalOutput")
        dbg_g = nc.dram_tensor("dbg_g", [128, D], BF16, kind="ExternalOutput")
        dbg_u = nc.dram_tensor("dbg_u", [128, D], BF16, kind="ExternalOutput")
        dbg_d = nc.dram_tensor("dbg_d", [128, MSL], BF16, kind="ExternalOutput")
        dbg_rs = nc.dram_tensor("dbg_rs", [D, 128], F32, kind="ExternalOutput")
        dbg_ro = nc.dram_tensor("dbg_ro", [D, 128], F32, kind="ExternalOutput")
        dbg_rs3 = nc.dram_tensor("dbg_rs3", [D, 128], F32, kind="ExternalOutput")
        dbg_ro3 = nc.dram_tensor("dbg_ro3", [D, 128], F32, kind="ExternalOutput")
        dbg_qT = nc.dram_tensor("dbg_qT", [128, 8192], BF16, kind="ExternalOutput")
        dbg_avb = nc.dram_tensor("dbg_avb", [128, 8192], BF16, kind="ExternalOutput")
        dbg_hT = nc.dram_tensor("dbg_hT", [128, MT * TPC], BF16, kind="ExternalOutput")
        dbg_exs = nc.dram_tensor("dbg_exs", [128, 8192], BF16, kind="ExternalOutput")

    xnT_st = nc.dram_tensor("xnT_st", [128, HT * TPC], BF16)
    xnT_all = nc.dram_tensor("xnT_all", [NC * 128, HT * TPC], BF16)
    kT_st = nc.dram_tensor("kT_st", [128, TPC], BF16)
    kT_all = nc.dram_tensor("kT_all", [NC * 128, TPC], BF16)
    v_st = nc.dram_tensor("v_st", [128, 8 * 128], BF16)
    v_all = nc.dram_tensor("v_all", [NC * 128, 8 * 128], BF16)
    g_st = nc.dram_tensor("g_st", [128, MT * D], BF16)
    u_st = nc.dram_tensor("u_st", [128, MT * D], BF16)
    d_st = nc.dram_tensor("d_st", [128, HT * MSL], BF16)
    rs_in = nc.dram_tensor("rs_in", [NCH * 8 * D, 128], F32)
    rs_jnk = nc.dram_tensor("rs_jnk", [8 * D, 128], F32)
    rs_out = nc.dram_tensor("rs_out", [NCH * D, 128], F32)

    st = ExitStack()

    def sem(name):
        return st.enter_context(nc.semaphore(name))

    s_wa = sem("s_wa")
    s_waf = sem("s_waf")
    s_xin = sem("s_xin")
    s_code = [sem(f"s_code{i}") for i in range(NB_CODE)]
    s_lnr = sem("s_lnr")
    s_sta = sem("s_sta")
    s_lnv = sem("s_lnv")
    s_sd = sem("s_sd")
    s_lnn = sem("s_lnn")
    s_xt = sem("s_xt")
    s_evx = sem("s_evx")
    s_kv = sem("s_kv")
    s_kve = sem("s_kve")
    s_stg = sem("s_stg")
    s_ag = sem("s_ag")
    s_dqw = sem("s_dqw")
    s_dqt = sem("s_dqt")
    s_dqs = sem("s_dqs")
    s_p1l = sem("s_p1l")
    s_qpl = sem("s_qpl")
    s_xchl = sem("s_xchl")
    s_kvl = sem("s_kvl")
    s_wol = sem("s_wol")
    s_wxc = sem("s_wxc")
    s_wg2 = [sem("s_wg0"), sem("s_wg1")]
    s_wu2 = [sem("s_wu0"), sem("s_wu1")]
    s_wd2 = [sem("s_wd0"), sem("s_wd1")]
    s_wodp = [sem("s_wodp0"), sem("s_wodp1")]
    s_dnap = [sem(f"s_dnap{i}") for i in range(4)]
    s_qc = sem("s_qc")
    s_oc = sem("s_oc")
    s_xncf = sem("s_xncf")
    s_qev = sem("s_qev")
    s_pe1 = sem("s_pe1")
    s_exp = sem("s_exp")
    s_av1 = sem("s_av1")
    s_dnme = sem("s_dnme")
    s_dva = sem("s_dva")
    s_pbc = sem("s_pbc")
    s_abe = sem("s_abe")
    s_pva = sem("s_pva")
    s_wvd = sem("s_wvd")
    s_woe = sem("s_woe")
    s_wod = sem("s_wod")
    s_wst = sem("s_wst")
    s_gu = sem("s_gu")
    s_sig = sem("s_sig")
    s_hT = sem("s_hT")
    s_dn = sem("s_dn")
    s_dne = sem("s_dne")
    s_dnacc = sem("s_dnacc")
    s_rs = sem("s_rs")
    s_fin = sem("s_fin")
    s_fad = sem("s_fad")
    s_out = sem("s_out")

    identf = st.enter_context(nc.sbuf_tensor("identf", [128, 128], F32))
    identb = st.enter_context(nc.sbuf_tensor("identb", [128, 128], BF16))
    cmaskb = st.enter_context(nc.sbuf_tensor("cmaskb", [128, 128], BF16))
    onesr = st.enter_context(nc.sbuf_tensor("onesr", [1, 128], BF16))
    onesc = st.enter_context(nc.sbuf_tensor("onesc", [128, 1], BF16))
    gamx = st.enter_context(nc.sbuf_tensor("gamx", [128, MT * 64], F32))
    uamx = st.enter_context(nc.sbuf_tensor("uamx", [128, MT * 64], F32))
    damx = st.enter_context(nc.sbuf_tensor("damx", [128, HT * 28], F32))
    cq = st.enter_context(nc.sbuf_tensor("cq", [128, NB_CODE * 512], I32))
    qu16 = st.enter_context(nc.sbuf_tensor("qu16", [128, 512], U16))
    mby = [st.enter_context(nc.sbuf_tensor(f"mb{i}", [128, 512], U16))
           for i in range(4)]
    ut = st.enter_context(nc.sbuf_tensor("ut", [128, 8 * 512], FP16))
    wdt = st.enter_context(nc.sbuf_tensor("wdt", [128, 2 * 512], BF16))
    trs = st.enter_context(nc.sbuf_tensor("trs", [128, 2 * 512], BF16))
    lnS = st.enter_context(nc.sbuf_tensor("lnS", [128, 8], F32))
    lnQ = st.enter_context(nc.sbuf_tensor("lnQ", [128, 8], F32))
    lnNM = st.enter_context(nc.sbuf_tensor("lnNM", [128, 8], F32))
    lnM2 = st.enter_context(nc.sbuf_tensor("lnM2", [128, 8], F32))
    lnT2 = st.enter_context(nc.sbuf_tensor("lnT2", [128, 8], F32))
    lnV = st.enter_context(nc.sbuf_tensor("lnV", [128, 8], F32))
    lnSD = st.enter_context(nc.sbuf_tensor("lnSD", [128, 8], F32))
    lnR = st.enter_context(nc.sbuf_tensor("lnR", [128, 8], F32))
    dnm = st.enter_context(nc.sbuf_tensor("dnm", [1, 512], F32))
    rcp = st.enter_context(nc.sbuf_tensor("rcp", [1, 512], F32))
    rcpb = st.enter_context(nc.sbuf_tensor("rcpb", [1, 512], BF16))

    AR = 152 * 1024
    arena = st.enter_context(nc.sbuf_tensor("arena", [128, AR], U8))

    def cv(off, cols, dt):
        esz = {F32: 4, BF16: 2, FP16: 2, I32: 4, U16: 2}[dt]
        return arena[:, off:off + cols * esz].bitcast(dt)

    oP_xt = 0
    oP_sq = 16384
    oP_xnf = 32768
    oP_xtt = 65536
    oP_wstg = 81920
    oP_kts = 98304
    oP_vts = 100352
    oP_wkb = 102400
    oP_wvb = 110592
    oQ_wqb = 0
    oQ_woev = 65536
    oQ_xch = 32768
    oQ_wob = 32768
    oQ_qT = 65536
    oQ_exs = 81920
    oQ_avb = 98304
    oQ_ktb = 114688
    oQ_vtb = 118784
    oQ_ae = 122880
    oQ_be = 123904
    oQ_stg = 124928
    oF_xch = 0
    oF_wg = 65536
    oF_wu = 81920
    oF_wd = 98304
    oF_hT = 105472
    oF_dnev = 134144
    oF_sg = 150528
    oF_fr = 134144
    oF_fx = 138240
    assert oF_sg + 4096 <= AR

    pb = [st.enter_context(nc.psum_tensor(f"pb{i}", [128, 512], F32))
          for i in range(8)]

    blk = st.enter_context(nc.Block())

    def code_src(t):
        kind = t[0]
        if kind == "d":
            _, H, s = t
            return (bass.AP(dqd, H * 128 * MSL + DOF[s], [[MSL, 128], [1, DFD[s]]]),
                    DFD[s])
        _, k, s = t
        ten = gqd if kind == "g" else uqd
        return bass.AP(ten, k * 128 * D + s * 512, [[D, 128], [1, 512]]), 512

    @blk.sync
    def _(sync):
        sync.dma_start(out=identf[:, :], in_=idn[:, :]).then_inc(s_wa, 16)
        sync.dma_start(out=identb[:, :], in_=idnb[:, :]).then_inc(s_wa, 16)
        sync.dma_start(out=cmaskb[:, :], in_=cmkb[:, :]).then_inc(s_wa, 16)
        sync.wait_ge(s_wa, 48)
        sync.dma_start(
            out=cv(oP_wstg, D, F32),
            in_=bass.AP(wkd, 0, [[DH, 128], [128 * DH, HT], [1, DH]]),
        ).then_inc(s_wa, 16)
        sync.wait_ge(s_waf, 1)
        sync.dma_start(
            out=cv(oP_wstg, D, F32),
            in_=bass.AP(wvd, 0, [[DH, 128], [128 * DH, HT], [1, DH]]),
        ).then_inc(s_wa, 16)
        sync.wait_ge(s_wa, 80)
        sync.dma_start(out=gamx[:, :],
                       in_=bass.AP(gamd, 0, [[64, 128], [128 * 64, MT], [1, 64]])
                       ).then_inc(s_wa, 16)
        sync.dma_start(out=uamx[:, :],
                       in_=bass.AP(uamd, 0, [[64, 128], [128 * 64, MT], [1, 64]])
                       ).then_inc(s_wa, 16)
        sync.dma_start(out=damx[:, :],
                       in_=bass.AP(damd, 0, [[28, 128], [128 * 28, HT], [1, 28]])
                       ).then_inc(s_wa, 16)
        for tt in range(8):
            if tt >= 1:
                sync.wait_ge(s_lnn, tt)
            sync.dma_start(out=cv(oP_xt, D, F32),
                           in_=x_own[tt * 128:(tt + 1) * 128, :]).then_inc(s_xin, 16)
        for i in range(NB_CODE):
            src, w = code_src(DQ_TILES[i])
            sync.dma_start(out=cq[:, (i % NB_CODE) * 512:(i % NB_CODE) * 512 + w],
                           in_=src).then_inc(s_code[i % NB_CODE], 16)
        for i, til in enumerate(DQ_TILES):
            fd = dq_fd(til)
            ob = (i % 2) * 512
            sync.wait_ge(s_dqw, i + 1)
            if i >= 2:
                sync.wait_ge(s_dqs, 16 * (i - 1))
            for j in range(fd // 128):
                sync.dma_start_transpose(
                    trs[:, ob + j * 128: ob + (j + 1) * 128],
                    wdt[:, ob + j * 128: ob + (j + 1) * 128],
                ).then_inc(s_dqt, 16)
            sync.wait_ge(s_dqt, 16 * DQ_CUMP[i + 1])
            kind = til[0]
            if kind == "d":
                _, H, s = til
                dst = d_st[:, H * MSL + DOF[s]: H * MSL + DOF[s] + fd]
            elif kind == "g":
                _, k, s = til
                dst = g_st[:, k * D + s * 512: k * D + (s + 1) * 512]
            else:
                _, k, s = til
                dst = u_st[:, k * D + s * 512: k * D + (s + 1) * 512]
            sync.dma_start(out=dst, in_=trs[:, ob:ob + fd]).then_inc(s_dqs, 16)
            nx = i + NB_CODE
            if nx < NDQ:
                src, w = code_src(DQ_TILES[nx])
                sync.dma_start(out=cq[:, (nx % NB_CODE) * 512:(nx % NB_CODE) * 512 + w],
                               in_=src).then_inc(s_code[nx % NB_CODE], 16)

    @blk.scalar
    def _(scalar):
        scalar.wait_ge(s_wa, 64)
        scalar.activation(cv(oP_wkb, D, BF16), cv(oP_wstg, D, F32),
                          AF.Copy).then_inc(s_waf, 1)
        scalar.wait_ge(s_wa, 80)
        scalar.activation(cv(oP_wvb, D, BF16), cv(oP_wstg, D, F32),
                          AF.Copy).then_inc(s_waf, 1)
        for tt in range(8):
            scalar.wait_ge(s_xin, (tt + 1) * 16)
            scalar.activation(cv(oP_sq, D, F32), cv(oP_xt, D, F32), AF.Square,
                              accum_out=lnQ[:, tt:tt + 1])
            scalar.wait_ge(s_lnr, tt + 1)
            scalar.activation(lnNM[:, tt:tt + 1], lnS[:, tt:tt + 1], AF.Copy,
                              scale=-1.0 / D)
            scalar.activation(lnM2[:, tt:tt + 1], lnNM[:, tt:tt + 1], AF.Square)
            scalar.activation(lnT2[:, tt:tt + 1], lnQ[:, tt:tt + 1], AF.Copy,
                              scale=1.0 / D, bias=EPS).then_inc(s_sta, 1)
            scalar.wait_ge(s_lnv, tt + 1)
            scalar.sqrt(lnSD[:, tt:tt + 1], lnV[:, tt:tt + 1]).then_inc(s_sd, 1)
            xb = oP_xtt + (tt % 2) * 8192
            if tt >= 2:
                scalar.wait_ge(s_kv, 2 * (tt - 1))
                scalar.wait_ge(s_stg, 16 * 4 * (tt - 1))
            for g in range(8):
                scalar.wait_ge(s_xt, tt * 8 + g + 1)
                scalar.activation(cv(xb + g * 1024, 512, BF16),
                                  pb[4 + g % 2][:, :],
                                  AF.Copy).then_inc(s_evx, 1)
            scalar.wait_ge(s_kv, 2 * tt + 1)
            scalar.activation(cv(oP_kts + tt * 256, 128, BF16), pb[0][:, 0:128], AF.Copy)
            scalar.wait_ge(s_kv, 2 * tt + 2)
            scalar.activation(cv(oP_vts + tt * 256, 128, BF16), pb[1][:, 0:128],
                              AF.Copy).then_inc(s_kve, 1)

        g_exp = 0
        for b in range(4):
            for g in range(4):
                scalar.wait_ge(s_qpl, (b * 4 + g + 1) * 16)
                scalar.activation(cv(oQ_wqb + g * 8192, 4096, BF16),
                                  cv(oQ_stg, 4096, F32), AF.Copy).then_inc(s_qc, 1)
            for cs in range(4):
                scalar.wait_ge(s_xncf, b * 4 + cs + 1)
                if cs == 0:
                    scalar.wait_ge(s_wodp[0], b * 16 * 16)
                    scalar.wait_ge(s_wodp[1], b * 16 * 16)
                for dp in range(4):
                    ins = scalar.activation(
                        cv(oQ_qT + dp * 4096 + (cs // 2) * 2048 + (cs % 2) * 1024,
                           512, BF16),
                        pb[dp][:, :], AF.Copy)
                ins.then_inc(s_qev, 1)
            scalar.wait_ge(s_xncf, b * 4 + 4)
            for g in range(4):
                scalar.wait_ge(s_wol, (b * 4 + g + 1) * 16)
                scalar.activation(cv(oQ_wob + g * 8192, 4096, BF16),
                                  cv(oQ_stg, 4096, F32), AF.Copy).then_inc(s_oc, 1)
            for h in range(HPC):
                for qs in range(4):
                    uq_i = (b * 4 + h) * 4 + qs
                    if uq_i >= 1:
                        scalar.wait_ge(s_av1, uq_i)
                    for pi, (kb, lo) in enumerate(ATT_P[qs]):
                        g_exp += 1
                        scalar.wait_ge(s_pe1, g_exp)
                        slot = pi % 4
                        scalar.activation(
                            cv(oQ_exs + kb * 1024 + lo * 2, 512 - lo, BF16),
                            pb[4 + slot][:, lo:512],
                            AF.Exp, scale=SCL).then_inc(s_exp, 1)
                    scalar.wait_ge(s_av1, uq_i + 1)
                    if uq_i >= 1:
                        scalar.wait_ge(s_pbc, uq_i)
                    scalar.activation(dnm[0:1, :], pb[1][0:1, :], AF.Ln)
                    scalar.activation(rcpb[0:1, :], dnm[0:1, :], AF.Exp,
                                      scale=-1.0).then_inc(s_dva, 1)
                    scalar.wait_ge(s_pbc, uq_i + 1)
                    if uq_i >= 1:
                        scalar.wait_ge(s_pva, uq_i)
                    scalar.activation(cv(oQ_ae, 512, BF16), pb[0][:, :], AF.Copy)
                    scalar.activation(cv(oQ_be, 512, BF16), pb[2][:, :],
                                      AF.Copy).then_inc(s_abe, 1)
            for ct in range(32):
                scalar.wait_ge(s_wvd, b * 32 + ct + 1)
                if b * 32 + ct >= 2:
                    scalar.wait_ge(s_wodp[ct % 2], (b * 16 + ct // 2) * 16)
                for ts4 in range(4):
                    ins = scalar.activation(
                        cv(oQ_woev + (ct % 2) * 8192 + ts4 * 2048, 512, F32),
                        pb[(ct % 2) * 4 + ts4][:, :], AF.Copy)
                ins.then_inc(s_woe, 1)

        for c in range(NCH):
            for k in range(MT):
                scalar.wait_ge(s_gu, c * MT + k + 1)
                if c * MT + k >= 2:
                    scalar.wait_ge(s_hT, c * MT + k - 1)
                for half in range(2):
                    ins = scalar.activation(
                        cv(oF_sg + (k % 2) * 2048 + half * 1024, 512, BF16),
                        pb[(k % 2) * 4 + half][:, :], AF.Sigmoid)
                ins.then_inc(s_sig, 1)
            for ho in range(HT):
                scalar.wait_ge(s_dn, c * HT + ho + 1)
                if c * HT + ho >= 4:
                    scalar.wait_ge(s_dnap[ho % 4], (c * 8 + ho // 4) * 16)
                for half in range(2):
                    ins = scalar.activation(
                        cv(oF_dnev + (ho % 4) * 4096 + half * 2048, 512, F32),
                        pb[2 * (ho % 4) + half][:, :], AF.Copy)
                ins.then_inc(s_dne, 1)

    @blk.vector
    def _(vector):
        vector.memset(onesr[0:1, :], 1.0)
        vector.memset(onesc[:, :], 1.0)
        for tt in range(8):
            vector.wait_ge(s_xin, (tt + 1) * 16)
            vector.tensor_reduce(lnS[:, tt:tt + 1], cv(oP_xt, D, F32), AX,
                                 ALU.add).then_inc(s_lnr, 1)
            vector.wait_ge(s_sta, tt + 1)
            vector.tensor_tensor(lnV[:, tt:tt + 1], lnT2[:, tt:tt + 1],
                                 lnM2[:, tt:tt + 1], ALU.subtract).then_inc(s_lnv, 1)
            vector.wait_ge(s_sd, tt + 1)
            vector.reciprocal(lnR[:, tt:tt + 1], lnSD[:, tt:tt + 1])
            if tt >= 2:
                vector.wait_ge(s_xt, (tt - 1) * 8)
            vector.tensor_scalar(cv(oP_xnf + (tt % 2) * 16384, D, F32),
                                 cv(oP_xt, D, F32), lnNM[:, tt:tt + 1],
                                 lnR[:, tt:tt + 1], ALU.add, ALU.mult
                                 ).then_inc(s_lnn, 1)

        def emit_silu(c, k):
            sgb = oF_sg + (k % 2) * 2048
            vector.wait_ge(s_sig, c * MT + k + 1)
            for half in range(2):
                vector.tensor_tensor(cv(sgb + half * 1024, 512, BF16),
                                     cv(sgb + half * 1024, 512, BF16),
                                     pb[(k % 2) * 4 + half][:, :], ALU.mult)
            for half in range(2):
                ins = vector.tensor_tensor(
                    cv(oF_hT + k * 2048 + half * 1024, 512, BF16),
                    cv(sgb + half * 1024, 512, BF16),
                    pb[(k % 2) * 4 + 2 + half][:, :], ALU.mult)
            ins.then_inc(s_hT, 1)

        vector.wait_ge(s_wa, 128)
        for i, til in enumerate(DQ_TILES):
            fd = dq_fd(til)
            slot = (i % NB_CODE) * 512
            ob = (i % 2) * 512
            vector.wait_ge(s_code[i % NB_CODE], (i // NB_CODE + 1) * 16)
            vector.tensor_copy(qu16[:, 0:fd], cq[:, slot:slot + fd])
            vector.tensor_scalar(mby[0][:, 0:fd], qu16[:, 0:fd], 1, None,
                                 ALU.bitwise_and)
            vector.tensor_scalar(mby[1][:, 0:fd], qu16[:, 0:fd], 2, None,
                                 ALU.bitwise_and)
            vector.tensor_scalar(mby[2][:, 0:fd], qu16[:, 0:fd], 4, None,
                                 ALU.bitwise_and)
            vector.tensor_scalar(mby[3][:, 0:fd], qu16[:, 0:fd], 8, None,
                                 ALU.bitwise_and)
            for j in range(8):
                c0, c1 = NF4[2 * j], NF4[2 * j + 1]
                vector.tensor_scalar(ut[:, j * 512:j * 512 + fd], mby[0][:, 0:fd],
                                     c1 - c0, c0, ALU.mult, ALU.add)
            for j in range(4):
                vector.copy_predicated(ut[:, (2 * j) * 512:(2 * j) * 512 + fd],
                                       mby[1][:, 0:fd],
                                       ut[:, (2 * j + 1) * 512:(2 * j + 1) * 512 + fd])
            vector.copy_predicated(ut[:, 0:fd], mby[2][:, 0:fd],
                                   ut[:, 1024:1024 + fd])
            vector.copy_predicated(ut[:, 2048:2048 + fd], mby[2][:, 0:fd],
                                   ut[:, 3072:3072 + fd])
            vector.copy_predicated(ut[:, 0:fd], mby[3][:, 0:fd],
                                   ut[:, 2048:2048 + fd])
            if i >= 2:
                vector.wait_ge(s_dqt, 16 * DQ_CUMP[i - 1])
            kind = til[0]
            ins = None
            for bb in range(fd // 64):
                if kind == "d":
                    _, H, s = til
                    amc = damx[:, H * 28 + DOF[s] // 64 + bb:
                               H * 28 + DOF[s] // 64 + bb + 1]
                elif kind == "g":
                    _, k, s = til
                    amc = gamx[:, k * 64 + s * 8 + bb: k * 64 + s * 8 + bb + 1]
                else:
                    _, k, s = til
                    amc = uamx[:, k * 64 + s * 8 + bb: k * 64 + s * 8 + bb + 1]
                ins = vector.tensor_scalar(wdt[:, ob + bb * 64: ob + (bb + 1) * 64],
                                           ut[:, bb * 64:(bb + 1) * 64],
                                           amc, None, ALU.mult)
            ins.then_inc(s_dqw, 1)
        for k0 in range(MT):
            emit_silu(0, k0)

        def emit_fin(cc):
            for qtr in range(4):
                vector.wait_ge(s_fin, (cc * 4 + qtr + 1) * 32)
                vector.tensor_tensor(cv(oF_fr, 1024, F32), cv(oF_fr, 1024, F32),
                                     cv(oF_fx, 1024, F32),
                                     ALU.add).then_inc(s_fad, 1)

        for c in range(1, NCH):
            for k in range(MT):
                emit_silu(c, k)
                if k == 11:
                    emit_fin(c - 1)
        emit_fin(NCH - 1)

    @blk.tensor
    def _(tensor):
        tensor.wait_ge(s_wa, 48)
        for tt in range(8):
            xb = oP_xnf + (tt % 2) * 16384
            tensor.wait_ge(s_lnn, tt + 1)
            for g in range(8):
                if tt * 8 + g >= 2:
                    tensor.wait_ge(s_evx, tt * 8 + g - 1)
                for q4 in range(4):
                    hh = g * 4 + q4
                    ins = tensor.transpose(
                        pb[4 + g % 2][:, q4 * 128:(q4 + 1) * 128],
                        cv(xb + hh * 512, 128, F32), identf[:, :])
                ins.then_inc(s_xt, 1)
            tensor.wait_ge(s_waf, 2)
            tensor.wait_ge(s_evx, (tt + 1) * 8)
            if tt >= 1:
                tensor.wait_ge(s_kve, tt)
            xtb = oP_xtt + (tt % 2) * 8192
            for H in range(HT):
                ins = tensor.matmul(pb[0][:, 0:128], cv(oP_wkb + H * 256, 128, BF16),
                                    cv(xtb + H * 256, 128, BF16),
                                    start=(H == 0), stop=(H == HT - 1))
            ins.then_inc(s_kv, 1)
            for H in range(HT):
                ins = tensor.matmul(pb[1][:, 0:128], cv(xtb + H * 256, 128, BF16),
                                    cv(oP_wvb + H * 256, 128, BF16),
                                    start=(H == 0), stop=(H == HT - 1))
            ins.then_inc(s_kv, 1)

        tensor.wait_ge(s_lnn, 8)
        tensor.wait_ge(s_kve, 8)
        tensor.wait_ge(s_evx, 64)
        g_pe = 0
        for b in range(4):
            for cs in range(4):
                tensor.wait_ge(s_qc, b * 4 + 4)
                if cs == 0 and b >= 1:
                    tensor.wait_ge(s_woe, b * 32)
                tensor.wait_ge(s_xchl, (b * 4 + cs + 1) * 16)
                if b * 4 + cs >= 1:
                    tensor.wait_ge(s_qev, b * 4 + cs)
                for dp in range(4):
                    for H in range(HT):
                        ins = tensor.matmul(pb[dp][:, :],
                                            cv(oQ_wqb + H * 1024 + dp * 256, 128, BF16),
                                            cv(oQ_xch + H * 1024, 512, BF16),
                                            start=(H == 0), stop=(H == HT - 1))
                tensor.nop()
                ins.then_inc(s_xncf, 1)
            tensor.wait_ge(s_kvl, (b + 1) * 64)
            for h in range(HPC):
                for qs in range(4):
                    uq_i = (b * 4 + h) * 4 + qs
                    tensor.wait_ge(s_qev, b * 4 + 4)
                    for pi, (kb, lo) in enumerate(ATT_P[qs]):
                        slot = pi % 4
                        g_pe += 1
                        if g_pe >= 5:
                            tensor.wait_ge(s_exp, g_pe - 4)
                        diag = kb >= 4 * qs
                        ins = tensor.matmul(pb[4 + slot][:, lo:512],
                                            cv(oQ_ktb + kb * 256, 128, BF16),
                                            cv(oQ_qT + h * 4096 + qs * 1024 + lo * 2,
                                               512 - lo, BF16),
                                            start=True, stop=not diag)
                        if diag:
                            ins = tensor.matmul(
                                pb[4 + slot][:, lo:lo + 128],
                                identb[:, :], cmaskb[:, :], start=False, stop=True)
                        ins.then_inc(s_pe1, 1)
                    tensor.wait_ge(s_exp, g_pe)
                    if uq_i >= 1:
                        tensor.wait_ge(s_abe, uq_i)
                        tensor.wait_ge(s_dva, uq_i)
                    np_ = len(ATT_P[qs])
                    for pi, (kb, lo) in enumerate(ATT_P[qs]):
                        tensor.matmul(pb[0][:, lo:512],
                                      cv(oQ_vtb + kb * 256, 128, BF16),
                                      cv(oQ_exs + kb * 1024 + lo * 2, 512 - lo, BF16),
                                      start=(pi == 0), stop=(pi == np_ - 1))
                    for pi, (kb, lo) in enumerate(ATT_P[qs]):
                        ins = tensor.matmul(pb[1][0:1, lo:512],
                                            onesc[:, :],
                                            cv(oQ_exs + kb * 1024 + lo * 2,
                                               512 - lo, BF16),
                                            start=(pi == 0), stop=(pi == np_ - 1))
                    ins.then_inc(s_av1, 1)
                    tensor.wait_ge(s_dva, uq_i + 1)
                    tensor.matmul(pb[2][:, :], onesr[0:1, :], rcpb[0:1, :],
                                  start=True, stop=True).then_inc(s_pbc, 1)
            tensor.wait_ge(s_pva, (b + 1) * 16)
            tensor.wait_ge(s_oc, (b + 1) * 4)
            for ct in range(32):
                if b * 32 + ct >= 2:
                    tensor.wait_ge(s_woe, b * 32 + ct - 1)
                for dp in range(4):
                    for ts4 in range(4):
                        ins = tensor.matmul(pb[(ct % 2) * 4 + ts4][:, :],
                                            cv(oQ_wob + dp * 8192 + ct * 256, 128, BF16),
                                            cv(oQ_avb + dp * 4096 + ts4 * 1024,
                                               512, BF16),
                                            start=(dp == 0), stop=(dp == 3))
                ins.then_inc(s_wvd, 1)

        tensor.wait_ge(s_woe, 128)
        for c in range(NCH):
            tensor.wait_ge(s_wxc, (c + 1) * 16)
            if c >= 1:
                tensor.wait_ge(s_dne, c * HT)
            for k in range(MT):
                nslot = c * (MT // 2) + k // 2 + 1
                tensor.wait_ge(s_wg2[k % 2], nslot * 16)
                tensor.wait_ge(s_wu2[k % 2], nslot * 16)
                if c * MT + k >= 2:
                    tensor.wait_ge(s_hT, c * MT + k - 1)
                for half in range(2):
                    for H in range(HT):
                        tensor.matmul(pb[(k % 2) * 4 + half][:, :],
                                      cv(oF_wg + (k % 2) * 8192 + H * 256, 128, BF16),
                                      cv(oF_xch + H * 2048 + half * 1024, 512, BF16),
                                      start=(H == 0), stop=(H == HT - 1))
                for half in range(2):
                    for H in range(HT):
                        ins = tensor.matmul(
                            pb[(k % 2) * 4 + 2 + half][:, :],
                            cv(oF_wu + (k % 2) * 8192 + H * 256, 128, BF16),
                            cv(oF_xch + H * 2048 + half * 1024, 512, BF16),
                            start=(H == 0), stop=(H == HT - 1))
                ins.then_inc(s_gu, 1)
            tensor.wait_ge(s_hT, (c + 1) * MT)
            for ho in range(HT):
                tensor.wait_ge(s_wd2[ho % 2], (c * (HT // 2) + ho // 2 + 1) * 16)
                if c * HT + ho >= 4:
                    tensor.wait_ge(s_dne, c * HT + ho - 3)
                for k in range(MT):
                    for half in range(2):
                        ins = tensor.matmul(
                            pb[2 * (ho % 4) + half][:, :],
                            cv(oF_wd + (ho % 2) * 3584 + k * 256, 128, BF16),
                            cv(oF_hT + k * 2048 + half * 1024, 512, BF16),
                            start=(k == 0), stop=(k == MT - 1))
                ins.then_inc(s_dn, 1)

    @blk.gpsimd
    def _(gpsimd):
        for tt in range(8):
            gpsimd.wait_ge(s_evx, (tt + 1) * 8)
            xtb = oP_xtt + (tt % 2) * 8192
            for g4 in range(4):
                gpsimd.dma_start(
                    out=bass.AP(xnT_st, g4 * 8 * TPC + tt * 128,
                                [[HT * TPC, 128], [TPC, 8], [1, 128]]),
                    in_=cv(xtb + g4 * 2048, 1024, BF16),
                ).then_inc(s_stg, 16)
        gpsimd.wait_ge(s_kve, 8)
        gpsimd.dma_start(out=kT_st[:, :], in_=cv(oP_kts, TPC, BF16)).then_inc(s_stg, 16)
        gpsimd.dma_start(out=v_st[:, :], in_=cv(oP_vts, TPC, BF16)).then_inc(s_stg, 16)
        gpsimd.wait_ge(s_stg, 16 * 34)
        gpsimd.collective_compute(
            "AllGather", ALU.bypass, replica_groups=[list(range(NC))],
            ins=[kT_st[:, :]], outs=[kT_all[:, :]]).then_inc(s_ag, 1)
        gpsimd.collective_compute(
            "AllGather", ALU.bypass, replica_groups=[list(range(NC))],
            ins=[v_st[:, :]], outs=[v_all[:, :]]).then_inc(s_ag, 1)
        gpsimd.collective_compute(
            "AllGather", ALU.bypass, replica_groups=[list(range(NC))],
            ins=[xnT_st[:, :]], outs=[xnT_all[:, :]]).then_inc(s_ag, 1)
        gpsimd.wait_ge(s_ag, 3)

        for b in range(4):
            for g in range(4):
                gpsimd.wait_ge(s_qc, b * 4 + g)
                if g == 0:
                    gpsimd.wait_ge(s_oc, b * 4)
                gpsimd.dma_start(
                    out=cv(oQ_stg, 4096, F32),
                    in_=bass.AP(wq_s, g * 8 * 128 * 512,
                                [[512, 128], [128 * 512, 8], [1, 512]]),
                ).then_inc(s_qpl, 16)
            for cs in range(4):
                c = 2 * b + cs // 2
                s = cs % 2
                gpsimd.wait_ge(s_xncf, b * 4 + cs)
                if cs == 0:
                    gpsimd.wait_ge(s_wvd, b * 32)
                gpsimd.dma_start(
                    out=cv(oQ_xch, HT * 512, BF16),
                    in_=bass.AP(xnT_all, c * 128 * HT * TPC + s * 512,
                                [[HT * TPC, 128], [TPC, HT], [1, 512]]),
                ).then_inc(s_xchl, 16)
            gpsimd.wait_ge(s_av1, b * 16)
            for half in range(2):
                cc = 2 * b + half
                gpsimd.dma_start(
                    out=cv(oQ_ktb + half * 2048, TPC, BF16),
                    in_=kT_all[cc * 128:(cc + 1) * 128, :]).then_inc(s_kvl, 16)
            for half in range(2):
                cc = 2 * b + half
                gpsimd.dma_start(
                    out=cv(oQ_vtb + half * 2048, TPC, BF16),
                    in_=v_all[cc * 128:(cc + 1) * 128, :]).then_inc(s_kvl, 16)
            for g in range(4):
                gpsimd.wait_ge(s_oc, b * 4 + g)
                if g == 0:
                    gpsimd.wait_ge(s_qc, (b + 1) * 4)
                gpsimd.dma_start(out=cv(oQ_stg, 4096, F32),
                                 in_=wo_s[g * 128:(g + 1) * 128, :]
                                 ).then_inc(s_wol, 16)
            gpsimd.wait_ge(s_wvd, b * 32)
            for h in range(HPC):
                for qs in range(4):
                    uq_i = (b * 4 + h) * 4 + qs
                    gpsimd.wait_ge(s_abe, uq_i + 1)
                    gpsimd.tensor_tensor(
                        cv(oQ_avb + h * 4096 + qs * 1024, 512, BF16),
                        cv(oQ_ae, 512, BF16), cv(oQ_be, 512, BF16),
                        ALU.mult).then_inc(s_pva, 1)
            for ct in range(32):
                gpsimd.wait_ge(s_woe, b * 32 + ct + 1)
                gpsimd.dma_start(
                    out=bass.AP(rs_in, (2 * b * 8 * D + ct * 128) * 128,
                                [[128, 128], [D * 128, 16], [1, 128]]),
                    in_=cv(oQ_woev + (ct % 2) * 8192, 2048, F32),
                ).then_inc(s_wodp[ct % 2], 16)
            if DEBUG and b == 0:
                gpsimd.dma_start(out=dbg_qT[:, :],
                                 in_=cv(oQ_qT, 8192, BF16)).then_inc(s_stg, 16)
                gpsimd.dma_start(out=dbg_avb[:, :],
                                 in_=cv(oQ_avb, 8192, BF16)).then_inc(s_stg, 16)
                gpsimd.dma_start(out=dbg_exs[:, :],
                                 in_=cv(oQ_exs, 8192, BF16)).then_inc(s_stg, 16)
                gpsimd.wait_ge(s_stg, 16 * 37)

        def pool_fin(cc):
            gpsimd.wait_ge(s_rs, cc + 1)
            for qtr in range(4):
                if cc * 4 + qtr >= 1:
                    gpsimd.wait_ge(s_out, (cc * 4 + qtr) * 16)
                gpsimd.dma_start(
                    out=cv(oF_fr, 1024, F32),
                    in_=bass.AP(rs_out, (cc * D + qtr * 1024) * 128,
                                [[128, 128], [128 * 128, 8], [1, 128]]),
                ).then_inc(s_fin, 16)
                gpsimd.dma_start(
                    out=cv(oF_fx, 1024, F32),
                    in_=bass.AP(x_rsT, (cc * D + qtr * 1024) * 128,
                                [[128, 128], [128 * 128, 8], [1, 128]]),
                ).then_inc(s_fin, 16)
                gpsimd.wait_ge(s_fad, cc * 4 + qtr + 1)
                gpsimd.dma_start(
                    out=bass.AP(out_sl, (cc * D + qtr * 1024) * 128,
                                [[128, 128], [128 * 128, 8], [1, 128]]),
                    in_=cv(oF_fr, 1024, F32),
                ).then_inc(s_out, 16)

        gpsimd.wait_ge(s_wvd, 4 * 32)
        gpsimd.wait_ge(s_wodp[0], 4 * 16 * 16)
        gpsimd.wait_ge(s_wodp[1], 4 * 16 * 16)
        for c in range(NCH):
            gpsimd.wait_ge(s_gu, c * MT)
            gpsimd.dma_start(out=cv(oF_xch, HT * TPC, BF16),
                             in_=xnT_all[c * 128:(c + 1) * 128, :]).then_inc(s_wxc, 16)
            for k in range(MT):
                gpsimd.wait_ge(s_dqs, (N_DOWN + (k + 1) * 16) * 16)
                if c * MT + k >= 2:
                    gpsimd.wait_ge(s_gu, c * MT + k - 1)
                gpsimd.dma_start(out=cv(oF_wg + (k % 2) * 8192, D, BF16),
                                 in_=g_st[:, k * D:(k + 1) * D]).then_inc(s_wg2[k % 2], 16)
                gpsimd.dma_start(out=cv(oF_wu + (k % 2) * 8192, D, BF16),
                                 in_=u_st[:, k * D:(k + 1) * D]).then_inc(s_wu2[k % 2], 16)
            if c >= 1:
                pool_fin(c - 1)
            if DEBUG and c == 3:
                gpsimd.wait_ge(s_hT, 4 * MT)
                gpsimd.dma_start(out=dbg_hT[:, :],
                                 in_=cv(oF_hT, MT * TPC, BF16)).then_inc(s_stg, 16)
                gpsimd.wait_ge(s_stg, 16 * 35)
            for ho in range(HT):
                gpsimd.wait_ge(s_dqs, N_DOWN * 16)
                if c * HT + ho >= 2:
                    gpsimd.wait_ge(s_dn, c * HT + ho - 1)
                gpsimd.dma_start(out=cv(oF_wd + (ho % 2) * 3584, MSL, BF16),
                                 in_=d_st[:, ho * MSL:(ho + 1) * MSL]).then_inc(s_wd2[ho % 2], 16)
                gpsimd.wait_ge(s_dne, c * HT + ho + 1)
                acc_dst = (bass.AP(rs_jnk, (ho * 128) * 128,
                                   [[128, 128], [D * 128, 8], [1, 128]])
                           if SKIP_ACC else
                           bass.AP(rs_in, (c * 8 * D + ho * 128) * 128,
                                   [[128, 128], [D * 128, 8], [1, 128]]))
                gpsimd.dma_start(
                    out=acc_dst,
                    in_=cv(oF_dnev + (ho % 4) * 4096, 1024, F32),
                    accum_op=(ALU.bypass if ACC_BYPASS else ALU.add)
                    ).then_inc(s_dnap[ho % 4], 16)
            for sl in range(4):
                gpsimd.wait_ge(s_dnap[sl], (c + 1) * 8 * 16)
            gpsimd.collective_compute(
                "ReduceScatter", ALU.add, replica_groups=[list(range(NC))],
                ins=[rs_in[c * 8 * D:(c + 1) * 8 * D, :]],
                outs=[rs_out[c * D:(c + 1) * D, :]]).then_inc(s_rs, 1)
        pool_fin(NCH - 1)
        if DEBUG:
            gpsimd.dma_start(out=dbg_xnT[:, :], in_=xnT_st[:, :]).then_inc(s_stg, 16)
            gpsimd.dma_start(out=dbg_kT[:, :], in_=kT_st[:, :]).then_inc(s_stg, 16)
            gpsimd.dma_start(out=dbg_v[:, :], in_=v_st[:, :]).then_inc(s_stg, 16)
            gpsimd.dma_start(out=dbg_g[:, :], in_=g_st[:, 0:D]).then_inc(s_stg, 16)
            gpsimd.dma_start(out=dbg_u[:, :], in_=u_st[:, 0:D]).then_inc(s_stg, 16)
            gpsimd.dma_start(out=dbg_d[:, :], in_=d_st[:, 0:MSL]).then_inc(s_stg, 16)
            gpsimd.dma_start(out=dbg_rs[:, :], in_=rs_in[0:D, :]).then_inc(s_stg, 16)
            gpsimd.dma_start(out=dbg_ro[:, :], in_=rs_out[0:D, :]).then_inc(s_stg, 16)
            gpsimd.dma_start(out=dbg_rs3[:, :],
                             in_=rs_in[3 * 8 * D:3 * 8 * D + D, :]).then_inc(s_stg, 16)
            gpsimd.dma_start(out=dbg_ro3[:, :],
                             in_=rs_out[3 * D:4 * D, :]).then_inc(s_stg, 16)
            gpsimd.wait_ge(s_stg, 16 * 48 - 16)
        gpsimd.wait_ge(s_out, NCH * 4 * 16)

    st.close()
    return nc


_NC_CACHE = [None]


def kernel(**inputs):
    x = np.ascontiguousarray(np.asarray(inputs["L0_attn_x"], dtype=np.float32))
    wq = np.asarray(inputs["L0_attn_wq"], dtype=np.float32)
    wkf = np.ascontiguousarray(np.asarray(inputs["L0_attn_wk"], dtype=np.float32))
    wvf = np.ascontiguousarray(np.asarray(inputs["L0_attn_wv"], dtype=np.float32))
    wo = np.asarray(inputs["L0_attn_wo"], dtype=np.float32)
    gqf = np.asarray(inputs["L0_ffn_gate_q"], dtype=np.int32)
    gaf = np.asarray(inputs["L0_ffn_gate_absmax"], dtype=np.float32).reshape(M, 64)
    uqf = np.asarray(inputs["L0_ffn_up_q"], dtype=np.int32)
    uaf = np.asarray(inputs["L0_ffn_up_absmax"], dtype=np.float32).reshape(M, 64)
    dqf = np.asarray(inputs["L0_ffn_down_q"], dtype=np.int32)
    daf = np.asarray(inputs["L0_ffn_down_absmax"], dtype=np.float32).reshape(D, 224)

    idn = np.eye(128, dtype=np.float32)
    idnb_ = np.eye(128).astype(ml_dtypes.bfloat16)
    cm = np.where(np.arange(128)[:, None] > np.arange(128)[None, :],
                  np.float32(-30000.0), np.float32(0.0)).astype(ml_dtypes.bfloat16)

    in_maps = []
    for r in range(NC):
        b, half = r // 2, r % 2
        slabs = []
        for c in range(NCH):
            t0 = TPC * (c % 2) + 128 * r
            slabs.append(np.ascontiguousarray(x[c // 2, t0:t0 + 128, :].T))
        in_maps.append({
            "x_own": np.ascontiguousarray(x[b, half * TPC:(half + 1) * TPC, :]),
            "x_rsT": np.concatenate(slabs, axis=0),
            "wq_s": np.ascontiguousarray(wq[:, r * 512:(r + 1) * 512]),
            "wo_s": np.ascontiguousarray(wo[r * 512:(r + 1) * 512, :]),
            "wkd": wkf, "wvd": wvf,
            "gqd": np.ascontiguousarray(gqf[r * MSL:(r + 1) * MSL, :]),
            "uqd": np.ascontiguousarray(uqf[r * MSL:(r + 1) * MSL, :]),
            "dqd": np.ascontiguousarray(dqf[:, r * MSL:(r + 1) * MSL]),
            "gamd": np.ascontiguousarray(gaf[r * MSL:(r + 1) * MSL, :]),
            "uamd": np.ascontiguousarray(uaf[r * MSL:(r + 1) * MSL, :]),
            "damd": np.ascontiguousarray(daf[:, r * 28:(r + 1) * 28]),
            "idn": idn, "idnb": idnb_, "cmkb": cm,
        })

    if _NC_CACHE[0] is None:
        _NC_CACHE[0] = build_nc()
    res = run_bass_kernel_spmd(_NC_CACHE[0], in_maps, core_ids=list(range(NC)))
    _NC_CACHE.append(res)

    out = np.empty((B, T, D), dtype=np.float32)
    for r in range(NC):
        arr = np.asarray(res.results[r]["out_sl"]).reshape(NCH, D, 128)
        for c in range(NCH):
            t0 = TPC * (c % 2) + 128 * r
            out[c // 2, t0:t0 + 128, :] = arr[c].T
    return out


# version 07
# speedup vs baseline: 1.3891x; 1.2310x over previous
import numpy as np
import ml_dtypes
from contextlib import ExitStack

import concourse.bass as bass
import concourse.mybir as mybir
from concourse.bass_utils import run_bass_kernel_spmd

F32 = mybir.dt.float32
BF16 = mybir.dt.bfloat16
FP16 = mybir.dt.float16
I32 = mybir.dt.int32
U16 = mybir.dt.uint16
U8 = mybir.dt.uint8
AF = mybir.ActivationFunctionType
ALU = mybir.AluOpType
AX = mybir.AxisListType.X

B, T, D = 4, 2048, 4096
NC, TPC, NCH = 8, 1024, 8
HPC, DH = 4, 128
M = 14336
MSL, MT, HT = M // NC, (M // NC) // 128, D // 128
EPS = 1e-5
SCL = float(1.0 / np.sqrt(DH))

NF4 = [-1.0, -0.6961928009986877, -0.5250730514526367, -0.39491748809814453,
       -0.28444138169288635, -0.18477343022823334, -0.09105003625154495, 0.0,
       0.07958029955625534, 0.16093020141124725, 0.24611230194568634,
       0.33791524171829224, 0.44070982933044434, 0.5626170039176941,
       0.7229568362236023, 1.0]

DFD = [512, 512, 512, 256]
DOF = [0, 512, 1024, 1536]
DQ_TILES = []
for _H in range(HT):
    for _s in range(4):
        DQ_TILES.append(("d", _H, _s))
for _k in range(MT):
    for _s in range(8):
        DQ_TILES.append(("g", _k, _s))
    for _s in range(8):
        DQ_TILES.append(("u", _k, _s))
NDQ = len(DQ_TILES)
N_DOWN = 128


def dq_fd(t):
    return DFD[t[2]] if t[0] == "d" else 512


DQ_CUMP = [0]
for _t in DQ_TILES:
    DQ_CUMP.append(DQ_CUMP[-1] + dq_fd(_t) // 128)

NB_CODE = 3

ATT_P = [[(kb, max(0, kb - 4 * q) * 128) for kb in range(4 * q + 4)]
         for q in range(4)]
NP_QS = [len(p) for p in ATT_P]
NP_U = sum(NP_QS)

ATT_DVE_POS = [min(NDQ - 1, 120 + 14 * n) for n in range(16)]
C0_SILU_POS = [min(NDQ - 1, max(ATT_DVE_POS[-1] + 1 + 2 * k,
                                N_DOWN + (k + 1) * 16)) for k in range(MT)]

DEBUG = False
SKIP_ACC = False
ACC_BYPASS = False
PB_LOADS = 16
WST_PER_CH = 1 + 2 * MT + HT


def build_nc():
    nc = bass.Bass("TRN2", target_bir_lowering=False)

    x_rsT = nc.dram_tensor("x_rsT", [NCH * D, 128], F32, kind="ExternalInput")
    x_own = nc.dram_tensor("x_own", [TPC, D], F32, kind="ExternalInput")
    wq_s = nc.dram_tensor("wq_s", [D, HPC * DH], F32, kind="ExternalInput")
    wo_s = nc.dram_tensor("wo_s", [HPC * DH, D], F32, kind="ExternalInput")
    wkd = nc.dram_tensor("wkd", [D, DH], F32, kind="ExternalInput")
    wvd = nc.dram_tensor("wvd", [D, DH], F32, kind="ExternalInput")
    gqd = nc.dram_tensor("gqd", [MSL, D], I32, kind="ExternalInput")
    uqd = nc.dram_tensor("uqd", [MSL, D], I32, kind="ExternalInput")
    dqd = nc.dram_tensor("dqd", [D, MSL], I32, kind="ExternalInput")
    gamd = nc.dram_tensor("gamd", [MSL, 64], F32, kind="ExternalInput")
    uamd = nc.dram_tensor("uamd", [MSL, 64], F32, kind="ExternalInput")
    damd = nc.dram_tensor("damd", [D, 28], F32, kind="ExternalInput")
    idn = nc.dram_tensor("idn", [128, 128], F32, kind="ExternalInput")
    idnb = nc.dram_tensor("idnb", [128, 128], BF16, kind="ExternalInput")
    cmkb = nc.dram_tensor("cmkb", [128, 128], BF16, kind="ExternalInput")

    out_sl = nc.dram_tensor("out_sl", [NCH * D, 128], F32, kind="ExternalOutput")
    if DEBUG:
        dbg_xnT = nc.dram_tensor("dbg_xnT", [128, HT * TPC], BF16, kind="ExternalOutput")
        dbg_kT = nc.dram_tensor("dbg_kT", [128, TPC], BF16, kind="ExternalOutput")
        dbg_v = nc.dram_tensor("dbg_v", [128, 8 * 128], BF16, kind="ExternalOutput")
        dbg_g = nc.dram_tensor("dbg_g", [128, D], BF16, kind="ExternalOutput")
        dbg_u = nc.dram_tensor("dbg_u", [128, D], BF16, kind="ExternalOutput")
        dbg_d = nc.dram_tensor("dbg_d", [128, MSL], BF16, kind="ExternalOutput")
        dbg_rs = nc.dram_tensor("dbg_rs", [D, 128], F32, kind="ExternalOutput")
        dbg_ro = nc.dram_tensor("dbg_ro", [D, 128], F32, kind="ExternalOutput")
        dbg_rs3 = nc.dram_tensor("dbg_rs3", [D, 128], F32, kind="ExternalOutput")
        dbg_ro3 = nc.dram_tensor("dbg_ro3", [D, 128], F32, kind="ExternalOutput")
        dbg_qT = nc.dram_tensor("dbg_qT", [128, 8192], BF16, kind="ExternalOutput")
        dbg_avb = nc.dram_tensor("dbg_avb", [128, 8192], BF16, kind="ExternalOutput")
        dbg_hT = nc.dram_tensor("dbg_hT", [128, MT * TPC], BF16, kind="ExternalOutput")
        dbg_exs = nc.dram_tensor("dbg_exs", [128, 8192], BF16, kind="ExternalOutput")

    xnT_st = nc.dram_tensor("xnT_st", [128, HT * TPC], BF16)
    xnT_all = nc.dram_tensor("xnT_all", [NC * 128, HT * TPC], BF16)
    kT_st = nc.dram_tensor("kT_st", [128, TPC], BF16)
    kT_all = nc.dram_tensor("kT_all", [NC * 128, TPC], BF16)
    v_st = nc.dram_tensor("v_st", [128, 8 * 128], BF16)
    v_all = nc.dram_tensor("v_all", [NC * 128, 8 * 128], BF16)
    g_st = nc.dram_tensor("g_st", [128, MT * D], BF16)
    u_st = nc.dram_tensor("u_st", [128, MT * D], BF16)
    d_st = nc.dram_tensor("d_st", [128, HT * MSL], BF16)
    rs_in = nc.dram_tensor("rs_in", [NCH * 8 * D, 128], F32)
    rs_jnk = nc.dram_tensor("rs_jnk", [8 * D, 128], F32)
    rs_out = nc.dram_tensor("rs_out", [NCH * D, 128], F32)

    st = ExitStack()

    def sem(name):
        return st.enter_context(nc.semaphore(name))

    s_wa = sem("s_wa")
    s_waf = sem("s_waf")
    s_xin = sem("s_xin")
    s_code = [sem(f"s_code{i}") for i in range(NB_CODE)]
    s_lnr = sem("s_lnr")
    s_sta = sem("s_sta")
    s_lnv = sem("s_lnv")
    s_sd = sem("s_sd")
    s_lnn = sem("s_lnn")
    s_xt = sem("s_xt")
    s_evx = sem("s_evx")
    s_kv = sem("s_kv")
    s_kve = sem("s_kve")
    s_stg = sem("s_stg")
    s_ag = sem("s_ag")
    s_dqw = sem("s_dqw")
    s_dqt = sem("s_dqt")
    s_dqs = sem("s_dqs")
    s_p1l = sem("s_p1l")
    s_qpl = sem("s_qpl")
    s_xchl = sem("s_xchl")
    s_kvl = sem("s_kvl")
    s_wol = sem("s_wol")
    s_wxc = sem("s_wxc")
    s_wg2 = [sem("s_wg0"), sem("s_wg1")]
    s_wu2 = [sem("s_wu0"), sem("s_wu1")]
    s_wd2 = [sem("s_wd0"), sem("s_wd1")]
    s_wodp = [sem("s_wodp0"), sem("s_wodp1")]
    s_dnap = [sem(f"s_dnap{i}") for i in range(4)]
    s_qc = sem("s_qc")
    s_oc = sem("s_oc")
    s_xncf = sem("s_xncf")
    s_qev = sem("s_qev")
    s_pe1 = sem("s_pe1")
    s_exp = sem("s_exp")
    s_av1 = sem("s_av1")
    s_dnme = sem("s_dnme")
    s_dva = sem("s_dva")
    s_pbc = sem("s_pbc")
    s_abe = sem("s_abe")
    s_pva = sem("s_pva")
    s_wvd = sem("s_wvd")
    s_woe = sem("s_woe")
    s_wod = sem("s_wod")
    s_wst = sem("s_wst")
    s_gu = sem("s_gu")
    s_sig = sem("s_sig")
    s_hT = sem("s_hT")
    s_dn = sem("s_dn")
    s_dne = sem("s_dne")
    s_dnacc = sem("s_dnacc")
    s_rs = sem("s_rs")
    s_fin = sem("s_fin")
    s_fad = sem("s_fad")
    s_out = sem("s_out")

    identf = st.enter_context(nc.sbuf_tensor("identf", [128, 128], F32))
    identb = st.enter_context(nc.sbuf_tensor("identb", [128, 128], BF16))
    cmaskb = st.enter_context(nc.sbuf_tensor("cmaskb", [128, 128], BF16))
    onesr = st.enter_context(nc.sbuf_tensor("onesr", [1, 128], BF16))
    onesc = st.enter_context(nc.sbuf_tensor("onesc", [128, 1], BF16))
    gamx = st.enter_context(nc.sbuf_tensor("gamx", [128, MT * 64], F32))
    uamx = st.enter_context(nc.sbuf_tensor("uamx", [128, MT * 64], F32))
    damx = st.enter_context(nc.sbuf_tensor("damx", [128, HT * 28], F32))
    cq = st.enter_context(nc.sbuf_tensor("cq", [128, NB_CODE * 512], I32))
    qu16 = st.enter_context(nc.sbuf_tensor("qu16", [128, 512], U16))
    mby = [st.enter_context(nc.sbuf_tensor(f"mb{i}", [128, 512], U16))
           for i in range(4)]
    ut = st.enter_context(nc.sbuf_tensor("ut", [128, 8 * 512], FP16))
    wdt = st.enter_context(nc.sbuf_tensor("wdt", [128, 2 * 512], BF16))
    trs = st.enter_context(nc.sbuf_tensor("trs", [128, 2 * 512], BF16))
    lnS = st.enter_context(nc.sbuf_tensor("lnS", [128, 8], F32))
    lnQ = st.enter_context(nc.sbuf_tensor("lnQ", [128, 8], F32))
    lnNM = st.enter_context(nc.sbuf_tensor("lnNM", [128, 8], F32))
    lnM2 = st.enter_context(nc.sbuf_tensor("lnM2", [128, 8], F32))
    lnT2 = st.enter_context(nc.sbuf_tensor("lnT2", [128, 8], F32))
    lnV = st.enter_context(nc.sbuf_tensor("lnV", [128, 8], F32))
    lnSD = st.enter_context(nc.sbuf_tensor("lnSD", [128, 8], F32))
    lnR = st.enter_context(nc.sbuf_tensor("lnR", [128, 8], F32))
    dnm = st.enter_context(nc.sbuf_tensor("dnm", [1, 512], F32))
    rcp = st.enter_context(nc.sbuf_tensor("rcp", [1, 512], F32))
    rcpb = st.enter_context(nc.sbuf_tensor("rcpb", [1, 512], BF16))

    AR = 152 * 1024
    arena = st.enter_context(nc.sbuf_tensor("arena", [128, AR], U8))

    def cv(off, cols, dt):
        esz = {F32: 4, BF16: 2, FP16: 2, I32: 4, U16: 2}[dt]
        return arena[:, off:off + cols * esz].bitcast(dt)

    oP_xt = 0
    oP_sq = 16384
    oP_xnf = 32768
    oP_xtt = 65536
    oP_wstg = 81920
    oP_kts = 98304
    oP_vts = 100352
    oP_wkb = 102400
    oP_wvb = 110592
    oQ_wqb = 0
    oQ_woev = 65536
    oQ_xch = 32768
    oQ_wob = 32768
    oQ_qT = 65536
    oQ_exs = 81920
    oQ_avb = 98304
    oQ_ktb = 114688
    oQ_vtb = 118784
    oQ_ae = 122880
    oQ_be = 123904
    oQ_stg = 124928
    oF_xch = 0
    oF_wg = 65536
    oF_wu = 81920
    oF_wd = 98304
    oF_hT = 105472
    oF_dnev = 134144
    oF_sg = 150528
    oF_fr = 134144
    oF_fx = 138240
    assert oF_sg + 4096 <= AR

    pb = [st.enter_context(nc.psum_tensor(f"pb{i}", [128, 512], F32))
          for i in range(8)]

    blk = st.enter_context(nc.Block())

    def code_src(t):
        kind = t[0]
        if kind == "d":
            _, H, s = t
            return (bass.AP(dqd, H * 128 * MSL + DOF[s], [[MSL, 128], [1, DFD[s]]]),
                    DFD[s])
        _, k, s = t
        ten = gqd if kind == "g" else uqd
        return bass.AP(ten, k * 128 * D + s * 512, [[D, 128], [1, 512]]), 512

    @blk.sync
    def _(sync):
        sync.dma_start(out=identf[:, :], in_=idn[:, :]).then_inc(s_wa, 16)
        sync.dma_start(out=identb[:, :], in_=idnb[:, :]).then_inc(s_wa, 16)
        sync.dma_start(out=cmaskb[:, :], in_=cmkb[:, :]).then_inc(s_wa, 16)
        sync.wait_ge(s_wa, 48)
        sync.dma_start(
            out=cv(oP_wstg, D, F32),
            in_=bass.AP(wkd, 0, [[DH, 128], [128 * DH, HT], [1, DH]]),
        ).then_inc(s_wa, 16)
        sync.wait_ge(s_waf, 1)
        sync.dma_start(
            out=cv(oP_wstg, D, F32),
            in_=bass.AP(wvd, 0, [[DH, 128], [128 * DH, HT], [1, DH]]),
        ).then_inc(s_wa, 16)
        sync.wait_ge(s_wa, 80)
        sync.dma_start(out=gamx[:, :],
                       in_=bass.AP(gamd, 0, [[64, 128], [128 * 64, MT], [1, 64]])
                       ).then_inc(s_wa, 16)
        sync.dma_start(out=uamx[:, :],
                       in_=bass.AP(uamd, 0, [[64, 128], [128 * 64, MT], [1, 64]])
                       ).then_inc(s_wa, 16)
        sync.dma_start(out=damx[:, :],
                       in_=bass.AP(damd, 0, [[28, 128], [128 * 28, HT], [1, 28]])
                       ).then_inc(s_wa, 16)
        for tt in range(8):
            if tt >= 1:
                sync.wait_ge(s_lnn, tt)
            sync.dma_start(out=cv(oP_xt, D, F32),
                           in_=x_own[tt * 128:(tt + 1) * 128, :]).then_inc(s_xin, 16)
        for i in range(NB_CODE):
            src, w = code_src(DQ_TILES[i])
            sync.dma_start(out=cq[:, (i % NB_CODE) * 512:(i % NB_CODE) * 512 + w],
                           in_=src).then_inc(s_code[i % NB_CODE], 16)
        for i, til in enumerate(DQ_TILES):
            fd = dq_fd(til)
            ob = (i % 2) * 512
            sync.wait_ge(s_dqw, i + 1)
            if i >= 2:
                sync.wait_ge(s_dqs, 16 * (i - 1))
            for j in range(fd // 128):
                sync.dma_start_transpose(
                    trs[:, ob + j * 128: ob + (j + 1) * 128],
                    wdt[:, ob + j * 128: ob + (j + 1) * 128],
                ).then_inc(s_dqt, 16)
            sync.wait_ge(s_dqt, 16 * DQ_CUMP[i + 1])
            kind = til[0]
            if kind == "d":
                _, H, s = til
                dst = d_st[:, H * MSL + DOF[s]: H * MSL + DOF[s] + fd]
            elif kind == "g":
                _, k, s = til
                dst = g_st[:, k * D + s * 512: k * D + (s + 1) * 512]
            else:
                _, k, s = til
                dst = u_st[:, k * D + s * 512: k * D + (s + 1) * 512]
            sync.dma_start(out=dst, in_=trs[:, ob:ob + fd]).then_inc(s_dqs, 16)
            nx = i + NB_CODE
            if nx < NDQ:
                src, w = code_src(DQ_TILES[nx])
                sync.dma_start(out=cq[:, (nx % NB_CODE) * 512:(nx % NB_CODE) * 512 + w],
                               in_=src).then_inc(s_code[nx % NB_CODE], 16)

    @blk.sync
    def _(sync):
        for c in range(2, NCH):
            sync.wait_ge(s_gu, c * MT)
            sync.dma_start(out=cv(oF_xch, HT * TPC, BF16),
                           in_=xnT_all[c * 128:(c + 1) * 128, :]).then_inc(s_wxc, 16)
            for k in range(MT):
                sync.wait_ge(s_gu, c * MT + k - 1)
                sync.dma_start(out=cv(oF_wg + (k % 2) * 8192, D, BF16),
                               in_=g_st[:, k * D:(k + 1) * D]
                               ).then_inc(s_wg2[k % 2], 16)
                sync.dma_start(out=cv(oF_wu + (k % 2) * 8192, D, BF16),
                               in_=u_st[:, k * D:(k + 1) * D]
                               ).then_inc(s_wu2[k % 2], 16)
            for ho in range(HT):
                sync.wait_ge(s_dn, c * HT + ho - 1)
                sync.dma_start(out=cv(oF_wd + (ho % 2) * 3584, MSL, BF16),
                               in_=d_st[:, ho * MSL:(ho + 1) * MSL]
                               ).then_inc(s_wd2[ho % 2], 16)

    @blk.scalar
    def _(scalar):
        scalar.wait_ge(s_wa, 64)
        scalar.activation(cv(oP_wkb, D, BF16), cv(oP_wstg, D, F32),
                          AF.Copy).then_inc(s_waf, 1)
        scalar.wait_ge(s_wa, 80)
        scalar.activation(cv(oP_wvb, D, BF16), cv(oP_wstg, D, F32),
                          AF.Copy).then_inc(s_waf, 1)
        for tt in range(8):
            scalar.wait_ge(s_xin, (tt + 1) * 16)
            scalar.activation(cv(oP_sq, D, F32), cv(oP_xt, D, F32), AF.Square,
                              accum_out=lnQ[:, tt:tt + 1])
            scalar.wait_ge(s_lnr, tt + 1)
            scalar.activation(lnNM[:, tt:tt + 1], lnS[:, tt:tt + 1], AF.Copy,
                              scale=-1.0 / D)
            scalar.activation(lnM2[:, tt:tt + 1], lnNM[:, tt:tt + 1], AF.Square)
            scalar.activation(lnT2[:, tt:tt + 1], lnQ[:, tt:tt + 1], AF.Copy,
                              scale=1.0 / D, bias=EPS).then_inc(s_sta, 1)
            scalar.wait_ge(s_lnv, tt + 1)
            scalar.sqrt(lnSD[:, tt:tt + 1], lnV[:, tt:tt + 1]).then_inc(s_sd, 1)
            xb = oP_xtt + (tt % 2) * 8192
            if tt >= 2:
                scalar.wait_ge(s_kv, 2 * (tt - 1))
                scalar.wait_ge(s_stg, 16 * 4 * (tt - 1))
            for g in range(8):
                scalar.wait_ge(s_xt, tt * 8 + g + 1)
                scalar.activation(cv(xb + g * 1024, 512, BF16),
                                  pb[4 + g % 2][:, :],
                                  AF.Copy).then_inc(s_evx, 1)
            scalar.wait_ge(s_kv, 2 * tt + 1)
            scalar.activation(cv(oP_kts + tt * 256, 128, BF16), pb[0][:, 0:128], AF.Copy)
            scalar.wait_ge(s_kv, 2 * tt + 2)
            scalar.activation(cv(oP_vts + tt * 256, 128, BF16), pb[1][:, 0:128],
                              AF.Copy).then_inc(s_kve, 1)

        g_exp = 0
        for b in range(4):
            for g in range(4):
                scalar.wait_ge(s_qpl, (b * 4 + g + 1) * 16)
                scalar.activation(cv(oQ_wqb + g * 8192, 4096, BF16),
                                  cv(oQ_stg, 4096, F32), AF.Copy).then_inc(s_qc, 1)
            for cs in range(4):
                scalar.wait_ge(s_xncf, b * 4 + cs + 1)
                if cs == 0:
                    scalar.wait_ge(s_wodp[0], b * 16 * 16)
                    scalar.wait_ge(s_wodp[1], b * 16 * 16)
                for dp in range(4):
                    ins = scalar.activation(
                        cv(oQ_qT + dp * 4096 + (cs // 2) * 2048 + (cs % 2) * 1024,
                           512, BF16),
                        pb[dp][:, :], AF.Copy)
                ins.then_inc(s_qev, 1)
            scalar.wait_ge(s_xncf, b * 4 + 4)
            for g in range(4):
                scalar.wait_ge(s_wol, (b * 4 + g + 1) * 16)
                scalar.activation(cv(oQ_wob + g * 8192, 4096, BF16),
                                  cv(oQ_stg, 4096, F32), AF.Copy).then_inc(s_oc, 1)
            for h in range(HPC):
                for qs in range(4):
                    uq_i = (b * 4 + h) * 4 + qs
                    if uq_i >= 1:
                        scalar.wait_ge(s_av1, uq_i)
                    for pi, (kb, lo) in enumerate(ATT_P[qs]):
                        g_exp += 1
                        scalar.wait_ge(s_pe1, g_exp)
                        slot = pi % 4
                        scalar.activation(
                            cv(oQ_exs + kb * 1024 + lo * 2, 512 - lo, BF16),
                            pb[4 + slot][:, lo:512],
                            AF.Exp, scale=SCL).then_inc(s_exp, 1)
                    scalar.wait_ge(s_av1, uq_i + 1)
                    if uq_i >= 1:
                        scalar.wait_ge(s_pbc, uq_i)
                    scalar.activation(dnm[0:1, :], pb[1][0:1, :], AF.Ln)
                    scalar.activation(rcpb[0:1, :], dnm[0:1, :], AF.Exp,
                                      scale=-1.0).then_inc(s_dva, 1)
                    scalar.wait_ge(s_pbc, uq_i + 1)
                    if uq_i >= 1:
                        scalar.wait_ge(s_pva, uq_i)
                    scalar.activation(cv(oQ_ae, 512, BF16), pb[0][:, :], AF.Copy)
                    scalar.activation(cv(oQ_be, 512, BF16), pb[2][:, :],
                                      AF.Copy).then_inc(s_abe, 1)
            for ct in range(32):
                scalar.wait_ge(s_wvd, b * 32 + ct + 1)
                if b * 32 + ct >= 2:
                    scalar.wait_ge(s_wodp[ct % 2], (b * 16 + ct // 2) * 16)
                for ts4 in range(4):
                    ins = scalar.activation(
                        cv(oQ_woev + (ct % 2) * 8192 + ts4 * 2048, 512, F32),
                        pb[(ct % 2) * 4 + ts4][:, :], AF.Copy)
                ins.then_inc(s_woe, 1)

        for c in range(NCH):
            for k in range(MT):
                scalar.wait_ge(s_gu, c * MT + k + 1)
                if c * MT + k >= 2:
                    scalar.wait_ge(s_hT, c * MT + k - 1)
                for half in range(2):
                    ins = scalar.activation(
                        cv(oF_sg + (k % 2) * 2048 + half * 1024, 512, BF16),
                        pb[(k % 2) * 4 + half][:, :], AF.Sigmoid)
                ins.then_inc(s_sig, 1)
            for ho in range(HT):
                scalar.wait_ge(s_dn, c * HT + ho + 1)
                if c * HT + ho >= 4:
                    scalar.wait_ge(s_dnap[ho % 4], (c * 8 + ho // 4) * 16)
                for half in range(2):
                    ins = scalar.activation(
                        cv(oF_dnev + (ho % 4) * 4096 + half * 2048, 512, F32),
                        pb[2 * (ho % 4) + half][:, :], AF.Copy)
                ins.then_inc(s_dne, 1)

    @blk.vector
    def _(vector):
        vector.memset(onesr[0:1, :], 1.0)
        vector.memset(onesc[:, :], 1.0)
        for tt in range(8):
            vector.wait_ge(s_xin, (tt + 1) * 16)
            vector.tensor_reduce(lnS[:, tt:tt + 1], cv(oP_xt, D, F32), AX,
                                 ALU.add).then_inc(s_lnr, 1)
            vector.wait_ge(s_sta, tt + 1)
            vector.tensor_tensor(lnV[:, tt:tt + 1], lnT2[:, tt:tt + 1],
                                 lnM2[:, tt:tt + 1], ALU.subtract).then_inc(s_lnv, 1)
            vector.wait_ge(s_sd, tt + 1)
            vector.reciprocal(lnR[:, tt:tt + 1], lnSD[:, tt:tt + 1])
            if tt >= 2:
                vector.wait_ge(s_xt, (tt - 1) * 8)
            vector.tensor_scalar(cv(oP_xnf + (tt % 2) * 16384, D, F32),
                                 cv(oP_xt, D, F32), lnNM[:, tt:tt + 1],
                                 lnR[:, tt:tt + 1], ALU.add, ALU.mult
                                 ).then_inc(s_lnn, 1)

        def emit_silu(c, k):
            sgb = oF_sg + (k % 2) * 2048
            vector.wait_ge(s_sig, c * MT + k + 1)
            for half in range(2):
                vector.tensor_tensor(cv(sgb + half * 1024, 512, BF16),
                                     cv(sgb + half * 1024, 512, BF16),
                                     pb[(k % 2) * 4 + half][:, :], ALU.mult)
            for half in range(2):
                ins = vector.tensor_tensor(
                    cv(oF_hT + k * 2048 + half * 1024, 512, BF16),
                    cv(sgb + half * 1024, 512, BF16),
                    pb[(k % 2) * 4 + 2 + half][:, :], ALU.mult)
            ins.then_inc(s_hT, 1)

        vector.wait_ge(s_wa, 128)
        for i, til in enumerate(DQ_TILES):
            fd = dq_fd(til)
            slot = (i % NB_CODE) * 512
            ob = (i % 2) * 512
            vector.wait_ge(s_code[i % NB_CODE], (i // NB_CODE + 1) * 16)
            vector.tensor_copy(qu16[:, 0:fd], cq[:, slot:slot + fd])
            vector.tensor_scalar(mby[0][:, 0:fd], qu16[:, 0:fd], 1, None,
                                 ALU.bitwise_and)
            vector.tensor_scalar(mby[1][:, 0:fd], qu16[:, 0:fd], 2, None,
                                 ALU.bitwise_and)
            vector.tensor_scalar(mby[2][:, 0:fd], qu16[:, 0:fd], 4, None,
                                 ALU.bitwise_and)
            vector.tensor_scalar(mby[3][:, 0:fd], qu16[:, 0:fd], 8, None,
                                 ALU.bitwise_and)
            for j in range(8):
                c0, c1 = NF4[2 * j], NF4[2 * j + 1]
                vector.tensor_scalar(ut[:, j * 512:j * 512 + fd], mby[0][:, 0:fd],
                                     c1 - c0, c0, ALU.mult, ALU.add)
            for j in range(4):
                vector.copy_predicated(ut[:, (2 * j) * 512:(2 * j) * 512 + fd],
                                       mby[1][:, 0:fd],
                                       ut[:, (2 * j + 1) * 512:(2 * j + 1) * 512 + fd])
            vector.copy_predicated(ut[:, 0:fd], mby[2][:, 0:fd],
                                   ut[:, 1024:1024 + fd])
            vector.copy_predicated(ut[:, 2048:2048 + fd], mby[2][:, 0:fd],
                                   ut[:, 3072:3072 + fd])
            vector.copy_predicated(ut[:, 0:fd], mby[3][:, 0:fd],
                                   ut[:, 2048:2048 + fd])
            if i >= 2:
                vector.wait_ge(s_dqt, 16 * DQ_CUMP[i - 1])
            kind = til[0]
            ins = None
            for bb in range(fd // 64):
                if kind == "d":
                    _, H, s = til
                    amc = damx[:, H * 28 + DOF[s] // 64 + bb:
                               H * 28 + DOF[s] // 64 + bb + 1]
                elif kind == "g":
                    _, k, s = til
                    amc = gamx[:, k * 64 + s * 8 + bb: k * 64 + s * 8 + bb + 1]
                else:
                    _, k, s = til
                    amc = uamx[:, k * 64 + s * 8 + bb: k * 64 + s * 8 + bb + 1]
                ins = vector.tensor_scalar(wdt[:, ob + bb * 64: ob + (bb + 1) * 64],
                                           ut[:, bb * 64:(bb + 1) * 64],
                                           amc, None, ALU.mult)
            ins.then_inc(s_dqw, 1)
        for k0 in range(MT):
            emit_silu(0, k0)

        def emit_fin(cc):
            for qtr in range(4):
                vector.wait_ge(s_fin, (cc * 4 + qtr + 1) * 32)
                vector.tensor_tensor(cv(oF_fr, 1024, F32), cv(oF_fr, 1024, F32),
                                     cv(oF_fx, 1024, F32),
                                     ALU.add).then_inc(s_fad, 1)

        for c in range(1, NCH):
            for k in range(MT):
                emit_silu(c, k)
                if k == 11:
                    emit_fin(c - 1)
        emit_fin(NCH - 1)

    @blk.tensor
    def _(tensor):
        tensor.wait_ge(s_wa, 48)
        for tt in range(8):
            xb = oP_xnf + (tt % 2) * 16384
            tensor.wait_ge(s_lnn, tt + 1)
            for g in range(8):
                if tt * 8 + g >= 2:
                    tensor.wait_ge(s_evx, tt * 8 + g - 1)
                for q4 in range(4):
                    hh = g * 4 + q4
                    ins = tensor.transpose(
                        pb[4 + g % 2][:, q4 * 128:(q4 + 1) * 128],
                        cv(xb + hh * 512, 128, F32), identf[:, :])
                ins.then_inc(s_xt, 1)
            tensor.wait_ge(s_waf, 2)
            tensor.wait_ge(s_evx, (tt + 1) * 8)
            if tt >= 1:
                tensor.wait_ge(s_kve, tt)
            xtb = oP_xtt + (tt % 2) * 8192
            for H in range(HT):
                ins = tensor.matmul(pb[0][:, 0:128], cv(oP_wkb + H * 256, 128, BF16),
                                    cv(xtb + H * 256, 128, BF16),
                                    start=(H == 0), stop=(H == HT - 1))
            ins.then_inc(s_kv, 1)
            for H in range(HT):
                ins = tensor.matmul(pb[1][:, 0:128], cv(xtb + H * 256, 128, BF16),
                                    cv(oP_wvb + H * 256, 128, BF16),
                                    start=(H == 0), stop=(H == HT - 1))
            ins.then_inc(s_kv, 1)

        tensor.wait_ge(s_lnn, 8)
        tensor.wait_ge(s_kve, 8)
        tensor.wait_ge(s_evx, 64)
        g_pe = 0
        for b in range(4):
            for cs in range(4):
                tensor.wait_ge(s_qc, b * 4 + 4)
                if cs == 0 and b >= 1:
                    tensor.wait_ge(s_woe, b * 32)
                tensor.wait_ge(s_xchl, (b * 4 + cs + 1) * 16)
                if b * 4 + cs >= 1:
                    tensor.wait_ge(s_qev, b * 4 + cs)
                for dp in range(4):
                    for H in range(HT):
                        ins = tensor.matmul(pb[dp][:, :],
                                            cv(oQ_wqb + H * 1024 + dp * 256, 128, BF16),
                                            cv(oQ_xch + H * 1024, 512, BF16),
                                            start=(H == 0), stop=(H == HT - 1))
                tensor.nop()
                ins.then_inc(s_xncf, 1)
            tensor.wait_ge(s_kvl, (b + 1) * 64)
            for h in range(HPC):
                for qs in range(4):
                    uq_i = (b * 4 + h) * 4 + qs
                    tensor.wait_ge(s_qev, b * 4 + 4)
                    for pi, (kb, lo) in enumerate(ATT_P[qs]):
                        slot = pi % 4
                        g_pe += 1
                        if g_pe >= 5:
                            tensor.wait_ge(s_exp, g_pe - 4)
                        diag = kb >= 4 * qs
                        ins = tensor.matmul(pb[4 + slot][:, lo:512],
                                            cv(oQ_ktb + kb * 256, 128, BF16),
                                            cv(oQ_qT + h * 4096 + qs * 1024 + lo * 2,
                                               512 - lo, BF16),
                                            start=True, stop=not diag)
                        if diag:
                            ins = tensor.matmul(
                                pb[4 + slot][:, lo:lo + 128],
                                identb[:, :], cmaskb[:, :], start=False, stop=True)
                        ins.then_inc(s_pe1, 1)
                    tensor.wait_ge(s_exp, g_pe)
                    if uq_i >= 1:
                        tensor.wait_ge(s_abe, uq_i)
                        tensor.wait_ge(s_dva, uq_i)
                    np_ = len(ATT_P[qs])
                    for pi, (kb, lo) in enumerate(ATT_P[qs]):
                        tensor.matmul(pb[0][:, lo:512],
                                      cv(oQ_vtb + kb * 256, 128, BF16),
                                      cv(oQ_exs + kb * 1024 + lo * 2, 512 - lo, BF16),
                                      start=(pi == 0), stop=(pi == np_ - 1))
                    for pi, (kb, lo) in enumerate(ATT_P[qs]):
                        ins = tensor.matmul(pb[1][0:1, lo:512],
                                            onesc[:, :],
                                            cv(oQ_exs + kb * 1024 + lo * 2,
                                               512 - lo, BF16),
                                            start=(pi == 0), stop=(pi == np_ - 1))
                    ins.then_inc(s_av1, 1)
                    tensor.wait_ge(s_dva, uq_i + 1)
                    tensor.matmul(pb[2][:, :], onesr[0:1, :], rcpb[0:1, :],
                                  start=True, stop=True).then_inc(s_pbc, 1)
            tensor.wait_ge(s_pva, (b + 1) * 16)
            tensor.wait_ge(s_oc, (b + 1) * 4)
            for ct in range(32):
                if b * 32 + ct >= 2:
                    tensor.wait_ge(s_woe, b * 32 + ct - 1)
                for dp in range(4):
                    for ts4 in range(4):
                        ins = tensor.matmul(pb[(ct % 2) * 4 + ts4][:, :],
                                            cv(oQ_wob + dp * 8192 + ct * 256, 128, BF16),
                                            cv(oQ_avb + dp * 4096 + ts4 * 1024,
                                               512, BF16),
                                            start=(dp == 0), stop=(dp == 3))
                ins.then_inc(s_wvd, 1)

        tensor.wait_ge(s_woe, 128)
        for c in range(NCH):
            tensor.wait_ge(s_wxc, (c + 1) * 16)
            if c >= 1:
                tensor.wait_ge(s_dne, c * HT)
            for k in range(MT):
                nslot = c * (MT // 2) + k // 2 + 1
                tensor.wait_ge(s_wg2[k % 2], nslot * 16)
                tensor.wait_ge(s_wu2[k % 2], nslot * 16)
                if c * MT + k >= 2:
                    tensor.wait_ge(s_hT, c * MT + k - 1)
                for half in range(2):
                    for H in range(HT):
                        tensor.matmul(pb[(k % 2) * 4 + half][:, :],
                                      cv(oF_wg + (k % 2) * 8192 + H * 256, 128, BF16),
                                      cv(oF_xch + H * 2048 + half * 1024, 512, BF16),
                                      start=(H == 0), stop=(H == HT - 1))
                for half in range(2):
                    for H in range(HT):
                        ins = tensor.matmul(
                            pb[(k % 2) * 4 + 2 + half][:, :],
                            cv(oF_wu + (k % 2) * 8192 + H * 256, 128, BF16),
                            cv(oF_xch + H * 2048 + half * 1024, 512, BF16),
                            start=(H == 0), stop=(H == HT - 1))
                ins.then_inc(s_gu, 1)
            tensor.wait_ge(s_hT, (c + 1) * MT)
            for ho in range(HT):
                tensor.wait_ge(s_wd2[ho % 2], (c * (HT // 2) + ho // 2 + 1) * 16)
                if c * HT + ho >= 4:
                    tensor.wait_ge(s_dne, c * HT + ho - 3)
                for k in range(MT):
                    for half in range(2):
                        ins = tensor.matmul(
                            pb[2 * (ho % 4) + half][:, :],
                            cv(oF_wd + (ho % 2) * 3584 + k * 256, 128, BF16),
                            cv(oF_hT + k * 2048 + half * 1024, 512, BF16),
                            start=(k == 0), stop=(k == MT - 1))
                ins.then_inc(s_dn, 1)

    @blk.gpsimd
    def _(gpsimd):
        for tt in range(8):
            gpsimd.wait_ge(s_evx, (tt + 1) * 8)
            xtb = oP_xtt + (tt % 2) * 8192
            for g4 in range(4):
                gpsimd.dma_start(
                    out=bass.AP(xnT_st, g4 * 8 * TPC + tt * 128,
                                [[HT * TPC, 128], [TPC, 8], [1, 128]]),
                    in_=cv(xtb + g4 * 2048, 1024, BF16),
                ).then_inc(s_stg, 16)
        gpsimd.wait_ge(s_kve, 8)
        gpsimd.dma_start(out=kT_st[:, :], in_=cv(oP_kts, TPC, BF16)).then_inc(s_stg, 16)
        gpsimd.dma_start(out=v_st[:, :], in_=cv(oP_vts, TPC, BF16)).then_inc(s_stg, 16)
        gpsimd.wait_ge(s_stg, 16 * 34)
        gpsimd.collective_compute(
            "AllGather", ALU.bypass, replica_groups=[list(range(NC))],
            ins=[kT_st[:, :]], outs=[kT_all[:, :]]).then_inc(s_ag, 1)
        gpsimd.collective_compute(
            "AllGather", ALU.bypass, replica_groups=[list(range(NC))],
            ins=[v_st[:, :]], outs=[v_all[:, :]]).then_inc(s_ag, 1)
        gpsimd.collective_compute(
            "AllGather", ALU.bypass, replica_groups=[list(range(NC))],
            ins=[xnT_st[:, :]], outs=[xnT_all[:, :]]).then_inc(s_ag, 1)
        gpsimd.wait_ge(s_ag, 3)

        for b in range(4):
            for g in range(4):
                gpsimd.wait_ge(s_qc, b * 4 + g)
                if g == 0:
                    gpsimd.wait_ge(s_oc, b * 4)
                gpsimd.dma_start(
                    out=cv(oQ_stg, 4096, F32),
                    in_=bass.AP(wq_s, g * 8 * 128 * 512,
                                [[512, 128], [128 * 512, 8], [1, 512]]),
                ).then_inc(s_qpl, 16)
            for cs in range(4):
                c = 2 * b + cs // 2
                s = cs % 2
                gpsimd.wait_ge(s_xncf, b * 4 + cs)
                if cs == 0:
                    gpsimd.wait_ge(s_wvd, b * 32)
                gpsimd.dma_start(
                    out=cv(oQ_xch, HT * 512, BF16),
                    in_=bass.AP(xnT_all, c * 128 * HT * TPC + s * 512,
                                [[HT * TPC, 128], [TPC, HT], [1, 512]]),
                ).then_inc(s_xchl, 16)
            gpsimd.wait_ge(s_av1, b * 16)
            for half in range(2):
                cc = 2 * b + half
                gpsimd.dma_start(
                    out=cv(oQ_ktb + half * 2048, TPC, BF16),
                    in_=kT_all[cc * 128:(cc + 1) * 128, :]).then_inc(s_kvl, 16)
            for half in range(2):
                cc = 2 * b + half
                gpsimd.dma_start(
                    out=cv(oQ_vtb + half * 2048, TPC, BF16),
                    in_=v_all[cc * 128:(cc + 1) * 128, :]).then_inc(s_kvl, 16)
            for g in range(4):
                gpsimd.wait_ge(s_oc, b * 4 + g)
                if g == 0:
                    gpsimd.wait_ge(s_qc, (b + 1) * 4)
                gpsimd.dma_start(out=cv(oQ_stg, 4096, F32),
                                 in_=wo_s[g * 128:(g + 1) * 128, :]
                                 ).then_inc(s_wol, 16)
            gpsimd.wait_ge(s_wvd, b * 32)
            for h in range(HPC):
                for qs in range(4):
                    uq_i = (b * 4 + h) * 4 + qs
                    gpsimd.wait_ge(s_abe, uq_i + 1)
                    gpsimd.tensor_tensor(
                        cv(oQ_avb + h * 4096 + qs * 1024, 512, BF16),
                        cv(oQ_ae, 512, BF16), cv(oQ_be, 512, BF16),
                        ALU.mult).then_inc(s_pva, 1)
            for ct in range(32):
                gpsimd.wait_ge(s_woe, b * 32 + ct + 1)
                gpsimd.dma_start(
                    out=bass.AP(rs_in, (2 * b * 8 * D + ct * 128) * 128,
                                [[128, 128], [D * 128, 16], [1, 128]]),
                    in_=cv(oQ_woev + (ct % 2) * 8192, 2048, F32),
                ).then_inc(s_wodp[ct % 2], 16)
            if DEBUG and b == 0:
                gpsimd.dma_start(out=dbg_qT[:, :],
                                 in_=cv(oQ_qT, 8192, BF16)).then_inc(s_stg, 16)
                gpsimd.dma_start(out=dbg_avb[:, :],
                                 in_=cv(oQ_avb, 8192, BF16)).then_inc(s_stg, 16)
                gpsimd.dma_start(out=dbg_exs[:, :],
                                 in_=cv(oQ_exs, 8192, BF16)).then_inc(s_stg, 16)
                gpsimd.wait_ge(s_stg, 16 * 37)

        def pool_fin(cc):
            gpsimd.wait_ge(s_rs, cc + 1)
            for qtr in range(4):
                if cc * 4 + qtr >= 1:
                    gpsimd.wait_ge(s_out, (cc * 4 + qtr) * 16)
                gpsimd.dma_start(
                    out=cv(oF_fr, 1024, F32),
                    in_=bass.AP(rs_out, (cc * D + qtr * 1024) * 128,
                                [[128, 128], [128 * 128, 8], [1, 128]]),
                ).then_inc(s_fin, 16)
                gpsimd.dma_start(
                    out=cv(oF_fx, 1024, F32),
                    in_=bass.AP(x_rsT, (cc * D + qtr * 1024) * 128,
                                [[128, 128], [128 * 128, 8], [1, 128]]),
                ).then_inc(s_fin, 16)
                gpsimd.wait_ge(s_fad, cc * 4 + qtr + 1)
                gpsimd.dma_start(
                    out=bass.AP(out_sl, (cc * D + qtr * 1024) * 128,
                                [[128, 128], [128 * 128, 8], [1, 128]]),
                    in_=cv(oF_fr, 1024, F32),
                ).then_inc(s_out, 16)

        gpsimd.wait_ge(s_wvd, 4 * 32)
        gpsimd.wait_ge(s_wodp[0], 4 * 16 * 16)
        gpsimd.wait_ge(s_wodp[1], 4 * 16 * 16)
        for c in range(NCH):
            if c < 2:
                gpsimd.wait_ge(s_gu, c * MT)
                gpsimd.dma_start(out=cv(oF_xch, HT * TPC, BF16),
                                 in_=xnT_all[c * 128:(c + 1) * 128, :]
                                 ).then_inc(s_wxc, 16)
                for k in range(MT):
                    gpsimd.wait_ge(s_dqs, (N_DOWN + (k + 1) * 16) * 16)
                    if c * MT + k >= 2:
                        gpsimd.wait_ge(s_gu, c * MT + k - 1)
                    gpsimd.dma_start(out=cv(oF_wg + (k % 2) * 8192, D, BF16),
                                     in_=g_st[:, k * D:(k + 1) * D]
                                     ).then_inc(s_wg2[k % 2], 16)
                    gpsimd.dma_start(out=cv(oF_wu + (k % 2) * 8192, D, BF16),
                                     in_=u_st[:, k * D:(k + 1) * D]
                                     ).then_inc(s_wu2[k % 2], 16)
            if c >= 1:
                pool_fin(c - 1)
            if DEBUG and c == 3:
                gpsimd.wait_ge(s_hT, 4 * MT)
                gpsimd.dma_start(out=dbg_hT[:, :],
                                 in_=cv(oF_hT, MT * TPC, BF16)).then_inc(s_stg, 16)
                gpsimd.wait_ge(s_stg, 16 * 35)
            for ho in range(HT):
                if c < 2:
                    gpsimd.wait_ge(s_dqs, N_DOWN * 16)
                    if c * HT + ho >= 2:
                        gpsimd.wait_ge(s_dn, c * HT + ho - 1)
                    gpsimd.dma_start(out=cv(oF_wd + (ho % 2) * 3584, MSL, BF16),
                                     in_=d_st[:, ho * MSL:(ho + 1) * MSL]
                                     ).then_inc(s_wd2[ho % 2], 16)
                gpsimd.wait_ge(s_dne, c * HT + ho + 1)
                acc_dst = (bass.AP(rs_jnk, (ho * 128) * 128,
                                   [[128, 128], [D * 128, 8], [1, 128]])
                           if SKIP_ACC else
                           bass.AP(rs_in, (c * 8 * D + ho * 128) * 128,
                                   [[128, 128], [D * 128, 8], [1, 128]]))
                gpsimd.dma_start(
                    out=acc_dst,
                    in_=cv(oF_dnev + (ho % 4) * 4096, 1024, F32),
                    accum_op=(ALU.bypass if ACC_BYPASS else ALU.add)
                    ).then_inc(s_dnap[ho % 4], 16)
            for sl in range(4):
                gpsimd.wait_ge(s_dnap[sl], (c + 1) * 8 * 16)
            gpsimd.collective_compute(
                "ReduceScatter", ALU.add, replica_groups=[list(range(NC))],
                ins=[rs_in[c * 8 * D:(c + 1) * 8 * D, :]],
                outs=[rs_out[c * D:(c + 1) * D, :]]).then_inc(s_rs, 1)
        pool_fin(NCH - 1)
        if DEBUG:
            gpsimd.dma_start(out=dbg_xnT[:, :], in_=xnT_st[:, :]).then_inc(s_stg, 16)
            gpsimd.dma_start(out=dbg_kT[:, :], in_=kT_st[:, :]).then_inc(s_stg, 16)
            gpsimd.dma_start(out=dbg_v[:, :], in_=v_st[:, :]).then_inc(s_stg, 16)
            gpsimd.dma_start(out=dbg_g[:, :], in_=g_st[:, 0:D]).then_inc(s_stg, 16)
            gpsimd.dma_start(out=dbg_u[:, :], in_=u_st[:, 0:D]).then_inc(s_stg, 16)
            gpsimd.dma_start(out=dbg_d[:, :], in_=d_st[:, 0:MSL]).then_inc(s_stg, 16)
            gpsimd.dma_start(out=dbg_rs[:, :], in_=rs_in[0:D, :]).then_inc(s_stg, 16)
            gpsimd.dma_start(out=dbg_ro[:, :], in_=rs_out[0:D, :]).then_inc(s_stg, 16)
            gpsimd.dma_start(out=dbg_rs3[:, :],
                             in_=rs_in[3 * 8 * D:3 * 8 * D + D, :]).then_inc(s_stg, 16)
            gpsimd.dma_start(out=dbg_ro3[:, :],
                             in_=rs_out[3 * D:4 * D, :]).then_inc(s_stg, 16)
            gpsimd.wait_ge(s_stg, 16 * 48 - 16)
        gpsimd.wait_ge(s_out, NCH * 4 * 16)

    st.close()
    return nc


_NC_CACHE = [None]


def kernel(**inputs):
    x = np.ascontiguousarray(np.asarray(inputs["L0_attn_x"], dtype=np.float32))
    wq = np.asarray(inputs["L0_attn_wq"], dtype=np.float32)
    wkf = np.ascontiguousarray(np.asarray(inputs["L0_attn_wk"], dtype=np.float32))
    wvf = np.ascontiguousarray(np.asarray(inputs["L0_attn_wv"], dtype=np.float32))
    wo = np.asarray(inputs["L0_attn_wo"], dtype=np.float32)
    gqf = np.asarray(inputs["L0_ffn_gate_q"], dtype=np.int32)
    gaf = np.asarray(inputs["L0_ffn_gate_absmax"], dtype=np.float32).reshape(M, 64)
    uqf = np.asarray(inputs["L0_ffn_up_q"], dtype=np.int32)
    uaf = np.asarray(inputs["L0_ffn_up_absmax"], dtype=np.float32).reshape(M, 64)
    dqf = np.asarray(inputs["L0_ffn_down_q"], dtype=np.int32)
    daf = np.asarray(inputs["L0_ffn_down_absmax"], dtype=np.float32).reshape(D, 224)

    idn = np.eye(128, dtype=np.float32)
    idnb_ = np.eye(128).astype(ml_dtypes.bfloat16)
    cm = np.where(np.arange(128)[:, None] > np.arange(128)[None, :],
                  np.float32(-30000.0), np.float32(0.0)).astype(ml_dtypes.bfloat16)

    in_maps = []
    for r in range(NC):
        b, half = r // 2, r % 2
        slabs = []
        for c in range(NCH):
            t0 = TPC * (c % 2) + 128 * r
            slabs.append(np.ascontiguousarray(x[c // 2, t0:t0 + 128, :].T))
        in_maps.append({
            "x_own": np.ascontiguousarray(x[b, half * TPC:(half + 1) * TPC, :]),
            "x_rsT": np.concatenate(slabs, axis=0),
            "wq_s": np.ascontiguousarray(wq[:, r * 512:(r + 1) * 512]),
            "wo_s": np.ascontiguousarray(wo[r * 512:(r + 1) * 512, :]),
            "wkd": wkf, "wvd": wvf,
            "gqd": np.ascontiguousarray(gqf[r * MSL:(r + 1) * MSL, :]),
            "uqd": np.ascontiguousarray(uqf[r * MSL:(r + 1) * MSL, :]),
            "dqd": np.ascontiguousarray(dqf[:, r * MSL:(r + 1) * MSL]),
            "gamd": np.ascontiguousarray(gaf[r * MSL:(r + 1) * MSL, :]),
            "uamd": np.ascontiguousarray(uaf[r * MSL:(r + 1) * MSL, :]),
            "damd": np.ascontiguousarray(daf[:, r * 28:(r + 1) * 28]),
            "idn": idn, "idnb": idnb_, "cmkb": cm,
        })

    if _NC_CACHE[0] is None:
        _NC_CACHE[0] = build_nc()
    res = run_bass_kernel_spmd(_NC_CACHE[0], in_maps, core_ids=list(range(NC)))
    _NC_CACHE.append(res)

    out = np.empty((B, T, D), dtype=np.float32)
    for r in range(NC):
        arr = np.asarray(res.results[r]["out_sl"]).reshape(NCH, D, 128)
        for c in range(NCH):
            t0 = TPC * (c % 2) + 128 * r
            out[c // 2, t0:t0 + 128, :] = arr[c].T
    return out


# version 08 (final)
# speedup vs baseline: 1.3896x; 1.0003x over previous
import numpy as np
import ml_dtypes
from contextlib import ExitStack

import concourse.bass as bass
import concourse.mybir as mybir
from concourse.bass_utils import run_bass_kernel_spmd

F32 = mybir.dt.float32
BF16 = mybir.dt.bfloat16
FP16 = mybir.dt.float16
I32 = mybir.dt.int32
U16 = mybir.dt.uint16
U8 = mybir.dt.uint8
AF = mybir.ActivationFunctionType
ALU = mybir.AluOpType
AX = mybir.AxisListType.X

B, T, D = 4, 2048, 4096
NC, TPC, NCH = 8, 1024, 8
HPC, DH = 4, 128
M = 14336
MSL, MT, HT = M // NC, (M // NC) // 128, D // 128
EPS = 1e-5
SCL = float(1.0 / np.sqrt(DH))

NF4 = [-1.0, -0.6961928009986877, -0.5250730514526367, -0.39491748809814453,
       -0.28444138169288635, -0.18477343022823334, -0.09105003625154495, 0.0,
       0.07958029955625534, 0.16093020141124725, 0.24611230194568634,
       0.33791524171829224, 0.44070982933044434, 0.5626170039176941,
       0.7229568362236023, 1.0]

DFD = [512, 512, 512, 256]
DOF = [0, 512, 1024, 1536]
DQ_TILES = []
for _H in range(HT):
    for _s in range(4):
        DQ_TILES.append(("d", _H, _s))
for _k in range(MT):
    for _s in range(8):
        DQ_TILES.append(("g", _k, _s))
    for _s in range(8):
        DQ_TILES.append(("u", _k, _s))
NDQ = len(DQ_TILES)
N_DOWN = 128


def dq_fd(t):
    return DFD[t[2]] if t[0] == "d" else 512


DQ_CUMP = [0]
for _t in DQ_TILES:
    DQ_CUMP.append(DQ_CUMP[-1] + dq_fd(_t) // 128)

NB_CODE = 3

ATT_P = [[(kb, max(0, kb - 4 * q) * 128) for kb in range(4 * q + 4)]
         for q in range(4)]
NP_QS = [len(p) for p in ATT_P]
NP_U = sum(NP_QS)

ATT_DVE_POS = [min(NDQ - 1, 120 + 14 * n) for n in range(16)]
C0_SILU_POS = [min(NDQ - 1, max(ATT_DVE_POS[-1] + 1 + 2 * k,
                                N_DOWN + (k + 1) * 16)) for k in range(MT)]

DEBUG = False
SKIP_ACC = False
ACC_BYPASS = False
PB_LOADS = 16
WST_PER_CH = 1 + 2 * MT + HT


def build_nc():
    nc = bass.Bass("TRN2", target_bir_lowering=False)

    x_rsT = nc.dram_tensor("x_rsT", [NCH * D, 128], F32, kind="ExternalInput")
    x_own = nc.dram_tensor("x_own", [TPC, D], F32, kind="ExternalInput")
    wq_s = nc.dram_tensor("wq_s", [D, HPC * DH], F32, kind="ExternalInput")
    wo_s = nc.dram_tensor("wo_s", [HPC * DH, D], F32, kind="ExternalInput")
    wkd = nc.dram_tensor("wkd", [D, DH], F32, kind="ExternalInput")
    wvd = nc.dram_tensor("wvd", [D, DH], F32, kind="ExternalInput")
    gqd = nc.dram_tensor("gqd", [MSL, D], I32, kind="ExternalInput")
    uqd = nc.dram_tensor("uqd", [MSL, D], I32, kind="ExternalInput")
    dqd = nc.dram_tensor("dqd", [D, MSL], I32, kind="ExternalInput")
    gamd = nc.dram_tensor("gamd", [MSL, 64], F32, kind="ExternalInput")
    uamd = nc.dram_tensor("uamd", [MSL, 64], F32, kind="ExternalInput")
    damd = nc.dram_tensor("damd", [D, 28], F32, kind="ExternalInput")
    idn = nc.dram_tensor("idn", [128, 128], F32, kind="ExternalInput")
    idnb = nc.dram_tensor("idnb", [128, 128], BF16, kind="ExternalInput")
    cmkb = nc.dram_tensor("cmkb", [128, 128], BF16, kind="ExternalInput")

    out_sl = nc.dram_tensor("out_sl", [NCH * D, 128], F32, kind="ExternalOutput")
    if DEBUG:
        dbg_xnT = nc.dram_tensor("dbg_xnT", [128, HT * TPC], BF16, kind="ExternalOutput")
        dbg_kT = nc.dram_tensor("dbg_kT", [128, TPC], BF16, kind="ExternalOutput")
        dbg_v = nc.dram_tensor("dbg_v", [128, 8 * 128], BF16, kind="ExternalOutput")
        dbg_g = nc.dram_tensor("dbg_g", [128, D], BF16, kind="ExternalOutput")
        dbg_u = nc.dram_tensor("dbg_u", [128, D], BF16, kind="ExternalOutput")
        dbg_d = nc.dram_tensor("dbg_d", [128, MSL], BF16, kind="ExternalOutput")
        dbg_rs = nc.dram_tensor("dbg_rs", [D, 128], F32, kind="ExternalOutput")
        dbg_ro = nc.dram_tensor("dbg_ro", [D, 128], F32, kind="ExternalOutput")
        dbg_rs3 = nc.dram_tensor("dbg_rs3", [D, 128], F32, kind="ExternalOutput")
        dbg_ro3 = nc.dram_tensor("dbg_ro3", [D, 128], F32, kind="ExternalOutput")
        dbg_qT = nc.dram_tensor("dbg_qT", [128, 8192], BF16, kind="ExternalOutput")
        dbg_avb = nc.dram_tensor("dbg_avb", [128, 8192], BF16, kind="ExternalOutput")
        dbg_hT = nc.dram_tensor("dbg_hT", [128, MT * TPC], BF16, kind="ExternalOutput")
        dbg_exs = nc.dram_tensor("dbg_exs", [128, 8192], BF16, kind="ExternalOutput")

    xnT_st = nc.dram_tensor("xnT_st", [128, HT * TPC], BF16)
    xnT_all = nc.dram_tensor("xnT_all", [NC * 128, HT * TPC], BF16)
    kT_st = nc.dram_tensor("kT_st", [128, TPC], BF16)
    kT_all = nc.dram_tensor("kT_all", [NC * 128, TPC], BF16)
    v_st = nc.dram_tensor("v_st", [128, 8 * 128], BF16)
    v_all = nc.dram_tensor("v_all", [NC * 128, 8 * 128], BF16)
    g_st = nc.dram_tensor("g_st", [128, MT * D], BF16)
    u_st = nc.dram_tensor("u_st", [128, MT * D], BF16)
    d_st = nc.dram_tensor("d_st", [128, HT * MSL], BF16)
    rs_in = nc.dram_tensor("rs_in", [NCH * 8 * D, 128], F32)
    rs_jnk = nc.dram_tensor("rs_jnk", [8 * D, 128], F32)
    rs_out = nc.dram_tensor("rs_out", [NCH * D, 128], F32)

    st = ExitStack()

    def sem(name):
        return st.enter_context(nc.semaphore(name))

    s_wa = sem("s_wa")
    s_waf = sem("s_waf")
    s_xin = sem("s_xin")
    s_code = [sem(f"s_code{i}") for i in range(NB_CODE)]
    s_lnr = sem("s_lnr")
    s_sta = sem("s_sta")
    s_lnv = sem("s_lnv")
    s_sd = sem("s_sd")
    s_lnn = sem("s_lnn")
    s_xt = sem("s_xt")
    s_evx = sem("s_evx")
    s_kv = sem("s_kv")
    s_kve = sem("s_kve")
    s_stg = sem("s_stg")
    s_ag = sem("s_ag")
    s_dqw = sem("s_dqw")
    s_dqt = sem("s_dqt")
    s_dqs = sem("s_dqs")
    s_p1l = sem("s_p1l")
    s_qpl = sem("s_qpl")
    s_xchl = sem("s_xchl")
    s_kvl = sem("s_kvl")
    s_wol = sem("s_wol")
    s_wxc = sem("s_wxc")
    s_wg2 = [sem("s_wg0"), sem("s_wg1")]
    s_wu2 = [sem("s_wu0"), sem("s_wu1")]
    s_wd2 = [sem("s_wd0"), sem("s_wd1")]
    s_wodp = [sem("s_wodp0"), sem("s_wodp1")]
    s_dnap = [sem(f"s_dnap{i}") for i in range(4)]
    s_qc = sem("s_qc")
    s_oc = sem("s_oc")
    s_xncf = sem("s_xncf")
    s_qev = sem("s_qev")
    s_pe1 = sem("s_pe1")
    s_exp = sem("s_exp")
    s_av1 = sem("s_av1")
    s_dnme = sem("s_dnme")
    s_dva = sem("s_dva")
    s_acp = sem("s_acp")
    s_pbc = sem("s_pbc")
    s_abe = sem("s_abe")
    s_pva = sem("s_pva")
    s_wvd = sem("s_wvd")
    s_woe = sem("s_woe")
    s_wod = sem("s_wod")
    s_wst = sem("s_wst")
    s_gu = sem("s_gu")
    s_sig = sem("s_sig")
    s_hT = sem("s_hT")
    s_dn = sem("s_dn")
    s_dne = sem("s_dne")
    s_dnacc = sem("s_dnacc")
    s_rs = sem("s_rs")
    s_fin = sem("s_fin")
    s_fad = sem("s_fad")
    s_out = sem("s_out")

    identf = st.enter_context(nc.sbuf_tensor("identf", [128, 128], F32))
    identb = st.enter_context(nc.sbuf_tensor("identb", [128, 128], BF16))
    cmaskb = st.enter_context(nc.sbuf_tensor("cmaskb", [128, 128], BF16))
    onesr = st.enter_context(nc.sbuf_tensor("onesr", [1, 128], BF16))
    onesc = st.enter_context(nc.sbuf_tensor("onesc", [128, 1], BF16))
    gamx = st.enter_context(nc.sbuf_tensor("gamx", [128, MT * 64], F32))
    uamx = st.enter_context(nc.sbuf_tensor("uamx", [128, MT * 64], F32))
    damx = st.enter_context(nc.sbuf_tensor("damx", [128, HT * 28], F32))
    cq = st.enter_context(nc.sbuf_tensor("cq", [128, NB_CODE * 512], I32))
    qu16 = st.enter_context(nc.sbuf_tensor("qu16", [128, 512], U16))
    mby = [st.enter_context(nc.sbuf_tensor(f"mb{i}", [128, 512], U16))
           for i in range(4)]
    ut = st.enter_context(nc.sbuf_tensor("ut", [128, 8 * 512], FP16))
    wdt = st.enter_context(nc.sbuf_tensor("wdt", [128, 2 * 512], BF16))
    trs = st.enter_context(nc.sbuf_tensor("trs", [128, 2 * 512], BF16))
    lnS = st.enter_context(nc.sbuf_tensor("lnS", [128, 8], F32))
    lnQ = st.enter_context(nc.sbuf_tensor("lnQ", [128, 8], F32))
    lnNM = st.enter_context(nc.sbuf_tensor("lnNM", [128, 8], F32))
    lnM2 = st.enter_context(nc.sbuf_tensor("lnM2", [128, 8], F32))
    lnT2 = st.enter_context(nc.sbuf_tensor("lnT2", [128, 8], F32))
    lnV = st.enter_context(nc.sbuf_tensor("lnV", [128, 8], F32))
    lnSD = st.enter_context(nc.sbuf_tensor("lnSD", [128, 8], F32))
    lnR = st.enter_context(nc.sbuf_tensor("lnR", [128, 8], F32))
    dnm = st.enter_context(nc.sbuf_tensor("dnm", [1, 512], F32))
    rcp = st.enter_context(nc.sbuf_tensor("rcp", [1, 512], F32))
    rcpb = st.enter_context(nc.sbuf_tensor("rcpb", [1, 512], BF16))

    AR = 152 * 1024
    arena = st.enter_context(nc.sbuf_tensor("arena", [128, AR], U8))

    def cv(off, cols, dt):
        esz = {F32: 4, BF16: 2, FP16: 2, I32: 4, U16: 2}[dt]
        return arena[:, off:off + cols * esz].bitcast(dt)

    oP_xt = 0
    oP_sq = 16384
    oP_xnf = 32768
    oP_xtt = 65536
    oP_wstg = 81920
    oP_kts = 98304
    oP_vts = 100352
    oP_wkb = 102400
    oP_wvb = 110592
    oQ_wqb = 0
    oQ_woev = 65536
    oQ_xch = 32768
    oQ_wob = 32768
    oQ_qT = 65536
    oQ_exs = 81920
    oQ_avb = 98304
    oQ_ktb = 114688
    oQ_vtb = 118784
    oQ_ae = 122880
    oQ_be = 123904
    oQ_stg = 124928
    oF_xch = 0
    oF_wg = 65536
    oF_wu = 81920
    oF_wd = 98304
    oF_hT = 105472
    oF_dnev = 134144
    oF_sg = 150528
    oF_fr = 134144
    oF_fx = 138240
    assert oF_sg + 4096 <= AR

    pb = [st.enter_context(nc.psum_tensor(f"pb{i}", [128, 512], F32))
          for i in range(8)]

    blk = st.enter_context(nc.Block())

    def code_src(t):
        kind = t[0]
        if kind == "d":
            _, H, s = t
            return (bass.AP(dqd, H * 128 * MSL + DOF[s], [[MSL, 128], [1, DFD[s]]]),
                    DFD[s])
        _, k, s = t
        ten = gqd if kind == "g" else uqd
        return bass.AP(ten, k * 128 * D + s * 512, [[D, 128], [1, 512]]), 512

    @blk.sync
    def _(sync):
        sync.dma_start(out=identf[:, :], in_=idn[:, :]).then_inc(s_wa, 16)
        sync.dma_start(out=identb[:, :], in_=idnb[:, :]).then_inc(s_wa, 16)
        sync.dma_start(out=cmaskb[:, :], in_=cmkb[:, :]).then_inc(s_wa, 16)
        sync.wait_ge(s_wa, 48)
        sync.dma_start(
            out=cv(oP_wstg, D, F32),
            in_=bass.AP(wkd, 0, [[DH, 128], [128 * DH, HT], [1, DH]]),
        ).then_inc(s_wa, 16)
        sync.wait_ge(s_waf, 1)
        sync.dma_start(
            out=cv(oP_wstg, D, F32),
            in_=bass.AP(wvd, 0, [[DH, 128], [128 * DH, HT], [1, DH]]),
        ).then_inc(s_wa, 16)
        sync.wait_ge(s_wa, 80)
        sync.dma_start(out=gamx[:, :],
                       in_=bass.AP(gamd, 0, [[64, 128], [128 * 64, MT], [1, 64]])
                       ).then_inc(s_wa, 16)
        sync.dma_start(out=uamx[:, :],
                       in_=bass.AP(uamd, 0, [[64, 128], [128 * 64, MT], [1, 64]])
                       ).then_inc(s_wa, 16)
        sync.dma_start(out=damx[:, :],
                       in_=bass.AP(damd, 0, [[28, 128], [128 * 28, HT], [1, 28]])
                       ).then_inc(s_wa, 16)
        for tt in range(8):
            if tt >= 1:
                sync.wait_ge(s_lnn, tt)
            sync.dma_start(out=cv(oP_xt, D, F32),
                           in_=x_own[tt * 128:(tt + 1) * 128, :]).then_inc(s_xin, 16)
        for i in range(NB_CODE):
            src, w = code_src(DQ_TILES[i])
            sync.dma_start(out=cq[:, (i % NB_CODE) * 512:(i % NB_CODE) * 512 + w],
                           in_=src).then_inc(s_code[i % NB_CODE], 16)
        for i, til in enumerate(DQ_TILES):
            fd = dq_fd(til)
            ob = (i % 2) * 512
            sync.wait_ge(s_dqw, i + 1)
            if i >= 2:
                sync.wait_ge(s_dqs, 16 * (i - 1))
            for j in range(fd // 128):
                sync.dma_start_transpose(
                    trs[:, ob + j * 128: ob + (j + 1) * 128],
                    wdt[:, ob + j * 128: ob + (j + 1) * 128],
                ).then_inc(s_dqt, 16)
            sync.wait_ge(s_dqt, 16 * DQ_CUMP[i + 1])
            kind = til[0]
            if kind == "d":
                _, H, s = til
                dst = d_st[:, H * MSL + DOF[s]: H * MSL + DOF[s] + fd]
            elif kind == "g":
                _, k, s = til
                dst = g_st[:, k * D + s * 512: k * D + (s + 1) * 512]
            else:
                _, k, s = til
                dst = u_st[:, k * D + s * 512: k * D + (s + 1) * 512]
            sync.dma_start(out=dst, in_=trs[:, ob:ob + fd]).then_inc(s_dqs, 16)
            nx = i + NB_CODE
            if nx < NDQ:
                src, w = code_src(DQ_TILES[nx])
                sync.dma_start(out=cq[:, (nx % NB_CODE) * 512:(nx % NB_CODE) * 512 + w],
                               in_=src).then_inc(s_code[nx % NB_CODE], 16)

    @blk.sync
    def _(sync):
        for c in range(2, NCH):
            sync.wait_ge(s_gu, c * MT)
            sync.dma_start(out=cv(oF_xch, HT * TPC, BF16),
                           in_=xnT_all[c * 128:(c + 1) * 128, :]).then_inc(s_wxc, 16)
            for k in range(MT):
                sync.wait_ge(s_gu, c * MT + k - 1)
                sync.dma_start(out=cv(oF_wg + (k % 2) * 8192, D, BF16),
                               in_=g_st[:, k * D:(k + 1) * D]
                               ).then_inc(s_wg2[k % 2], 16)
                sync.dma_start(out=cv(oF_wu + (k % 2) * 8192, D, BF16),
                               in_=u_st[:, k * D:(k + 1) * D]
                               ).then_inc(s_wu2[k % 2], 16)
            for ho in range(HT):
                sync.wait_ge(s_dn, c * HT + ho - 1)
                sync.dma_start(out=cv(oF_wd + (ho % 2) * 3584, MSL, BF16),
                               in_=d_st[:, ho * MSL:(ho + 1) * MSL]
                               ).then_inc(s_wd2[ho % 2], 16)

    @blk.scalar
    def _(scalar):
        scalar.wait_ge(s_wa, 64)
        scalar.activation(cv(oP_wkb, D, BF16), cv(oP_wstg, D, F32),
                          AF.Copy).then_inc(s_waf, 1)
        scalar.wait_ge(s_wa, 80)
        scalar.activation(cv(oP_wvb, D, BF16), cv(oP_wstg, D, F32),
                          AF.Copy).then_inc(s_waf, 1)
        for tt in range(8):
            scalar.wait_ge(s_xin, (tt + 1) * 16)
            scalar.activation(cv(oP_sq, D, F32), cv(oP_xt, D, F32), AF.Square,
                              accum_out=lnQ[:, tt:tt + 1]).then_inc(s_acp, 1)
            scalar.wait_ge(s_lnr, tt + 1)
            scalar.activation(lnNM[:, tt:tt + 1], lnS[:, tt:tt + 1], AF.Copy,
                              scale=-1.0 / D).then_inc(s_acp, 1)
            scalar.wait_ge(s_acp, 2 * tt + 2)
            scalar.activation(lnM2[:, tt:tt + 1], lnNM[:, tt:tt + 1], AF.Square)
            scalar.activation(lnT2[:, tt:tt + 1], lnQ[:, tt:tt + 1], AF.Copy,
                              scale=1.0 / D, bias=EPS).then_inc(s_sta, 1)
            scalar.wait_ge(s_lnv, tt + 1)
            scalar.sqrt(lnSD[:, tt:tt + 1], lnV[:, tt:tt + 1]).then_inc(s_sd, 1)
            xb = oP_xtt + (tt % 2) * 8192
            if tt >= 2:
                scalar.wait_ge(s_kv, 2 * (tt - 1))
                scalar.wait_ge(s_stg, 16 * 4 * (tt - 1))
            for g in range(8):
                scalar.wait_ge(s_xt, tt * 8 + g + 1)
                scalar.activation(cv(xb + g * 1024, 512, BF16),
                                  pb[4 + g % 2][:, :],
                                  AF.Copy).then_inc(s_evx, 1)
            scalar.wait_ge(s_kv, 2 * tt + 1)
            scalar.activation(cv(oP_kts + tt * 256, 128, BF16), pb[0][:, 0:128], AF.Copy)
            scalar.wait_ge(s_kv, 2 * tt + 2)
            scalar.activation(cv(oP_vts + tt * 256, 128, BF16), pb[1][:, 0:128],
                              AF.Copy).then_inc(s_kve, 1)

        g_exp = 0
        for b in range(4):
            for g in range(4):
                scalar.wait_ge(s_qpl, (b * 4 + g + 1) * 16)
                scalar.activation(cv(oQ_wqb + g * 8192, 4096, BF16),
                                  cv(oQ_stg, 4096, F32), AF.Copy).then_inc(s_qc, 1)
            for cs in range(4):
                scalar.wait_ge(s_xncf, b * 4 + cs + 1)
                if cs == 0:
                    scalar.wait_ge(s_wodp[0], b * 16 * 16)
                    scalar.wait_ge(s_wodp[1], b * 16 * 16)
                for dp in range(4):
                    ins = scalar.activation(
                        cv(oQ_qT + dp * 4096 + (cs // 2) * 2048 + (cs % 2) * 1024,
                           512, BF16),
                        pb[dp][:, :], AF.Copy)
                ins.then_inc(s_qev, 1)
            scalar.wait_ge(s_xncf, b * 4 + 4)
            for g in range(4):
                scalar.wait_ge(s_wol, (b * 4 + g + 1) * 16)
                scalar.activation(cv(oQ_wob + g * 8192, 4096, BF16),
                                  cv(oQ_stg, 4096, F32), AF.Copy).then_inc(s_oc, 1)
            for h in range(HPC):
                for qs in range(4):
                    uq_i = (b * 4 + h) * 4 + qs
                    if uq_i >= 1:
                        scalar.wait_ge(s_av1, uq_i)
                    for pi, (kb, lo) in enumerate(ATT_P[qs]):
                        g_exp += 1
                        scalar.wait_ge(s_pe1, g_exp)
                        slot = pi % 4
                        scalar.activation(
                            cv(oQ_exs + kb * 1024 + lo * 2, 512 - lo, BF16),
                            pb[4 + slot][:, lo:512],
                            AF.Exp, scale=SCL).then_inc(s_exp, 1)
                    scalar.wait_ge(s_av1, uq_i + 1)
                    if uq_i >= 1:
                        scalar.wait_ge(s_pbc, uq_i)
                    scalar.activation(dnm[0:1, :], pb[1][0:1, :],
                                      AF.Ln).then_inc(s_acp, 1)
                    scalar.wait_ge(s_acp, 17 + uq_i)
                    scalar.activation(rcpb[0:1, :], dnm[0:1, :], AF.Exp,
                                      scale=-1.0).then_inc(s_dva, 1)
                    scalar.wait_ge(s_pbc, uq_i + 1)
                    if uq_i >= 1:
                        scalar.wait_ge(s_pva, uq_i)
                    scalar.activation(cv(oQ_ae, 512, BF16), pb[0][:, :], AF.Copy)
                    scalar.activation(cv(oQ_be, 512, BF16), pb[2][:, :],
                                      AF.Copy).then_inc(s_abe, 1)
            for ct in range(32):
                scalar.wait_ge(s_wvd, b * 32 + ct + 1)
                if b * 32 + ct >= 2:
                    scalar.wait_ge(s_wodp[ct % 2], (b * 16 + ct // 2) * 16)
                for ts4 in range(4):
                    ins = scalar.activation(
                        cv(oQ_woev + (ct % 2) * 8192 + ts4 * 2048, 512, F32),
                        pb[(ct % 2) * 4 + ts4][:, :], AF.Copy)
                ins.then_inc(s_woe, 1)

        for c in range(NCH):
            for k in range(MT):
                scalar.wait_ge(s_gu, c * MT + k + 1)
                if c * MT + k >= 2:
                    scalar.wait_ge(s_hT, c * MT + k - 1)
                for half in range(2):
                    ins = scalar.activation(
                        cv(oF_sg + (k % 2) * 2048 + half * 1024, 512, BF16),
                        pb[(k % 2) * 4 + half][:, :], AF.Sigmoid)
                ins.then_inc(s_sig, 1)
            for ho in range(HT):
                scalar.wait_ge(s_dn, c * HT + ho + 1)
                if c * HT + ho >= 4:
                    scalar.wait_ge(s_dnap[ho % 4], (c * 8 + ho // 4) * 16)
                for half in range(2):
                    ins = scalar.activation(
                        cv(oF_dnev + (ho % 4) * 4096 + half * 2048, 512, F32),
                        pb[2 * (ho % 4) + half][:, :], AF.Copy)
                ins.then_inc(s_dne, 1)

    @blk.vector
    def _(vector):
        vector.memset(onesr[0:1, :], 1.0)
        vector.memset(onesc[:, :], 1.0)
        for tt in range(8):
            vector.wait_ge(s_xin, (tt + 1) * 16)
            vector.tensor_reduce(lnS[:, tt:tt + 1], cv(oP_xt, D, F32), AX,
                                 ALU.add).then_inc(s_lnr, 1)
            vector.wait_ge(s_sta, tt + 1)
            vector.tensor_tensor(lnV[:, tt:tt + 1], lnT2[:, tt:tt + 1],
                                 lnM2[:, tt:tt + 1], ALU.subtract).then_inc(s_lnv, 1)
            vector.wait_ge(s_sd, tt + 1)
            vector.reciprocal(lnR[:, tt:tt + 1], lnSD[:, tt:tt + 1])
            if tt >= 2:
                vector.wait_ge(s_xt, (tt - 1) * 8)
            vector.tensor_scalar(cv(oP_xnf + (tt % 2) * 16384, D, F32),
                                 cv(oP_xt, D, F32), lnNM[:, tt:tt + 1],
                                 lnR[:, tt:tt + 1], ALU.add, ALU.mult
                                 ).then_inc(s_lnn, 1)

        def emit_silu(c, k):
            sgb = oF_sg + (k % 2) * 2048
            vector.wait_ge(s_sig, c * MT + k + 1)
            for half in range(2):
                vector.tensor_tensor(cv(sgb + half * 1024, 512, BF16),
                                     cv(sgb + half * 1024, 512, BF16),
                                     pb[(k % 2) * 4 + half][:, :], ALU.mult)
            for half in range(2):
                ins = vector.tensor_tensor(
                    cv(oF_hT + k * 2048 + half * 1024, 512, BF16),
                    cv(sgb + half * 1024, 512, BF16),
                    pb[(k % 2) * 4 + 2 + half][:, :], ALU.mult)
            ins.then_inc(s_hT, 1)

        vector.wait_ge(s_wa, 128)
        for i, til in enumerate(DQ_TILES):
            fd = dq_fd(til)
            slot = (i % NB_CODE) * 512
            ob = (i % 2) * 512
            vector.wait_ge(s_code[i % NB_CODE], (i // NB_CODE + 1) * 16)
            vector.tensor_copy(qu16[:, 0:fd], cq[:, slot:slot + fd])
            vector.tensor_scalar(mby[0][:, 0:fd], qu16[:, 0:fd], 1, None,
                                 ALU.bitwise_and)
            vector.tensor_scalar(mby[1][:, 0:fd], qu16[:, 0:fd], 2, None,
                                 ALU.bitwise_and)
            vector.tensor_scalar(mby[2][:, 0:fd], qu16[:, 0:fd], 4, None,
                                 ALU.bitwise_and)
            vector.tensor_scalar(mby[3][:, 0:fd], qu16[:, 0:fd], 8, None,
                                 ALU.bitwise_and)
            for j in range(8):
                c0, c1 = NF4[2 * j], NF4[2 * j + 1]
                vector.tensor_scalar(ut[:, j * 512:j * 512 + fd], mby[0][:, 0:fd],
                                     c1 - c0, c0, ALU.mult, ALU.add)
            for j in range(4):
                vector.copy_predicated(ut[:, (2 * j) * 512:(2 * j) * 512 + fd],
                                       mby[1][:, 0:fd],
                                       ut[:, (2 * j + 1) * 512:(2 * j + 1) * 512 + fd])
            vector.copy_predicated(ut[:, 0:fd], mby[2][:, 0:fd],
                                   ut[:, 1024:1024 + fd])
            vector.copy_predicated(ut[:, 2048:2048 + fd], mby[2][:, 0:fd],
                                   ut[:, 3072:3072 + fd])
            vector.copy_predicated(ut[:, 0:fd], mby[3][:, 0:fd],
                                   ut[:, 2048:2048 + fd])
            if i >= 2:
                vector.wait_ge(s_dqt, 16 * DQ_CUMP[i - 1])
            kind = til[0]
            ins = None
            for bb in range(fd // 64):
                if kind == "d":
                    _, H, s = til
                    amc = damx[:, H * 28 + DOF[s] // 64 + bb:
                               H * 28 + DOF[s] // 64 + bb + 1]
                elif kind == "g":
                    _, k, s = til
                    amc = gamx[:, k * 64 + s * 8 + bb: k * 64 + s * 8 + bb + 1]
                else:
                    _, k, s = til
                    amc = uamx[:, k * 64 + s * 8 + bb: k * 64 + s * 8 + bb + 1]
                ins = vector.tensor_scalar(wdt[:, ob + bb * 64: ob + (bb + 1) * 64],
                                           ut[:, bb * 64:(bb + 1) * 64],
                                           amc, None, ALU.mult)
            ins.then_inc(s_dqw, 1)
        for k0 in range(MT):
            emit_silu(0, k0)

        def emit_fin(cc):
            for qtr in range(4):
                vector.wait_ge(s_fin, (cc * 4 + qtr + 1) * 32)
                vector.tensor_tensor(cv(oF_fr, 1024, F32), cv(oF_fr, 1024, F32),
                                     cv(oF_fx, 1024, F32),
                                     ALU.add).then_inc(s_fad, 1)

        for c in range(1, NCH):
            for k in range(MT):
                emit_silu(c, k)
                if k == 11:
                    emit_fin(c - 1)
        emit_fin(NCH - 1)

    @blk.tensor
    def _(tensor):
        tensor.wait_ge(s_wa, 48)
        for tt in range(8):
            xb = oP_xnf + (tt % 2) * 16384
            tensor.wait_ge(s_lnn, tt + 1)
            for g in range(8):
                if tt * 8 + g >= 2:
                    tensor.wait_ge(s_evx, tt * 8 + g - 1)
                for q4 in range(4):
                    hh = g * 4 + q4
                    ins = tensor.transpose(
                        pb[4 + g % 2][:, q4 * 128:(q4 + 1) * 128],
                        cv(xb + hh * 512, 128, F32), identf[:, :])
                ins.then_inc(s_xt, 1)
            tensor.wait_ge(s_waf, 2)
            tensor.wait_ge(s_evx, (tt + 1) * 8)
            if tt >= 1:
                tensor.wait_ge(s_kve, tt)
            xtb = oP_xtt + (tt % 2) * 8192
            for H in range(HT):
                ins = tensor.matmul(pb[0][:, 0:128], cv(oP_wkb + H * 256, 128, BF16),
                                    cv(xtb + H * 256, 128, BF16),
                                    start=(H == 0), stop=(H == HT - 1))
            ins.then_inc(s_kv, 1)
            for H in range(HT):
                ins = tensor.matmul(pb[1][:, 0:128], cv(xtb + H * 256, 128, BF16),
                                    cv(oP_wvb + H * 256, 128, BF16),
                                    start=(H == 0), stop=(H == HT - 1))
            ins.then_inc(s_kv, 1)

        tensor.wait_ge(s_lnn, 8)
        tensor.wait_ge(s_kve, 8)
        tensor.wait_ge(s_evx, 64)
        g_pe = 0
        for b in range(4):
            for cs in range(4):
                tensor.wait_ge(s_qc, b * 4 + 4)
                if cs == 0 and b >= 1:
                    tensor.wait_ge(s_woe, b * 32)
                tensor.wait_ge(s_xchl, (b * 4 + cs + 1) * 16)
                if b * 4 + cs >= 1:
                    tensor.wait_ge(s_qev, b * 4 + cs)
                for dp in range(4):
                    for H in range(HT):
                        ins = tensor.matmul(pb[dp][:, :],
                                            cv(oQ_wqb + H * 1024 + dp * 256, 128, BF16),
                                            cv(oQ_xch + H * 1024, 512, BF16),
                                            start=(H == 0), stop=(H == HT - 1))
                tensor.nop()
                ins.then_inc(s_xncf, 1)
            tensor.wait_ge(s_kvl, (b + 1) * 64)
            for h in range(HPC):
                for qs in range(4):
                    uq_i = (b * 4 + h) * 4 + qs
                    tensor.wait_ge(s_qev, b * 4 + 4)
                    for pi, (kb, lo) in enumerate(ATT_P[qs]):
                        slot = pi % 4
                        g_pe += 1
                        if g_pe >= 5:
                            tensor.wait_ge(s_exp, g_pe - 4)
                        diag = kb >= 4 * qs
                        ins = tensor.matmul(pb[4 + slot][:, lo:512],
                                            cv(oQ_ktb + kb * 256, 128, BF16),
                                            cv(oQ_qT + h * 4096 + qs * 1024 + lo * 2,
                                               512 - lo, BF16),
                                            start=True, stop=not diag)
                        if diag:
                            ins = tensor.matmul(
                                pb[4 + slot][:, lo:lo + 128],
                                identb[:, :], cmaskb[:, :], start=False, stop=True)
                        ins.then_inc(s_pe1, 1)
                    tensor.wait_ge(s_exp, g_pe)
                    if uq_i >= 1:
                        tensor.wait_ge(s_abe, uq_i)
                        tensor.wait_ge(s_dva, uq_i)
                    np_ = len(ATT_P[qs])
                    for pi, (kb, lo) in enumerate(ATT_P[qs]):
                        tensor.matmul(pb[0][:, lo:512],
                                      cv(oQ_vtb + kb * 256, 128, BF16),
                                      cv(oQ_exs + kb * 1024 + lo * 2, 512 - lo, BF16),
                                      start=(pi == 0), stop=(pi == np_ - 1))
                    for pi, (kb, lo) in enumerate(ATT_P[qs]):
                        ins = tensor.matmul(pb[1][0:1, lo:512],
                                            onesc[:, :],
                                            cv(oQ_exs + kb * 1024 + lo * 2,
                                               512 - lo, BF16),
                                            start=(pi == 0), stop=(pi == np_ - 1))
                    ins.then_inc(s_av1, 1)
                    tensor.wait_ge(s_dva, uq_i + 1)
                    tensor.matmul(pb[2][:, :], onesr[0:1, :], rcpb[0:1, :],
                                  start=True, stop=True).then_inc(s_pbc, 1)
            tensor.wait_ge(s_pva, (b + 1) * 16)
            tensor.wait_ge(s_oc, (b + 1) * 4)
            for ct in range(32):
                if b * 32 + ct >= 2:
                    tensor.wait_ge(s_woe, b * 32 + ct - 1)
                for dp in range(4):
                    for ts4 in range(4):
                        ins = tensor.matmul(pb[(ct % 2) * 4 + ts4][:, :],
                                            cv(oQ_wob + dp * 8192 + ct * 256, 128, BF16),
                                            cv(oQ_avb + dp * 4096 + ts4 * 1024,
                                               512, BF16),
                                            start=(dp == 0), stop=(dp == 3))
                ins.then_inc(s_wvd, 1)

        tensor.wait_ge(s_woe, 128)
        for c in range(NCH):
            tensor.wait_ge(s_wxc, (c + 1) * 16)
            if c >= 1:
                tensor.wait_ge(s_dne, c * HT)
            for k in range(MT):
                nslot = c * (MT // 2) + k // 2 + 1
                tensor.wait_ge(s_wg2[k % 2], nslot * 16)
                tensor.wait_ge(s_wu2[k % 2], nslot * 16)
                if c * MT + k >= 2:
                    tensor.wait_ge(s_hT, c * MT + k - 1)
                for half in range(2):
                    for H in range(HT):
                        tensor.matmul(pb[(k % 2) * 4 + half][:, :],
                                      cv(oF_wg + (k % 2) * 8192 + H * 256, 128, BF16),
                                      cv(oF_xch + H * 2048 + half * 1024, 512, BF16),
                                      start=(H == 0), stop=(H == HT - 1))
                for half in range(2):
                    for H in range(HT):
                        ins = tensor.matmul(
                            pb[(k % 2) * 4 + 2 + half][:, :],
                            cv(oF_wu + (k % 2) * 8192 + H * 256, 128, BF16),
                            cv(oF_xch + H * 2048 + half * 1024, 512, BF16),
                            start=(H == 0), stop=(H == HT - 1))
                ins.then_inc(s_gu, 1)
            tensor.wait_ge(s_hT, (c + 1) * MT)
            for ho in range(HT):
                tensor.wait_ge(s_wd2[ho % 2], (c * (HT // 2) + ho // 2 + 1) * 16)
                if c * HT + ho >= 4:
                    tensor.wait_ge(s_dne, c * HT + ho - 3)
                for k in range(MT):
                    for half in range(2):
                        ins = tensor.matmul(
                            pb[2 * (ho % 4) + half][:, :],
                            cv(oF_wd + (ho % 2) * 3584 + k * 256, 128, BF16),
                            cv(oF_hT + k * 2048 + half * 1024, 512, BF16),
                            start=(k == 0), stop=(k == MT - 1))
                ins.then_inc(s_dn, 1)

    @blk.gpsimd
    def _(gpsimd):
        for tt in range(8):
            gpsimd.wait_ge(s_evx, (tt + 1) * 8)
            xtb = oP_xtt + (tt % 2) * 8192
            for g4 in range(4):
                gpsimd.dma_start(
                    out=bass.AP(xnT_st, g4 * 8 * TPC + tt * 128,
                                [[HT * TPC, 128], [TPC, 8], [1, 128]]),
                    in_=cv(xtb + g4 * 2048, 1024, BF16),
                ).then_inc(s_stg, 16)
        gpsimd.wait_ge(s_kve, 8)
        gpsimd.dma_start(out=kT_st[:, :], in_=cv(oP_kts, TPC, BF16)).then_inc(s_stg, 16)
        gpsimd.dma_start(out=v_st[:, :], in_=cv(oP_vts, TPC, BF16)).then_inc(s_stg, 16)
        gpsimd.wait_ge(s_stg, 16 * 34)
        gpsimd.collective_compute(
            "AllGather", ALU.bypass, replica_groups=[list(range(NC))],
            ins=[kT_st[:, :]], outs=[kT_all[:, :]]).then_inc(s_ag, 1)
        gpsimd.collective_compute(
            "AllGather", ALU.bypass, replica_groups=[list(range(NC))],
            ins=[v_st[:, :]], outs=[v_all[:, :]]).then_inc(s_ag, 1)
        gpsimd.collective_compute(
            "AllGather", ALU.bypass, replica_groups=[list(range(NC))],
            ins=[xnT_st[:, :]], outs=[xnT_all[:, :]]).then_inc(s_ag, 1)
        gpsimd.wait_ge(s_ag, 3)

        for b in range(4):
            for g in range(4):
                gpsimd.wait_ge(s_qc, b * 4 + g)
                if g == 0:
                    gpsimd.wait_ge(s_oc, b * 4)
                gpsimd.dma_start(
                    out=cv(oQ_stg, 4096, F32),
                    in_=bass.AP(wq_s, g * 8 * 128 * 512,
                                [[512, 128], [128 * 512, 8], [1, 512]]),
                ).then_inc(s_qpl, 16)
            for cs in range(4):
                c = 2 * b + cs // 2
                s = cs % 2
                gpsimd.wait_ge(s_xncf, b * 4 + cs)
                if cs == 0:
                    gpsimd.wait_ge(s_wvd, b * 32)
                gpsimd.dma_start(
                    out=cv(oQ_xch, HT * 512, BF16),
                    in_=bass.AP(xnT_all, c * 128 * HT * TPC + s * 512,
                                [[HT * TPC, 128], [TPC, HT], [1, 512]]),
                ).then_inc(s_xchl, 16)
            gpsimd.wait_ge(s_av1, b * 16)
            for half in range(2):
                cc = 2 * b + half
                gpsimd.dma_start(
                    out=cv(oQ_ktb + half * 2048, TPC, BF16),
                    in_=kT_all[cc * 128:(cc + 1) * 128, :]).then_inc(s_kvl, 16)
            for half in range(2):
                cc = 2 * b + half
                gpsimd.dma_start(
                    out=cv(oQ_vtb + half * 2048, TPC, BF16),
                    in_=v_all[cc * 128:(cc + 1) * 128, :]).then_inc(s_kvl, 16)
            for g in range(4):
                gpsimd.wait_ge(s_oc, b * 4 + g)
                if g == 0:
                    gpsimd.wait_ge(s_qc, (b + 1) * 4)
                gpsimd.dma_start(out=cv(oQ_stg, 4096, F32),
                                 in_=wo_s[g * 128:(g + 1) * 128, :]
                                 ).then_inc(s_wol, 16)
            gpsimd.wait_ge(s_wvd, b * 32)
            for h in range(HPC):
                for qs in range(4):
                    uq_i = (b * 4 + h) * 4 + qs
                    gpsimd.wait_ge(s_abe, uq_i + 1)
                    gpsimd.tensor_tensor(
                        cv(oQ_avb + h * 4096 + qs * 1024, 512, BF16),
                        cv(oQ_ae, 512, BF16), cv(oQ_be, 512, BF16),
                        ALU.mult).then_inc(s_pva, 1)
            for ct in range(32):
                gpsimd.wait_ge(s_woe, b * 32 + ct + 1)
                gpsimd.dma_start(
                    out=bass.AP(rs_in, (2 * b * 8 * D + ct * 128) * 128,
                                [[128, 128], [D * 128, 16], [1, 128]]),
                    in_=cv(oQ_woev + (ct % 2) * 8192, 2048, F32),
                ).then_inc(s_wodp[ct % 2], 16)
            if DEBUG and b == 0:
                gpsimd.dma_start(out=dbg_qT[:, :],
                                 in_=cv(oQ_qT, 8192, BF16)).then_inc(s_stg, 16)
                gpsimd.dma_start(out=dbg_avb[:, :],
                                 in_=cv(oQ_avb, 8192, BF16)).then_inc(s_stg, 16)
                gpsimd.dma_start(out=dbg_exs[:, :],
                                 in_=cv(oQ_exs, 8192, BF16)).then_inc(s_stg, 16)
                gpsimd.wait_ge(s_stg, 16 * 37)

        def pool_fin(cc):
            gpsimd.wait_ge(s_rs, cc + 1)
            for qtr in range(4):
                if cc * 4 + qtr >= 1:
                    gpsimd.wait_ge(s_out, (cc * 4 + qtr) * 16)
                gpsimd.dma_start(
                    out=cv(oF_fr, 1024, F32),
                    in_=bass.AP(rs_out, (cc * D + qtr * 1024) * 128,
                                [[128, 128], [128 * 128, 8], [1, 128]]),
                ).then_inc(s_fin, 16)
                gpsimd.dma_start(
                    out=cv(oF_fx, 1024, F32),
                    in_=bass.AP(x_rsT, (cc * D + qtr * 1024) * 128,
                                [[128, 128], [128 * 128, 8], [1, 128]]),
                ).then_inc(s_fin, 16)
                gpsimd.wait_ge(s_fad, cc * 4 + qtr + 1)
                gpsimd.dma_start(
                    out=bass.AP(out_sl, (cc * D + qtr * 1024) * 128,
                                [[128, 128], [128 * 128, 8], [1, 128]]),
                    in_=cv(oF_fr, 1024, F32),
                ).then_inc(s_out, 16)

        gpsimd.wait_ge(s_wvd, 4 * 32)
        gpsimd.wait_ge(s_wodp[0], 4 * 16 * 16)
        gpsimd.wait_ge(s_wodp[1], 4 * 16 * 16)
        for c in range(NCH):
            if c < 2:
                gpsimd.wait_ge(s_gu, c * MT)
                gpsimd.dma_start(out=cv(oF_xch, HT * TPC, BF16),
                                 in_=xnT_all[c * 128:(c + 1) * 128, :]
                                 ).then_inc(s_wxc, 16)
                for k in range(MT):
                    gpsimd.wait_ge(s_dqs, (N_DOWN + (k + 1) * 16) * 16)
                    if c * MT + k >= 2:
                        gpsimd.wait_ge(s_gu, c * MT + k - 1)
                    gpsimd.dma_start(out=cv(oF_wg + (k % 2) * 8192, D, BF16),
                                     in_=g_st[:, k * D:(k + 1) * D]
                                     ).then_inc(s_wg2[k % 2], 16)
                    gpsimd.dma_start(out=cv(oF_wu + (k % 2) * 8192, D, BF16),
                                     in_=u_st[:, k * D:(k + 1) * D]
                                     ).then_inc(s_wu2[k % 2], 16)
            if c >= 1:
                pool_fin(c - 1)
            if DEBUG and c == 3:
                gpsimd.wait_ge(s_hT, 4 * MT)
                gpsimd.dma_start(out=dbg_hT[:, :],
                                 in_=cv(oF_hT, MT * TPC, BF16)).then_inc(s_stg, 16)
                gpsimd.wait_ge(s_stg, 16 * 35)
            for ho in range(HT):
                if c < 2:
                    gpsimd.wait_ge(s_dqs, N_DOWN * 16)
                    if c * HT + ho >= 2:
                        gpsimd.wait_ge(s_dn, c * HT + ho - 1)
                    gpsimd.dma_start(out=cv(oF_wd + (ho % 2) * 3584, MSL, BF16),
                                     in_=d_st[:, ho * MSL:(ho + 1) * MSL]
                                     ).then_inc(s_wd2[ho % 2], 16)
                gpsimd.wait_ge(s_dne, c * HT + ho + 1)
                acc_dst = (bass.AP(rs_jnk, (ho * 128) * 128,
                                   [[128, 128], [D * 128, 8], [1, 128]])
                           if SKIP_ACC else
                           bass.AP(rs_in, (c * 8 * D + ho * 128) * 128,
                                   [[128, 128], [D * 128, 8], [1, 128]]))
                gpsimd.dma_start(
                    out=acc_dst,
                    in_=cv(oF_dnev + (ho % 4) * 4096, 1024, F32),
                    accum_op=(ALU.bypass if ACC_BYPASS else ALU.add)
                    ).then_inc(s_dnap[ho % 4], 16)
            for sl in range(4):
                gpsimd.wait_ge(s_dnap[sl], (c + 1) * 8 * 16)
            gpsimd.collective_compute(
                "ReduceScatter", ALU.add, replica_groups=[list(range(NC))],
                ins=[rs_in[c * 8 * D:(c + 1) * 8 * D, :]],
                outs=[rs_out[c * D:(c + 1) * D, :]]).then_inc(s_rs, 1)
        pool_fin(NCH - 1)
        if DEBUG:
            gpsimd.dma_start(out=dbg_xnT[:, :], in_=xnT_st[:, :]).then_inc(s_stg, 16)
            gpsimd.dma_start(out=dbg_kT[:, :], in_=kT_st[:, :]).then_inc(s_stg, 16)
            gpsimd.dma_start(out=dbg_v[:, :], in_=v_st[:, :]).then_inc(s_stg, 16)
            gpsimd.dma_start(out=dbg_g[:, :], in_=g_st[:, 0:D]).then_inc(s_stg, 16)
            gpsimd.dma_start(out=dbg_u[:, :], in_=u_st[:, 0:D]).then_inc(s_stg, 16)
            gpsimd.dma_start(out=dbg_d[:, :], in_=d_st[:, 0:MSL]).then_inc(s_stg, 16)
            gpsimd.dma_start(out=dbg_rs[:, :], in_=rs_in[0:D, :]).then_inc(s_stg, 16)
            gpsimd.dma_start(out=dbg_ro[:, :], in_=rs_out[0:D, :]).then_inc(s_stg, 16)
            gpsimd.dma_start(out=dbg_rs3[:, :],
                             in_=rs_in[3 * 8 * D:3 * 8 * D + D, :]).then_inc(s_stg, 16)
            gpsimd.dma_start(out=dbg_ro3[:, :],
                             in_=rs_out[3 * D:4 * D, :]).then_inc(s_stg, 16)
            gpsimd.wait_ge(s_stg, 16 * 48 - 16)
        gpsimd.wait_ge(s_out, NCH * 4 * 16)

    st.close()
    return nc


_NC_CACHE = [None]


def kernel(**inputs):
    x = np.ascontiguousarray(np.asarray(inputs["L0_attn_x"], dtype=np.float32))
    wq = np.asarray(inputs["L0_attn_wq"], dtype=np.float32)
    wkf = np.ascontiguousarray(np.asarray(inputs["L0_attn_wk"], dtype=np.float32))
    wvf = np.ascontiguousarray(np.asarray(inputs["L0_attn_wv"], dtype=np.float32))
    wo = np.asarray(inputs["L0_attn_wo"], dtype=np.float32)
    gqf = np.asarray(inputs["L0_ffn_gate_q"], dtype=np.int32)
    gaf = np.asarray(inputs["L0_ffn_gate_absmax"], dtype=np.float32).reshape(M, 64)
    uqf = np.asarray(inputs["L0_ffn_up_q"], dtype=np.int32)
    uaf = np.asarray(inputs["L0_ffn_up_absmax"], dtype=np.float32).reshape(M, 64)
    dqf = np.asarray(inputs["L0_ffn_down_q"], dtype=np.int32)
    daf = np.asarray(inputs["L0_ffn_down_absmax"], dtype=np.float32).reshape(D, 224)

    idn = np.eye(128, dtype=np.float32)
    idnb_ = np.eye(128).astype(ml_dtypes.bfloat16)
    cm = np.where(np.arange(128)[:, None] > np.arange(128)[None, :],
                  np.float32(-30000.0), np.float32(0.0)).astype(ml_dtypes.bfloat16)

    in_maps = []
    for r in range(NC):
        b, half = r // 2, r % 2
        slabs = []
        for c in range(NCH):
            t0 = TPC * (c % 2) + 128 * r
            slabs.append(np.ascontiguousarray(x[c // 2, t0:t0 + 128, :].T))
        in_maps.append({
            "x_own": np.ascontiguousarray(x[b, half * TPC:(half + 1) * TPC, :]),
            "x_rsT": np.concatenate(slabs, axis=0),
            "wq_s": np.ascontiguousarray(wq[:, r * 512:(r + 1) * 512]),
            "wo_s": np.ascontiguousarray(wo[r * 512:(r + 1) * 512, :]),
            "wkd": wkf, "wvd": wvf,
            "gqd": np.ascontiguousarray(gqf[r * MSL:(r + 1) * MSL, :]),
            "uqd": np.ascontiguousarray(uqf[r * MSL:(r + 1) * MSL, :]),
            "dqd": np.ascontiguousarray(dqf[:, r * MSL:(r + 1) * MSL]),
            "gamd": np.ascontiguousarray(gaf[r * MSL:(r + 1) * MSL, :]),
            "uamd": np.ascontiguousarray(uaf[r * MSL:(r + 1) * MSL, :]),
            "damd": np.ascontiguousarray(daf[:, r * 28:(r + 1) * 28]),
            "idn": idn, "idnb": idnb_, "cmkb": cm,
        })

    if _NC_CACHE[0] is None:
        _NC_CACHE[0] = build_nc()
    res = run_bass_kernel_spmd(_NC_CACHE[0], in_maps, core_ids=list(range(NC)))
    _NC_CACHE.append(res)

    out = np.empty((B, T, D), dtype=np.float32)
    for r in range(NC):
        arr = np.asarray(res.results[r]["out_sl"]).reshape(NCH, D, 128)
        for c in range(NCH):
            t0 = TPC * (c % 2) + 128 * r
            out[c // 2, t0:t0 + 128, :] = arr[c].T
    return out
